# version 03
# baseline (speedup 1.0000x reference)
import numpy as np
from contextlib import ExitStack
import concourse.bass as bass
import concourse.mybir as mybir
from concourse.bass_utils import run_bass_kernel_spmd

F32 = mybir.dt.float32
BF16 = mybir.dt.bfloat16
U8 = mybir.dt.uint8
AF = mybir.ActivationFunctionType
ALU = mybir.AluOpType
AX = mybir.AxisListType

T = 4096
D = 1024
TT = T // 128
CT = D // 128
NEG = -1e30


class Op:
    __slots__ = ("idx", "stream", "fn", "deps", "dma", "inc", "count_after",
                 "waits", "tag", "dma_ch")


class Sched:

    STREAMS = ("T", "V", "S", "P", "Q")
    NCH = 16

    def __init__(self):
        self.ops = []
        self.last_writer = {}
        self.readers = {}
        self._dma_rr = 0
        self._dma_last = [None] * self.NCH
        self._dma_g2 = {}
        self._last_op = {}
        self._fence_idx = None
        self._fence_seen = set()

    def op(self, stream, fn, reads=(), writes=(), dma=False, tag="", dma_group=None):
        o = Op()
        o.idx = len(self.ops)
        o.stream = stream
        o.fn = fn
        o.dma = dma
        o.inc = False
        o.waits = []
        o.tag = tag
        deps = set()
        if dma:
            if dma_group is None:
                ch = self._dma_rr % 12
                self._dma_rr += 1
            else:
                cnt = self._dma_g2.get(dma_group, 0)
                self._dma_g2[dma_group] = cnt + 1
                ch = 12 + cnt % 4
            o.dma_ch = ch
            if self._dma_last[ch] is not None:
                deps.add(self._dma_last[ch])
            self._dma_last[ch] = o.idx
        for b in reads:
            w = self.last_writer.get(b)
            if w is not None:
                deps.add(w)
        for b in writes:
            w = self.last_writer.get(b)
            if w is not None:
                deps.add(w)
            for r in self.readers.get(b, ()):
                deps.add(r)
        if self._fence_idx is not None and stream not in self._fence_seen:
            deps.add(self._fence_idx)
            self._fence_seen.add(stream)
        deps.discard(o.idx)
        o.deps = sorted(d for d in deps
                        if not (stream == "T" and not dma
                                and self.ops[d].stream == "T"
                                and not self.ops[d].dma))
        for b in reads:
            self.readers.setdefault(b, []).append(o.idx)
        for b in writes:
            self.last_writer[b] = o.idx
            self.readers[b] = []
        self._last_op[stream] = o.idx
        self.ops.append(o)
        return o

    def fence(self):
        deps = set()
        for s, i in self._last_op.items():
            deps.add(i)
        for i in self._dma_last:
            if i is not None:
                deps.add(i)
        o = Op()
        o.idx = len(self.ops)
        o.stream = "V"
        o.fn = lambda v: v.engine_nop()
        o.dma = False
        o.inc = False
        o.waits = []
        o.tag = "FENCE"
        o.deps = sorted(deps)
        self.ops.append(o)
        self._last_op["V"] = o.idx
        self.last_writer = {}
        self.readers = {}
        self._fence_idx = o.idx
        self._fence_seen = {"V"}

    def build(self, nc, es):
        sems = {}
        for s in self.STREAMS:
            sems["c" + s] = es.enter_context(nc.semaphore("sem_c" + s))
        for i in range(self.NCH):
            sems[f"m{i}"] = es.enter_context(nc.semaphore(f"sem_m{i}"))
        for o in self.ops:
            for d in o.deps:
                self.ops[d].inc = True
            if o.dma:
                o.inc = True
        counters = {}
        for o in self.ops:
            key = f"m{o.dma_ch}" if o.dma else ("c" + o.stream)
            if o.inc:
                counters[key] = counters.get(key, 0) + (16 if o.dma else 1)
                o.count_after = (key, counters[key])
            else:
                o.count_after = None
        know = {s: {} for s in self.STREAMS}
        snaps = {}
        for o in self.ops:
            k = know[o.stream]
            for d in o.deps:
                do = self.ops[d]
                sem, c = do.count_after
                if k.get(sem, 0) < c:
                    o.waits.append((sem, c))
                    for s2, c2 in snaps[d].items():
                        if k.get(s2, 0) < c2:
                            k[s2] = c2
            if o.inc:
                snap = dict(k)
                sem, c = o.count_after
                snap[sem] = c
                snaps[o.idx] = snap
        self._sems = sems
        self._final = dict(counters)
        return sems

    def emit(self, nc):
        sems = self._sems
        per_stream = {s: [] for s in self.STREAMS}
        for o in self.ops:
            per_stream[o.stream].append(o)

        def mk_body(ops):
            def body(eng):
                for o in ops:
                    for (semkey, c) in o.waits:
                        eng.wait_ge(sems[semkey], c)
                    if o.fn is None:
                        continue
                    ins = o.fn(eng)
                    if o.inc:
                        semkey, c = o.count_after
                        ins.then_inc(sems[semkey], 16 if o.dma else 1)
            return body

        with nc.Block() as block:
            block.tensor(mk_body(per_stream["T"]))
            block.vector(mk_body(per_stream["V"]))
            block.scalar(mk_body(per_stream["S"]))
            block.gpsimd(mk_body(per_stream["P"]))

            def sync_body(eng):
                mk_body(per_stream["Q"])(eng)
                for key, c in self._final.items():
                    eng.wait_ge(sems[key], c)
            block.sync(sync_body)
        return nc


class Bufs:

    def __init__(self, nc, es):
        self.nc = nc
        self.es = es
        self.t = {}
        self.remap = {}

    def sbuf(self, name, shape, dtype):
        if name in self.t:
            return self.t[name]
        if name in self.remap:
            pool, lo = self.remap[name]
            ph = self.t[pool]
            ap = ph[:, lo:lo + shape[1]] if shape[0] == 128 else \
                ph[0:shape[0], lo:lo + shape[1]]
            self.t[name] = ap
            return ap
        h = self.es.enter_context(self.nc.sbuf_tensor(name, list(shape), dtype))
        self.t[name] = h
        return h

    def psum(self, name, shape, dtype=F32):
        h = self.es.enter_context(self.nc.psum_tensor(name, list(shape), dtype))
        self.t[name] = h
        return h

    def dram(self, name, shape, dtype=F32, kind="Internal"):
        if name in self.t:
            return self.t[name]
        h = self.nc.dram_tensor(name, list(shape), dtype, kind=kind)
        self.t[name] = h
        return h

    def __getitem__(self, name):
        return self.t[name]

    def const_col(self, sc, value, dtype=F32):
        key = f"cc_{dtype}_{value!r}"
        if key not in self.t:
            h = self.sbuf(key, [128, 1], dtype)
            sc.op("V", lambda v, h=h, value=value: v.memset(h[:], float(value)),
                  writes=[key], tag=key)
        return self.t[key], key


def emit_setup(sc, b):
    nc = b.nc
    for i in range(6):
        b.psum(f"P{i}", [128, 512], F32)
    b.psum("PT0", [128, 128], BF16)
    b.psum("PT1", [128, 128], BF16)
    b.sbuf("SCR", [128, 2048], F32)
    ident = b.sbuf("ident_bf", [128, 128], BF16)
    onesrow = b.sbuf("ones_row", [1, 128], F32)
    onescol_bf = b.sbuf("ones_col_bf", [128, 1], BF16)
    iden_i = b.sbuf("ident_i32", [128, 128], F32)
    sc.op("P", lambda g: g.iota(iden_i[:], pattern=[[-1, 128]], base=0,
                                channel_multiplier=1,
                                allow_small_or_imprecise_dtypes=True),
          writes=["iden_i"], tag="iota")
    sc.op("P", lambda g: g.tensor_scalar(ident[:], iden_i[:], 0.0, None,
                                         ALU.is_equal),
          reads=["iden_i"], writes=["ident_bf"], tag="ident")
    sc.op("V", lambda v: v.memset(onesrow[:], 1.0), writes=["ones_row"])
    sc.op("V", lambda v: v.memset(onescol_bf[:], 1.0), writes=["ones_col_bf"])


def dma_in(sc, b, dst_ap, src_ap, key_w, keys_r=(), tag=""):
    sc.op("Q", lambda q: q.dma_start(dst_ap, src_ap), reads=keys_r,
          writes=[key_w], dma=True, tag=tag)


def load_convert_bf16(sc, b, name, hbm_ap, rows, cols, stage=None, stage_key=None,
                      dst=None):
    nchunks = rows // 128
    if stage is None:
        stage = b.sbuf(name + "_f32", [128, cols], F32)
        stage_key = f"{name}_f32"
    kfs = [stage_key] if isinstance(stage_key, str) else list(stage_key)
    wbf = dst if dst is not None else b.sbuf(name, [128, nchunks * cols], BF16)
    for c in range(nchunks):
        sc.op("Q", lambda q, c=c: q.dma_start(
            stage[:, 0:cols], hbm_ap[c * 128:(c + 1) * 128, :]),
            reads=kfs, writes=kfs[:1] if len(kfs) == 1 else kfs, dma=True,
            tag=f"ld_{name}_{c}")
        sc.op("V", lambda v, c=c: v.tensor_copy(
            wbf[:, c * cols:(c + 1) * cols], stage[:, 0:cols]),
            reads=kfs, writes=[f"{name}:{c}"], tag=f"cvt_{name}_{c}")
    return wbf


def bcast_bias(sc, b, name, hbm_ap, cols, scale=None):
    row = b.sbuf(name + "_row", [1, cols], F32)
    out = b.sbuf(name + "_bc", [128, cols], F32)
    dma_in(sc, b, row[:], hbm_ap, name + "_row", tag="ld_" + name)
    nch = (cols + 511) // 512
    for c in range(nch):
        lo, hi = c * 512, min(cols, (c + 1) * 512)
        pn = f"P{c % 2}"
        p = b[pn]
        sc.op("T", lambda t, lo=lo, hi=hi, p=p: t.matmul(
            p[:, 0:hi - lo], b["ones_row"][:], row[:, lo:hi], start=True, stop=True),
            reads=["ones_row", name + "_row"], writes=[pn])
        if scale is None:
            sc.op("V", lambda v, lo=lo, hi=hi, p=p: v.tensor_copy(
                out[:, lo:hi], p[:, 0:hi - lo]),
                reads=[pn], writes=[name])
        else:
            sc.op("V", lambda v, lo=lo, hi=hi, p=p: v.tensor_scalar(
                out[:, lo:hi], p[:, 0:hi - lo], float(scale), None, ALU.mult),
                reads=[pn], writes=[name])
    return out


def emit_ln_xnf(sc, b, xin, xn_name, prefix, want_u=None, extra_tile_cb=None):
    nc = b.nc
    xnf = b.t.get(xn_name) or b.sbuf(xn_name, [128, CT * T], BF16)
    xt = [b["SCR"][:, 0:1024], b["SCR"][:, 1024:2048]]
    stat = b.t.get(prefix + "_stat") or b.sbuf(prefix + "_stat", [128, 16], F32)
    xnt = [b.t.get(f"{prefix}_xnt{i}") or b.sbuf(f"{prefix}_xnt{i}", [128, D], BF16)
           for i in range(2)]
    ut = (b.t.get(prefix + "_ut") or b.sbuf(prefix + "_ut", [128, D], F32)) \
        if want_u is not None else None
    ptr = [b["PT0"], b["PT1"]]
    for t in range(TT):
        par = t % 2
        kx = f"SCR:{par}"
        kst = prefix + "_stat"
        kxn = f"{prefix}_xnt{par}"
        dma_in(sc, b, xt[par][:], xin.ap()[t * 128:(t + 1) * 128, :], kx,
               keys_r=[kx], tag=f"{prefix}_ldx{t}")
        sc.op("V", lambda v, par=par: v.bn_stats(stat[:, 0:6], xt[par][:, 0:512]),
              reads=[kx], writes=[kst + "a"], tag=f"{prefix}_bn0_{t}")
        sc.op("V", lambda v, par=par: v.bn_stats(stat[:, 6:12], xt[par][:, 512:1024]),
              reads=[kx], writes=[kst + "b"], tag=f"{prefix}_bn1_{t}")
        sc.op("V", lambda v: v.bn_aggr(stat[:, 12:14], stat[:, 0:12]),
              reads=[kst + "a", kst + "b"], writes=[kst + "mv"], tag=f"{prefix}_agg{t}")
        ceps, kceps = b.const_col(sc, 1e-5)
        sc.op("S", lambda s, ceps=ceps: s.activation(
            stat[:, 14:15], stat[:, 13:14], AF.Sqrt, bias=ceps[:]),
            reads=[kst + "mv", kceps], writes=[kst + "std"], tag=f"{prefix}_sqrt{t}")
        sc.op("V", lambda v: v.reciprocal(stat[:, 15:16], stat[:, 14:15]),
              reads=[kst + "std"], writes=[kst + "rstd"], tag=f"{prefix}_rstd{t}")
        if want_u is not None:
            sc.op("V", lambda v, par=par: v.tensor_scalar(
                ut[:], xt[par][:], stat[:, 12:13], stat[:, 15:16],
                ALU.subtract, ALU.mult),
                reads=[kx, kst + "mv", kst + "rstd"], writes=[prefix + "_ut"],
                tag=f"{prefix}_applyu{t}")
            want_u(sc, t, ut, prefix + "_ut")
            sc.op("V", lambda v, par=par: v.tensor_copy(xnt[par][:], ut[:]),
                  reads=[prefix + "_ut"], writes=[kxn], tag=f"{prefix}_cvt{t}")
        else:
            sc.op("V", lambda v, par=par: v.tensor_scalar(
                xnt[par][:], xt[par][:], stat[:, 12:13], stat[:, 15:16],
                ALU.subtract, ALU.mult),
                reads=[kx, kst + "mv", kst + "rstd"], writes=[kxn],
                tag=f"{prefix}_apply{t}")
        for c in range(CT):
            pk = f"PT{(t * CT + c) % 2}"
            sc.op("T", lambda tn, par=par, c=c, pk=pk: tn.transpose(
                b[pk][:], xnt[par][:, c * 128:(c + 1) * 128], b["ident_bf"][:]),
                reads=[kxn, "ident_bf"], writes=[pk], tag=f"{prefix}_tp{t}_{c}")
            sc.op("S", lambda s, t=t, c=c, pk=pk: s.copy(
                xnf[:, c * T + t * 128: c * T + (t + 1) * 128], b[pk][:]),
                reads=[pk], writes=[f"{xn_name}:{c}:{t}"], tag=f"{prefix}_ev{t}_{c}")
        if extra_tile_cb is not None:
            extra_tile_cb(sc, t, xt[par], kx)
    return xnf


def xnf_keys(xn_name, cs=range(CT), ts=range(TT)):
    return [f"{xn_name}:{c}:{t}" for c in cs for t in ts]


def emit_l0_attn(sc, b, W, xn_name, xin, xout, prefix="l0a"):
    nc = b.nc
    xnf = b[xn_name]
    NP = 8
    EPSL0 = 1.1920929e-07

    stage = [b.sbuf(prefix + f"_stage{i}", [128, D], F32) for i in range(2)]
    wp = load_convert_bf16(sc, b, prefix + "_wp", W["w_projT"].ap(), D, D,
                           stage=stage[0], stage_key=prefix + "_stage0")
    bproj = bcast_bias(sc, b, prefix + "_bproj", W["b_proj"].ap(), D)
    aofh = b.dram(prefix + "_aofh", [CT * TT, 128 * 128], BF16)

    wqkv = [b.sbuf(prefix + f"_wqkv{i}", [128, 3 * D], BF16) for i in range(2)]
    qf = b.sbuf(prefix + "_qf", [128, T], BF16)
    ktv = b.sbuf(prefix + "_ktv", [128, 2 * T], BF16)
    c2both = b.sbuf(prefix + "_c2b", [128, 65], BF16)
    aot = [b.sbuf(prefix + f"_aot{i}", [128, T], BF16) for i in range(2)]
    aofst = [b.sbuf(prefix + f"_aofst{i}", [128, 128], BF16) for i in range(2)]
    aofld = [b.sbuf(prefix + f"_aofld{i}", [128, CT * 128], BF16) for i in range(2)]
    recb = b.sbuf(prefix + "_recb", [128, 2], F32)
    xt = [b.sbuf(prefix + f"_xt{i}", [128, D], F32) for i in range(2)]
    yt = [b.sbuf(prefix + f"_yt{i}", [128, D], F32) for i in range(2)]

    pq = [b["P0"], b["P1"]]
    pkv = [b["P2"], b["P3"]]
    pc2b = b["P4"]
    pks = b["P5"]
    pout = [b["P4"], b["P5"]]
    ptr = [b["PT0"], b["PT1"]]

    XNK = xnf_keys(xn_name)

    def load_pair_weights(p):
        par = p % 2
        for wi, wn in enumerate(("wqT", "wkT", "wvT")):
            ks = f"{prefix}_stage{par}"
            src = W[wn].ap()[:, p * 128:(p + 1) * 128].rearrange(
                "(c p) j -> p c j", p=128)
            dma_in(sc, b, stage[par][:].rearrange("p (c j) -> p c j", j=128),
                   src, ks, keys_r=[ks], tag=f"{prefix}_ldw p{p} {wn}")
            sc.op("V", lambda v, par=par, wi=wi: v.tensor_copy(
                wqkv[par][:, wi * D:(wi + 1) * D], stage[par][:]),
                reads=[ks], writes=[f"{prefix}_wqkv{par}:{wi}"],
                tag=f"{prefix}_cvtw p{p} {wi}")

    load_pair_weights(0)
    for p in range(NP):
        par = p % 2
        if p + 1 < NP:
            load_pair_weights(p + 1)
        WQ = [f"{prefix}_wqkv{par}:0"]
        WK = [f"{prefix}_wqkv{par}:1"]
        WV = [f"{prefix}_wqkv{par}:2"]
        for n in range(T // 512):
            kq = f"P{n % 2}"
            def qmm(eng, par=par, n=n):
                pp = pq[n % 2]
                for c in range(CT):
                    ins = eng.matmul(
                        pp[:], wqkv[par][:, 0 * D + c * 128: 0 * D + c * 128 + 128],
                        xnf[:, c * T + n * 512: c * T + (n + 1) * 512],
                        start=(c == 0), stop=(c == CT - 1))
                return ins
            sc.op("T", qmm, reads=WQ + XNK, writes=[kq], tag=f"{prefix}_qmm p{p} n{n}")
            sc.op("S", lambda s, n=n: s.activation(
                qf[:, n * 512:(n + 1) * 512], pq[n % 2][:], AF.Relu),
                reads=[kq], writes=[f"{prefix}_qf:{n}"], tag=f"{prefix}_qev p{p} n{n}")
        for t in range(TT):
            kk = f"P{2 + t % 2}"
            def kmm(eng, par=par, t=t):
                pp = pkv[t % 2]
                for c in range(CT):
                    ins = eng.matmul(
                        pp[:, 0:128], xnf[:, c * T + t * 128: c * T + (t + 1) * 128],
                        wqkv[par][:, 1 * D + c * 128: 1 * D + c * 128 + 128],
                        start=(c == 0), stop=(c == CT - 1))
                return ins
            sc.op("T", kmm, reads=WK + XNK, writes=[kk], tag=f"{prefix}_kmm p{p} t{t}")
            sc.op("S", lambda s, t=t: s.activation(
                ktv[:, t * 128:(t + 1) * 128], pkv[t % 2][:, 0:128], AF.Relu),
                reads=[kk], writes=[f"{prefix}_kt:{t}"], tag=f"{prefix}_kev p{p} t{t}")
            def vmm(eng, par=par, t=t):
                pp = pkv[t % 2]
                for c in range(CT):
                    ins = eng.matmul(
                        pp[:, 0:128], xnf[:, c * T + t * 128: c * T + (t + 1) * 128],
                        wqkv[par][:, 2 * D + c * 128: 2 * D + c * 128 + 128],
                        start=(c == 0), stop=(c == CT - 1))
                return ins
            sc.op("T", vmm, reads=WV + XNK, writes=[kk], tag=f"{prefix}_vmm p{p} t{t}")
            sc.op("S", lambda s, t=t: s.copy(
                ktv[:, T + t * 128: T + (t + 1) * 128], pkv[t % 2][:, 0:128]),
                reads=[kk], writes=[f"{prefix}_vt:{t}"], tag=f"{prefix}_vev p{p} t{t}")
        KT = [f"{prefix}_kt:{t}" for t in range(TT)]
        VT = [f"{prefix}_vt:{t}" for t in range(TT)]
        def ksmm(eng):
            for t in range(TT):
                ins = eng.matmul(pks[:, 0:1], ktv[:, t * 128:(t + 1) * 128],
                                 b["ones_col_bf"][:], start=(t == 0),
                                 stop=(t == TT - 1))
            return ins
        sc.op("T", ksmm, reads=KT + ["ones_col_bf"], writes=["P5"],
              tag=f"{prefix}_ksum p{p}")
        sc.op("V", lambda v: v.tensor_copy(c2both[:, 64:65], pks[:, 0:1]),
              reads=["P5"], writes=[f"{prefix}_c2b:ks"],
              tag=f"{prefix}_ksev p{p}")
        for h in range(2):
            def c2mm(eng, h=h):
                for t in range(TT):
                    ins = eng.matmul(
                        pc2b[h * 64:(h + 1) * 64, 0:64],
                        ktv[:, t * 128 + h * 64: t * 128 + h * 64 + 64],
                        ktv[:, T + t * 128 + h * 64: T + t * 128 + h * 64 + 64],
                        start=(t == 0), stop=(t == TT - 1),
                        tile_position=(0, h * 64))
                return ins
            sc.op("T", c2mm, reads=KT + VT, writes=["P4"],
                  tag=f"{prefix}_c2 p{p} h{h}")
            sc.op("S", lambda s, h=h: s.copy(
                c2both[h * 64:(h + 1) * 64, 0:64], pc2b[h * 64:(h + 1) * 64, 0:64]),
                reads=["P4"], writes=[f"{prefix}_c2b:{h}"],
                tag=f"{prefix}_c2ev p{p} h{h}")
        for t in range(TT):
            for h in range(2):
                pi = (t * 2 + h) % 2
                ko = f"P{4 + pi}"
                sc.op("T", lambda eng, t=t, h=h, pi=pi: eng.matmul(
                    pout[pi][:, 0:65], qf[h * 64:(h + 1) * 64, t * 128:(t + 1) * 128],
                    c2both[h * 64:(h + 1) * 64, :], start=True, stop=True),
                    reads=[f"{prefix}_qf:{t // 4}", f"{prefix}_c2b:{h}",
                           f"{prefix}_c2b:ks"],
                    writes=[ko], tag=f"{prefix}_omm p{p} t{t} h{h}")
                sc.op("V", lambda v, h=h, pi=pi: v.tensor_scalar(
                    recb[:, h:h + 1], pout[pi][:, 64:65], EPSL0, None, ALU.add),
                    reads=[ko], writes=[f"{prefix}_eps:{h}"],
                    tag=f"{prefix}_zeps p{p} t{t} h{h}")
                sc.op("V", lambda v, h=h: v.reciprocal(
                    recb[:, h:h + 1], recb[:, h:h + 1]),
                    reads=[f"{prefix}_eps:{h}"], writes=[f"{prefix}_rec:{h}"],
                    tag=f"{prefix}_zrec p{p} t{t} h{h}")
                sc.op("V", lambda v, t=t, h=h, par=par, pi=pi: v.tensor_scalar(
                    aot[par][:, t * 128 + h * 64: t * 128 + h * 64 + 64],
                    pout[pi][:, 0:64], recb[:, h:h + 1], None, ALU.mult),
                    reads=[ko, f"{prefix}_rec:{h}"],
                    writes=[f"{prefix}_aot{par}:{t}:{h}"],
                    tag=f"{prefix}_oev p{p} t{t} h{h}")
        emit_aot_transpose_store(sc, b, prefix, aofh, aot, par, p, aofst, ptr)

    emit_out_proj_residual(sc, b, prefix, aofh, wp, f"{prefix}_wp", bproj,
                           prefix + "_bproj", xin, xout, xt, yt, aofld, pq)


def emit_out_proj_residual(sc, b, prefix, aofh, wfull, wkeyroot, bias_bcast,
                           bias_key, xin, xout, xt, yt, aofld, pq,
                           yt_keys=None):
    for t in range(TT):
        par = t % 2
        kxt = f"{prefix}_ext{par}"
        dma_in(sc, b, xt[par][:], xin.ap()[t * 128:(t + 1) * 128, :], kxt,
               keys_r=[kxt], tag=f"{prefix}_eldx t{t}")
        kld = f"{prefix}_aofld{par}"
        for c in range(CT):
            sc.op("Q", lambda q, t=t, c=c, par=par: q.dma_start(
                aofld[par][:, c * 128:(c + 1) * 128],
                aofh.ap()[c * TT + t:c * TT + t + 1, :].rearrange(
                    "a (p2 f) -> (a p2) f", p2=128)),
                reads=[f"{prefix}_aofh:{c}:{t}", kld + f":{c}"],
                writes=[kld + f":{c}"], dma=True, tag=f"{prefix}_eldao t{t} c{c}")
        for half in range(2):
            kp = f"P{half}"
            def pmm(eng, par=par, half=half):
                pp = pq[half]
                for c in range(CT):
                    ins = eng.matmul(
                        pp[:], aofld[par][:, c * 128:(c + 1) * 128],
                        wfull[:, c * D + half * 512: c * D + half * 512 + 512],
                        start=(c == 0), stop=(c == CT - 1))
                return ins
            sc.op("T", pmm,
                  reads=[kld + f":{c}" for c in range(CT)] +
                        [f"{wkeyroot}:{c}" for c in range(CT)],
                  writes=[kp], tag=f"{prefix}_pmm t{t} h{half}")
            ytk = yt_keys[par] if yt_keys else f"{prefix}_yt{par}"
            sc.op("V", lambda v, par=par, half=half: v.tensor_tensor(
                yt[par][:, half * 512:(half + 1) * 512], pq[half][:],
                bias_bcast[:, half * 512:(half + 1) * 512], ALU.add),
                reads=[kp, bias_key, ytk], writes=[f"{prefix}_ya{par}:{half}", ytk],
                tag=f"{prefix}_ebias t{t} h{half}")
            sc.op("V", lambda v, par=par, half=half: v.tensor_tensor(
                yt[par][:, half * 512:(half + 1) * 512],
                yt[par][:, half * 512:(half + 1) * 512],
                xt[par][:, half * 512:(half + 1) * 512], ALU.add),
                reads=[f"{prefix}_ya{par}:{half}", kxt],
                writes=[f"{prefix}_yb{par}:{half}", ytk],
                tag=f"{prefix}_eres t{t} h{half}")
        ytk = yt_keys[par] if yt_keys else f"{prefix}_yt{par}"
        sc.op("Q", lambda q, t=t, par=par: q.dma_start(
            xout.ap()[t * 128:(t + 1) * 128, :], yt[par][:, 0:D]),
            reads=[f"{prefix}_yb{par}:0", f"{prefix}_yb{par}:1", ytk],
            writes=[f"xo:{t}", ytk], dma=True, tag=f"{prefix}_esto t{t}")


def emit_aot_transpose_store(sc, b, prefix, aofh, aot, par, cchunk, aofst, ptr,
                             extra_reads=()):
    for t in range(TT):
        pkk = f"PT{t % 2}"
        kst = f"{prefix}_aofst{t % 2}"
        sc.op("T", lambda eng, t=t, par=par, pkk=pkk: eng.transpose(
            b[pkk][:], aot[par][:, t * 128:(t + 1) * 128], b["ident_bf"][:]),
            reads=[f"{prefix}_aot{par}:{t}:0", f"{prefix}_aot{par}:{t}:1",
                   "ident_bf"] + list(extra_reads),
            writes=[pkk], tag=f"{prefix}_aotr c{cchunk} t{t}")
        sc.op("S", lambda s, t=t, pkk=pkk: s.copy(aofst[t % 2][:], b[pkk][:]),
              reads=[pkk], writes=[kst], tag=f"{prefix}_aotrev c{cchunk} t{t}")
        sc.op("Q", lambda q, t=t, cchunk=cchunk: q.dma_start(
            aofh.ap()[cchunk * TT + t:cchunk * TT + t + 1, :].rearrange(
                "a (p2 f) -> (a p2) f", p2=128),
            aofst[t % 2][:]),
            reads=[kst], writes=[f"{prefix}_aofh:{cchunk}:{t}"], dma=True,
            tag=f"{prefix}_aofst c{cchunk} t{t}")


def host_prep(inputs):
    g = {k: np.asarray(v) for k, v in inputs.items()}
    W = {}
    wqkv = g["L0_attn_w_qkv"]
    W["l0a_wqT"] = np.ascontiguousarray(wqkv[0:1024].T)
    W["l0a_wkT"] = np.ascontiguousarray(wqkv[1024:2048].T)
    W["l0a_wvT"] = np.ascontiguousarray(wqkv[2048:3072].T)
    W["l0a_wpT"] = np.ascontiguousarray(g["L0_attn_w_proj"].T)
    W["l0a_bp"] = np.ascontiguousarray(g["L0_attn_b_proj"])
    return W


def _unused_build_l0a(with_input_x=True):
    nc = bass.Bass("TRN2", target_bir_lowering=False)
    es = ExitStack()
    b = Bufs(nc, es)
    sc = Sched()
    xin = b.dram("x_in", [T, D], F32, kind="ExternalInput")
    xout = b.dram("x_out", [T, D], F32, kind="ExternalOutput")
    W = {
        "wqT": b.dram("l0a_wqT", [D, D], F32, kind="ExternalInput"),
        "wkT": b.dram("l0a_wkT", [D, D], F32, kind="ExternalInput"),
        "wvT": b.dram("l0a_wvT", [D, D], F32, kind="ExternalInput"),
        "w_projT": b.dram("l0a_wpT", [D, D], F32, kind="ExternalInput"),
        "b_proj": b.dram("l0a_bp", [D], F32, kind="ExternalInput"),
    }
    with es:
        emit_setup(sc, b)
        emit_ln_xnf(sc, b, xin, "xnf", "ln0")
        emit_l0_attn(sc, b, W, "xnf", xin, xout)
        sc.build(nc, es)
        sc.emit(nc)
    return nc





def emit_l1_attn(sc, b, W, xn_name, xin, xout, prefix="l1t"):
    nc = b.nc
    xnf = b[xn_name]
    NP = 8
    stage = [b.sbuf(prefix + f"_stage{i}", [128, D], F32) for i in range(2)]
    wo = load_convert_bf16(sc, b, prefix + "_wo", W["Wo"].ap(), D, D,
                           stage=stage[0], stage_key=prefix + "_stage0")
    bo = bcast_bias(sc, b, prefix + "_bo", W["bo"].ap(), D)
    aofh = b.dram(prefix + "_aofh", [CT * TT, 128 * 128], BF16)

    wqkv = [b.sbuf(prefix + f"_wqkv{i}", [128, 3 * D], BF16) for i in range(2)]
    qf = b.sbuf(prefix + "_qf", [128, T], BF16)
    kf = b.sbuf(prefix + "_kf", [128, T], BF16)
    vt2 = b.sbuf(prefix + "_vt2", [128, TT * 130], BF16)
    expb = b.sbuf(prefix + "_expb0", [128, 1024], BF16)
    aot = [b.sbuf(prefix + f"_aot{i}", [128, T], BF16) for i in range(2)]
    aofst = [b.sbuf(prefix + f"_aofst{i}", [128, 128], BF16) for i in range(2)]
    aofld = [b.sbuf(prefix + f"_aofld{i}", [128, CT * 128], BF16) for i in range(2)]
    bqk = b.sbuf(prefix + "_bqk", [128, 16], F32)
    bvrow = b.sbuf(prefix + "_bvrow", [1, D], F32)
    bvb = b.sbuf(f"{prefix}_XR:0", [128, 128], F32)
    masks = b.sbuf(prefix + "_masks", [128, 4 * 512], BF16)
    recb = b.sbuf(prefix + "_recb", [128, 4], F32)
    xt = [b.sbuf(prefix + f"_xt{i}", [128, D], F32) for i in range(2)]
    yt = stage

    pq = [b["P0"], b["P1"]]
    pq2 = [b["P0"], b["P1"], b["P2"], b["P3"]]
    pav = [b["P4"], b["P5"]]
    pkv = [b["P2"], b["P3"]]
    pA = [b["P2"], b["P3"], b["P4"], b["P5"]]
    ptr = [b["PT0"], b["PT1"]]
    XNK = xnf_keys(xn_name)

    dma_in(sc, b, bqk[:, 0:8], W["bq_dev"].ap(), prefix + "_bq", tag="ld_bq")
    dma_in(sc, b, bqk[:, 8:16], W["bk_dev"].ap(), prefix + "_bk", tag="ld_bk")
    dma_in(sc, b, bvrow[:], W["bv_row"].ap(), prefix + "_bvrow", tag="ld_bv")
    for jj in range(2):
        sc.op("P", lambda g, jj=jj: g.iota(
            stage[1][:, 0:256], pattern=[[1, 256]],
            base=-128 * jj, channel_multiplier=-1,
            allow_small_or_imprecise_dtypes=True),
            reads=[prefix + "_stage1"], writes=[prefix + "_stage1"],
            tag=f"{prefix}_miota{jj}")
        sc.op("P", lambda g, jj=jj: g.tensor_scalar(
            masks[:, jj * 256:(jj + 1) * 256], stage[1][:, 0:256],
            0.0, None, ALU.is_ge),
            reads=[prefix + "_stage1"], writes=[f"{prefix}_mask:{jj}",
                                                prefix + "_stage1"],
            tag=f"{prefix}_mge{jj}")

    def load_pair_weights(p):
        par = p % 2
        for wi, wn in enumerate(("Wq_big", "Wk_big", "Wv")):
            ks = f"{prefix}_stage{par}"
            src = W[wn].ap()[:, p * 128:(p + 1) * 128].rearrange(
                "(c p) j -> p c j", p=128)
            dma_in(sc, b, stage[par][:].rearrange("p (c j) -> p c j", j=128),
                   src, ks, keys_r=[ks], tag=f"{prefix}_ldw p{p} {wn}")
            sc.op("V", lambda v, par=par, wi=wi: v.tensor_copy(
                wqkv[par][:, wi * D:(wi + 1) * D], stage[par][:]),
                reads=[ks], writes=[f"{prefix}_wqkv{par}:{wi}"],
                tag=f"{prefix}_cvtw p{p} {wi}")

    load_pair_weights(0)
    for p in range(NP):
        par = p % 2
        if p + 1 < NP:
            load_pair_weights(p + 1)
        WQ = [f"{prefix}_wqkv{par}:0"]
        WK = [f"{prefix}_wqkv{par}:1"]
        WV = [f"{prefix}_wqkv{par}:2"]
        sc.op("T", lambda eng, p=p: eng.matmul(
            b["P5"][:, 0:128], b["ones_row"][:], bvrow[:, p * 128:(p + 1) * 128],
            start=True, stop=True),
            reads=["ones_row", prefix + "_bvrow"], writes=["P5"],
            tag=f"{prefix}_bvb p{p}")
        sc.op("V", lambda v: v.tensor_copy(bvb[:], b["P5"][:, 0:128]),
              reads=["P5"], writes=[f"{prefix}_XR:0"], tag=f"{prefix}_bvbev p{p}")
        for n in range(T // 512):
            for wi, (dst, bcol) in enumerate(((qf, 0), (kf, 1))):
                kq = f"P{n % 2}"
                def qkmm(eng, par=par, n=n, wi=wi):
                    pp = pq[n % 2]
                    for c in range(CT):
                        ins = eng.matmul(
                            pp[:], wqkv[par][:, wi * D + c * 128: wi * D + c * 128 + 128],
                            xnf[:, c * T + n * 512: c * T + (n + 1) * 512],
                            start=(c == 0), stop=(c == CT - 1))
                    return ins
                sc.op("T", qkmm, reads=(WQ if wi == 0 else WK) + XNK,
                      writes=[kq], tag=f"{prefix}_qkmm p{p} n{n} w{wi}")
                sc.op("V", lambda v, n=n, wi=wi, dst=dst, bcol=bcol, p=p: v.tensor_scalar(
                    dst[:, n * 512:(n + 1) * 512], pq[n % 2][:],
                    bqk[:, bcol * 8 + p: bcol * 8 + p + 1], None, ALU.add),
                    reads=[kq, prefix + ("_bq" if wi == 0 else "_bk")],
                    writes=[f"{prefix}_{'qf' if wi == 0 else 'kf'}:{n}"],
                    tag=f"{prefix}_qkev p{p} n{n} w{wi}")
        sc.op("V", lambda v: v.memset(
            vt2[:].rearrange("p (t c) -> p t c", c=130)[:, :, 64:65], 1.0),
            writes=[f"{prefix}_vones_a"], tag=f"{prefix}_vones p{p}")
        sc.op("V", lambda v: v.memset(
            vt2[:].rearrange("p (t c) -> p t c", c=130)[:, :, 129:130], 1.0),
            writes=[f"{prefix}_vones_b"], tag=f"{prefix}_vones2 p{p}")
        for t in range(TT):
            kk = f"P{2 + t % 2}"
            def vmm(eng, par=par, t=t):
                pp = pkv[t % 2]
                for c in range(CT):
                    ins = eng.matmul(
                        pp[:, 0:128], xnf[:, c * T + t * 128: c * T + (t + 1) * 128],
                        wqkv[par][:, 2 * D + c * 128: 2 * D + c * 128 + 128],
                        start=(c == 0), stop=(c == CT - 1))
                return ins
            sc.op("T", vmm, reads=WV + XNK, writes=[kk], tag=f"{prefix}_vmm p{p} t{t}")
            for h in range(2):
                sc.op("V", lambda v, t=t, h=h, pi=t % 2: v.tensor_tensor(
                    vt2[:, t * 130 + h * 65: t * 130 + h * 65 + 64],
                    pkv[pi][:, h * 64:(h + 1) * 64], bvb[:, h * 64:(h + 1) * 64],
                    ALU.add),
                    reads=[kk, f"{prefix}_XR:0", f"{prefix}_vones_a"],
                    writes=[f"{prefix}_vt:{t}:{h}"], tag=f"{prefix}_vev p{p} t{t} h{h}")
        for h in range(2):
            base = 64 * h
            QK = [f"{prefix}_qf:{n}" for n in range(8)]
            KK = [f"{prefix}_kf:{n}" for n in range(8)]
            for qc in range(16):
                nk = 2 * qc + 2
                for kc in range(nk):
                    pi = kc % 4
                    ks = f"P{pi}"
                    ke = f"{prefix}_expb:{pi}"
                    sc.op("T", lambda eng, base=base, kc=kc, qc=qc, pi=pi: eng.matmul(
                        pq2[pi][:, 0:256], kf[base:base + 64, kc * 128:(kc + 1) * 128],
                        qf[base:base + 64, qc * 256:(qc + 1) * 256],
                        start=True, stop=True),
                        reads=QK + KK, writes=[ks],
                        tag=f"{prefix}_smm p{p} h{h} q{qc} k{kc}")
                    sc.op("S", lambda s, pi=pi: s.activation(
                        expb[:, pi * 256:(pi + 1) * 256], pq2[pi][:, 0:256],
                        AF.Exp, scale=0.125),
                        reads=[ks], writes=[ke], tag=f"{prefix}_sexp p{p} h{h} q{qc} k{kc}")
                    if kc >= 2 * qc:
                        jj = kc - 2 * qc
                        sc.op("V", lambda v, pi=pi, jj=jj: v.tensor_tensor(
                            expb[:, pi * 256:(pi + 1) * 256],
                            expb[:, pi * 256:(pi + 1) * 256],
                            masks[:, jj * 256:(jj + 1) * 256], ALU.mult),
                            reads=[ke, f"{prefix}_mask:{jj}"], writes=[ke],
                            tag=f"{prefix}_smask p{p} h{h} q{qc} k{kc}")
                    def avmm(eng, pi=pi, kc=kc, h=h, nk=nk):
                        for qs in range(2):
                            ins = eng.matmul(
                                pav[qs][:, 0:65],
                                expb[:, pi * 256 + qs * 128: pi * 256 + qs * 128 + 128],
                                vt2[:, kc * 130 + h * 65: kc * 130 + h * 65 + 65],
                                start=(kc == 0), stop=(kc == nk - 1))
                        return ins
                    sc.op("T", avmm,
                          reads=[ke, f"{prefix}_vt:{kc}:{h}"],
                          writes=["P4", "P5"],
                          tag=f"{prefix}_av p{p} h{h} q{qc} k{kc}")
                for qs in range(2):
                    tglob = qc * 2 + qs
                    kA = f"P{4 + qs}"
                    sc.op("V", lambda v, qs=qs: v.reciprocal(
                        recb[:, qs:qs + 1], pav[qs][:, 64:65]),
                        reads=[kA], writes=[f"{prefix}_rec:{qs}"],
                        tag=f"{prefix}_arec p{p} h{h} q{qc} s{qs}")
                    sc.op("V", lambda v, qs=qs, tglob=tglob, h=h, par=par: v.tensor_scalar(
                        aot[par][:, tglob * 128 + h * 64: tglob * 128 + h * 64 + 64],
                        pav[qs][:, 0:64], recb[:, qs:qs + 1], None, ALU.mult),
                        reads=[kA, f"{prefix}_rec:{qs}"],
                        writes=[f"{prefix}_aot{par}:{tglob}:{h}"],
                        tag=f"{prefix}_aev p{p} h{h} q{qc} s{qs}")
        emit_aot_transpose_store(sc, b, prefix, aofh, aot, par, p, aofst, ptr)

    emit_out_proj_residual(sc, b, prefix, aofh, wo, f"{prefix}_wo", bo,
                           prefix + "_bo", xin, xout, xt, yt, aofld, pq,
                           yt_keys=[prefix + "_stage0", prefix + "_stage1"])


def emit_l0_ffn(sc, b, W, xn_name, xin, xout, prefix, w1bf=None, w2bf=None,
                convert_moe=True):
    nc = b.nc
    xnf = b[xn_name]
    SCL = 1.0 / 5.656854249492381

    AR1 = b.sbuf(prefix + "_AR1", [128, 16384], BF16)
    AR2 = b.sbuf(prefix + "_AR2", [128, 16384], BF16)
    stage0 = b["SCR"]
    stage = [stage0[:, 0:1024], stage0[:, 1024:2048]]
    q4 = AR1[:, 0:4096]
    k4 = AR1[:, 4096:8192]
    vt4 = AR1[:, 8192:12416]
    expb = AR1[:, 12416:13440]
    aot = [AR2[:, 0:4096], AR2[:, 4096:8192]]
    wqkv = [AR2[:, 8192:11264], AR2[:, 11264:14336]]
    ow = AR2[:, 8192:12288]
    aof = b.sbuf(prefix + "_aof", [128, 16384], BF16)
    WA = AR1
    WB = AR2
    rw = b.sbuf(prefix + "_rwS", [128, 64], BF16)
    qkb = b.sbuf(prefix + "_qkb", [128, 8], F32)
    obd = b.sbuf(prefix + "_obd", [128, 8], F32)
    b1d = b.sbuf(prefix + "_b1d", [128, 128], F32)
    carena = b.sbuf(prefix + "_car", [128, 8 * TT], F32)
    rt = b.sbuf(prefix + "_rt", [128, 80], F32)
    lnq = b.sbuf(prefix + "_lnq", [128, 16], F32)
    recb = b.sbuf(prefix + "_recb", [128, 4], F32)
    XR = b.sbuf(prefix + "_XR", [128, 8 * D], F32)
    vbrow = XR[0:1, 0:512]
    rbrow = XR[0:1, 512:520]
    rbb = XR[:, 640:648]
    bvb = XR[:, 768:896]

    pq = [b["P0"], b["P1"]]
    pq2 = [b["P0"], b["P1"], b["P2"], b["P3"]]
    pav = [b["P4"], b["P5"]]
    pA = [b["P2"], b["P3"], b["P4"], b["P5"]]
    XNK = xnf_keys(xn_name)

    G1 = prefix + "_AR1g"
    G2 = prefix + "_AR2g"
    _g1pref = (f"{prefix}_q4", f"{prefix}_k4", f"{prefix}_vt", f"{prefix}_expb",
               f"{prefix}_WA", f"{prefix}_vones")
    _g2pref = (f"{prefix}_aot", f"{prefix}_wqkv", f"{prefix}_ow:", f"{prefix}_WB")

    _orig_op = sc.op

    def _op(stream, fn, reads=(), writes=(), **kw):
        reads = list(reads)
        ks = list(reads) + list(writes)
        if any(k.startswith(_g1pref) for k in ks):
            reads.append(G1)
        if any(k.startswith(_g2pref) for k in ks):
            reads.append(G2)
        return _orig_op(stream, fn, reads=reads, writes=writes, **kw)

    class _SC:
        op = staticmethod(_op)
    sc2 = sc
    sc = _SC()

    def guard(gkey):
        _orig_op("V", lambda v: v.engine_nop(), reads=[], writes=[gkey],
                 tag="guard_" + gkey)

    cvb = [AR1[:, 13440:15488], AR2[:, 14336:16384]]
    _g1pref = _g1pref + (f"{prefix}_cvA",)
    _g2pref = _g2pref + (f"{prefix}_cvB",)
    CVK = [f"{prefix}_cvA", f"{prefix}_cvB"]

    def preconvert_moe():
        idx = 0
        for which, hbm_f32, hbm_bf, nch, cw in ((1, W["w1"], w1bf, 8, 2048),
                                                (2, W["w2"], w2bf, 16, 1024)):
            an = prefix + ("_WA" if which == 1 else "_WB")
            for e in range(8):
                for c in range(nch):
                    par = idx % 2
                    idx += 1
                    slots = [f"{prefix}_XR:{1 + 2 * par}", f"{prefix}_XR:{2 + 2 * par}"]
                    stg = XR[:, (1 + 2 * par) * 1024: (1 + 2 * par) * 1024 + cw]
                    sc.op("P", lambda q, stg=stg, hbm_f32=hbm_f32, e=e, c=c: q.dma_start(
                        stg, hbm_f32.ap()[e, c * 128:(c + 1) * 128, :]),
                        reads=slots, writes=slots, dma=True, dma_group="pc",
                        tag=f"{prefix}_pcl{which} e{e} c{c}")
                    sc.op("P", lambda v, stg=stg, par=par, cw=cw: v.tensor_copy(
                        cvb[par][:, 0:cw], stg),
                        reads=slots, writes=[CVK[par]],
                        tag=f"{prefix}_pcc{which} e{e} c{c}")
                    sc.op("P", lambda q, par=par, cw=cw, hbm_bf=hbm_bf, e=e, c=c: q.dma_start(
                        hbm_bf.ap()[e, c * 128:(c + 1) * 128, :], cvb[par][:, 0:cw]),
                        reads=[CVK[par]], writes=[f"{an}hbm:{e}:{c}", CVK[par]],
                        dma=True, dma_group="pc",
                        tag=f"{prefix}_pcs{which} e{e} c{c}")

    if convert_moe:
        preconvert_moe()

    dma_in(sc, b, qkb[:, 0:4], W["qb_dev"].ap(), prefix + "_qb", tag="ld_qb")
    dma_in(sc, b, qkb[:, 4:8], W["kb_dev"].ap(), prefix + "_kb", tag="ld_kb")
    dma_in(sc, b, obd[:], W["ob_dev"].ap(), prefix + "_obd", tag="ld_ob")
    dma_in(sc, b, b1d[:], W["b1_dev"].ap(), prefix + "_b1d", tag="ld_b1")
    dma_in(sc, b, vbrow[:], W["vb_row"].ap(), f"{prefix}_XR:0", tag="ld_vb")
    dma_in(sc, b, rbrow[:], W["rb_row"].ap(), f"{prefix}_XR:0", tag="ld_rb")
    sc.op("T", lambda t: t.matmul(b["P5"][:, 0:8], b["ones_row"][:], rbrow[:],
                                  start=True, stop=True),
          reads=["ones_row", f"{prefix}_XR:0"], writes=["P5"], tag="rbb_mm")
    sc.op("V", lambda v: v.tensor_copy(rbb[:], b["P5"][:, 0:8]),
          reads=["P5"], writes=[f"{prefix}_XR:0"], tag="rbb_ev")


    def load_group_weights(g):
        par = g % 2
        for wi, wn in enumerate(("qw", "kw", "vw")):
            ks = f"SCR:{par}"
            srcap = W[wn].ap()[:, g * 128:(g + 1) * 128].rearrange(
                "(c p) j -> p c j", p=128)
            dma_in(sc, b, stage[par].rearrange("p (c j) -> p c j", j=128),
                   srcap, ks, keys_r=[ks], tag=f"{prefix}_ldw g{g} {wn}")
            sc.op("V", lambda v, par=par, wi=wi: v.tensor_copy(
                wqkv[par][:, wi * 1024:(wi + 1) * 1024], stage[par]),
                reads=[ks], writes=[f"{prefix}_wqkv{par}:{wi}"],
                tag=f"{prefix}_cvtw g{g} {wi}")

    load_group_weights(0)
    for g in range(4):
        par = g % 2
        if g + 1 < 4:
            load_group_weights(g + 1)
        WQ = [f"{prefix}_wqkv{par}:0"]
        WK = [f"{prefix}_wqkv{par}:1"]
        WV = [f"{prefix}_wqkv{par}:2"]
        sc.op("T", lambda eng, g=g: eng.matmul(
            b["P5"][:, 0:128], b["ones_row"][:], vbrow[:, g * 128:(g + 1) * 128],
            start=True, stop=True),
            reads=["ones_row", f"{prefix}_XR:0"], writes=["P5"],
            tag=f"{prefix}_bvb g{g}")
        sc.op("V", lambda v: v.tensor_copy(bvb[:], b["P5"][:, 0:128]),
              reads=["P5"], writes=[f"{prefix}_XR:0"], tag=f"{prefix}_bvbev g{g}")
        for wi, (dst, bc) in enumerate(((q4, 0), (k4, 1))):
            for n in range(8):
                kq = f"P{n % 2}"
                def qmm(eng, par=par, n=n, wi=wi):
                    pp = pq[n % 2]
                    for c in range(CT):
                        ins = eng.matmul(
                            pp[:], wqkv[par][:, wi * 1024 + c * 128: wi * 1024 + c * 128 + 128],
                            xnf[:, c * T + n * 512: c * T + (n + 1) * 512],
                            start=(c == 0), stop=(c == CT - 1))
                    return ins
                sc.op("T", qmm, reads=(WQ if wi == 0 else WK) + XNK, writes=[kq],
                      tag=f"{prefix}_qkmm g{g} w{wi} n{n}")
                sc.op("V", lambda v, n=n, wi=wi, dst=dst, bc=bc, g=g: v.tensor_scalar(
                    dst[:, n * 512:(n + 1) * 512], pq[n % 2][:],
                    qkb[:, bc * 4 + g: bc * 4 + g + 1], None, ALU.add),
                    reads=[kq, prefix + ("_qb" if wi == 0 else "_kb")],
                    writes=[f"{prefix}_{'q4' if wi == 0 else 'k4'}:{n}"],
                    tag=f"{prefix}_qkev g{g} w{wi} n{n}")
        sc.op("V", lambda v: v.memset(
            vt4[:].rearrange("p (t h w) -> p t h w", h=4, w=33)[:, :, :, 32:33], 1.0),
            writes=[f"{prefix}_vones"], tag=f"{prefix}_vones g{g}")
        for t in range(TT):
            kk = f"P{2 + t % 2}"
            def vmm(eng, par=par, t=t):
                pp = pA[t % 2]
                for c in range(CT):
                    ins = eng.matmul(
                        pp[:, 0:128], xnf[:, c * T + t * 128: c * T + (t + 1) * 128],
                        wqkv[par][:, 2 * 1024 + c * 128: 2 * 1024 + c * 128 + 128],
                        start=(c == 0), stop=(c == CT - 1))
                return ins
            sc.op("T", vmm, reads=WV + XNK, writes=[kk], tag=f"{prefix}_vmm g{g} t{t}")
            for hh in range(4):
                sc.op("V", lambda v, t=t, hh=hh, pi=t % 2: v.tensor_tensor(
                    vt4[:, t * 132 + hh * 33: t * 132 + hh * 33 + 32],
                    pA[pi][:, hh * 32:(hh + 1) * 32], bvb[:, hh * 32:(hh + 1) * 32],
                    ALU.add),
                    reads=[kk, f"{prefix}_XR:0", f"{prefix}_vones"],
                    writes=[f"{prefix}_vt:{t}:{hh}"], tag=f"{prefix}_vev g{g} t{t} h{hh}")
        QK = [f"{prefix}_q4:{n}" for n in range(8)]
        KK = [f"{prefix}_k4:{n}" for n in range(8)]
        for hh in range(4):
            base = 32 * hh
            for qc in range(16):
                for kc in range(TT):
                    pi = kc % 4
                    ks = f"P{pi}"
                    ke = f"{prefix}_expb:{pi}"
                    sc.op("T", lambda eng, base=base, kc=kc, qc=qc, pi=pi: eng.matmul(
                        pq2[pi][:, 0:256], k4[base:base + 32, kc * 128:(kc + 1) * 128],
                        q4[base:base + 32, qc * 256:(qc + 1) * 256],
                        start=True, stop=True, tile_position=(base, 0)),
                        reads=QK + KK, writes=[ks],
                        tag=f"{prefix}_smm g{g} h{hh} q{qc} k{kc}")
                    sc.op("S", lambda s, pi=pi: s.activation(
                        expb[:, pi * 256:(pi + 1) * 256],
                        pq2[pi][:, 0:256], AF.Exp, scale=SCL),
                        reads=[ks], writes=[ke],
                        tag=f"{prefix}_sexp g{g} h{hh} q{qc} k{kc}")
                    def avmm(eng, kc=kc, hh=hh, pi=pi):
                        for qs in range(2):
                            ins = eng.matmul(
                                pav[qs][:, 0:33],
                                expb[:, pi * 256 + qs * 128: pi * 256 + qs * 128 + 128],
                                vt4[:, kc * 132 + hh * 33: kc * 132 + hh * 33 + 33],
                                start=(kc == 0), stop=(kc == TT - 1))
                        return ins
                    sc.op("T", avmm,
                          reads=[ke, f"{prefix}_vt:{kc}:{hh}"],
                          writes=["P4", "P5"],
                          tag=f"{prefix}_av g{g} h{hh} q{qc} k{kc}")
                for qs in range(2):
                    tglob = qc * 2 + qs
                    kA = f"P{4 + qs}"
                    sc.op("V", lambda v, qs=qs: v.reciprocal(
                        recb[:, qs:qs + 1], pav[qs][:, 32:33]),
                        reads=[kA], writes=[f"{prefix}_rec:{qs}"],
                        tag=f"{prefix}_arec g{g} h{hh} q{qc} s{qs}")
                    sc.op("V", lambda v, qs=qs, tglob=tglob, hh=hh, par=par: v.tensor_scalar(
                        aot[par][:, tglob * 128 + hh * 32: tglob * 128 + hh * 32 + 32],
                        pav[qs][:, 0:32],
                        recb[:, qs:qs + 1], None, ALU.mult),
                        reads=[kA, f"{prefix}_rec:{qs}"],
                        writes=[f"{prefix}_aot{par}:{tglob}:{hh}"],
                        tag=f"{prefix}_aev g{g} h{hh} q{qc} s{qs}")
        for t in range(TT):
            pkk = f"PT{t % 2}"
            sc.op("T", lambda eng, t=t, par=par, pkk=pkk: eng.transpose(
                b[pkk][:], aot[par][:, t * 128:(t + 1) * 128], b["ident_bf"][:]),
                reads=[f"{prefix}_aot{par}:{t}:{hh2}" for hh2 in range(4)] + ["ident_bf"],
                writes=[pkk], tag=f"{prefix}_aotr g{g} t{t}")
            sc.op("S", lambda s, t=t, g=g, pkk=pkk: s.copy(
                aof[:, g * 4096 + t * 128: g * 4096 + (t + 1) * 128], b[pkk][:]),
                reads=[pkk], writes=[f"{prefix}_aofA:{g}:{t}"],
                tag=f"{prefix}_aotrev g{g} t{t}")

    AOK = [f"{prefix}_aofA:{g}:{t}" for g in range(4) for t in range(TT)]
    OWK = [f"{prefix}_ow:{c}" for c in range(4)]
    guard(G2)
    load_convert_bf16(sc, b, prefix + "_ow", W["ow"].ap(), 512, D,
                      stage=stage[0], stage_key="SCR:0", dst=ow)
    load_convert_bf16(sc, b, prefix + "_rw", W["rw"].ap(), D, 8,
                      stage=stage[0], stage_key="SCR:0", dst=rw)
    for c in range(CT):
        for n in range(8):
            kq = f"P{n % 2}"
            def pmm(eng, c=c, n=n):
                pp = pq[n % 2]
                for a in range(4):
                    ins = eng.matmul(
                        pp[:], ow[:, a * 1024 + c * 128: a * 1024 + c * 128 + 128],
                        aof[:, a * 4096 + n * 512: a * 4096 + (n + 1) * 512],
                        start=(a == 0), stop=(a == 3))
                return ins
            sc.op("T", pmm, reads=AOK + OWK, writes=[kq],
                  tag=f"{prefix}_mlamm c{c} n{n}")
            sc.op("V", lambda v, c=c, n=n: v.tensor_scalar(
                xnf[:, c * T + n * 512: c * T + (n + 1) * 512], pq[n % 2][:],
                obd[:, c:c + 1], None, ALU.add),
                reads=[kq, prefix + "_obd"],
                writes=[f"{xn_name}:{c}:{4 * n + i}" for i in range(4)],
                tag=f"{prefix}_mlaev c{c} n{n}")

    RWK = [f"{prefix}_rw:{c}" for c in range(CT)]
    for t in range(TT):
        kr = f"P{4 + t % 2}"
        def rmm(eng, t=t):
            pp = pA[2 + t % 2]
            for c in range(CT):
                ins = eng.matmul(
                    pp[:, 0:8], xnf[:, c * T + t * 128: c * T + (t + 1) * 128],
                    rw[:, c * 8:(c + 1) * 8], start=(c == 0), stop=(c == CT - 1))
            return ins
        sc.op("T", rmm, reads=RWK + xnf_keys(xn_name, ts=[t]), writes=[kr],
              tag=f"{prefix}_rmm t{t}")
        RT = prefix + "_rt"
        sc.op("V", lambda v, t=t: v.tensor_tensor(
            rt[:, 0:8], pA[2 + t % 2][:, 0:8], rbb[:], ALU.add),
            reads=[kr, f"{prefix}_XR:0"], writes=[RT + ":l"], tag=f"{prefix}_rb t{t}")
        sc.op("S", lambda s: s.activation(rt[:, 8:16], rt[:, 0:8], AF.Exp,
                                          accum_out=rt[:, 16:17]),
              reads=[RT + ":l"], writes=[RT + ":e", RT + ":s"], tag=f"{prefix}_rexp t{t}")
        sc.op("V", lambda v: v.reciprocal(rt[:, 17:18], rt[:, 16:17]),
              reads=[RT + ":s"], writes=[RT + ":ri"], tag=f"{prefix}_rrec t{t}")
        sc.op("V", lambda v: v.tensor_scalar(rt[:, 24:32], rt[:, 8:16],
                                             rt[:, 17:18], None, ALU.mult),
              reads=[RT + ":e", RT + ":ri"], writes=[RT + ":p"], tag=f"{prefix}_rp t{t}")
        sc.op("V", lambda v: v.max(out=rt[:, 32:40], in_=rt[:, 24:32]),
              reads=[RT + ":p"], writes=[RT + ":m8"], tag=f"{prefix}_rmax t{t}")
        sc.op("V", lambda v: v.tensor_scalar(rt[:, 40:48], rt[:, 24:32],
                                             rt[:, 32:33], None, ALU.is_equal),
              reads=[RT + ":p", RT + ":m8"], writes=[RT + ":e1"], tag=f"{prefix}_req1 t{t}")
        sc.op("V", lambda v: v.tensor_tensor_scan(rt[:, 48:56], rt[:, 40:48],
                                                  rt[:, 40:48], 0.0, ALU.add, ALU.bypass),
              reads=[RT + ":e1"], writes=[RT + ":c1"], tag=f"{prefix}_rc1 t{t}")
        sc.op("V", lambda v: v.tensor_scalar(rt[:, 56:64], rt[:, 48:56], 1.0,
                                             None, ALU.is_equal),
              reads=[RT + ":c1"], writes=[RT + ":t1"], tag=f"{prefix}_rt1 t{t}")
        sc.op("V", lambda v: v.tensor_tensor(rt[:, 40:48], rt[:, 40:48],
                                             rt[:, 56:64], ALU.mult),
              reads=[RT + ":e1", RT + ":t1"], writes=[RT + ":x1"], tag=f"{prefix}_rx1 t{t}")
        sc.op("V", lambda v: v.tensor_tensor(rt[:, 18:19], rt[:, 32:33],
                                             rt[:, 33:34], ALU.is_equal),
              reads=[RT + ":m8"], writes=[RT + ":em"], tag=f"{prefix}_rem t{t}")
        sc.op("V", lambda v: v.tensor_scalar(rt[:, 19:20], rt[:, 18:19], 1.0,
                                             None, ALU.add),
              reads=[RT + ":em"], writes=[RT + ":th"], tag=f"{prefix}_rth t{t}")
        sc.op("V", lambda v: v.tensor_scalar(rt[:, 64:72], rt[:, 24:32],
                                             rt[:, 33:34], None, ALU.is_equal),
              reads=[RT + ":p", RT + ":m8"], writes=[RT + ":e2"], tag=f"{prefix}_req2 t{t}")
        sc.op("V", lambda v: v.tensor_tensor_scan(rt[:, 72:80], rt[:, 64:72],
                                                  rt[:, 64:72], 0.0, ALU.add, ALU.bypass),
              reads=[RT + ":e2"], writes=[RT + ":c2"], tag=f"{prefix}_rc2 t{t}")
        sc.op("V", lambda v: v.tensor_scalar(rt[:, 72:80], rt[:, 72:80],
                                             rt[:, 19:20], None, ALU.is_equal),
              reads=[RT + ":c2", RT + ":th"], writes=[RT + ":t2"], tag=f"{prefix}_rt2 t{t}")
        sc.op("V", lambda v: v.tensor_tensor(rt[:, 64:72], rt[:, 64:72],
                                             rt[:, 72:80], ALU.mult),
              reads=[RT + ":e2", RT + ":t2"], writes=[RT + ":x2"], tag=f"{prefix}_rx2 t{t}")
        sc.op("V", lambda v: v.tensor_tensor(rt[:, 20:21], rt[:, 32:33],
                                             rt[:, 33:34], ALU.add),
              reads=[RT + ":m8"], writes=[RT + ":ws"], tag=f"{prefix}_rws t{t}")
        sc.op("V", lambda v: v.tensor_scalar(rt[:, 20:21], rt[:, 20:21], 1e-8,
                                             None, ALU.add),
              reads=[RT + ":ws"], writes=[RT + ":ws2"], tag=f"{prefix}_rws2 t{t}")
        sc.op("V", lambda v: v.reciprocal(rt[:, 21:22], rt[:, 20:21]),
              reads=[RT + ":ws2"], writes=[RT + ":rw"], tag=f"{prefix}_rrw t{t}")
        sc.op("V", lambda v: v.tensor_scalar(rt[:, 22:23], rt[:, 32:33],
                                             rt[:, 21:22], 0.15, ALU.mult, ALU.mult),
              reads=[RT + ":m8", RT + ":rw"], writes=[RT + ":w1"], tag=f"{prefix}_rw1 t{t}")
        sc.op("V", lambda v: v.tensor_scalar(rt[:, 23:24], rt[:, 33:34],
                                             rt[:, 21:22], 0.15, ALU.mult, ALU.mult),
              reads=[RT + ":m8", RT + ":rw"], writes=[RT + ":w2"], tag=f"{prefix}_rw2 t{t}")
        sc.op("V", lambda v, t=t: v.tensor_scalar(rt[:, 40:48], rt[:, 40:48],
                                                  rt[:, 22:23], None, ALU.mult),
              reads=[RT + ":x1", RT + ":w1"], writes=[RT + ":y1"], tag=f"{prefix}_ry1 t{t}")
        sc.op("V", lambda v, t=t: v.tensor_scalar(rt[:, 64:72], rt[:, 64:72],
                                                  rt[:, 23:24], None, ALU.mult),
              reads=[RT + ":x2", RT + ":w2"], writes=[RT + ":y2"], tag=f"{prefix}_ry2 t{t}")
        sc.op("V", lambda v, t=t: v.tensor_tensor(carena[:, t * 8:(t + 1) * 8],
                                                  rt[:, 40:48], rt[:, 64:72], ALU.add),
              reads=[RT + ":y1", RT + ":y2"], writes=[f"{prefix}_car:{t}"],
              tag=f"{prefix}_rc t{t}")

    def load_w12(which, e, create):
        arena, hbm_bf, hbm_f32, nch, cw = (
            (WA, w1bf, W["w1"], 8, 2048) if which == 1 else
            (WB, w2bf, W["w2"], 16, 1024))
        an = prefix + ("_WA" if which == 1 else "_WB")
        for c in range(nch):
            ak = f"{an}:{c}"
            if create:
                kss = ["SCR:0", "SCR:1"] if cw == 2048 else [f"SCR:{c % 2}"]
                stg = stage0[:, 0:cw] if cw == 2048 else stage[c % 2]
                sc.op("Q", lambda q, c=c, stg=stg, e=e: q.dma_start(
                    stg, hbm_f32.ap()[e, c * 128:(c + 1) * 128, :]),
                    reads=kss, writes=kss, dma=True,
                    tag=f"{prefix}_ldwf{which} e{e} c{c}")
                sc.op("V", lambda v, c=c, cw=cw, arena=arena, stg=stg: v.tensor_copy(
                    arena[:, c * cw:(c + 1) * cw], stg),
                    reads=kss, writes=[ak], tag=f"{prefix}_cvw{which} e{e} c{c}")
                sc.op("Q", lambda q, c=c, cw=cw, arena=arena, e=e: q.dma_start(
                    hbm_bf.ap()[e, c * 128:(c + 1) * 128, :],
                    arena[:, c * cw:(c + 1) * cw]),
                    reads=[ak], writes=[f"{an}hbm:{e}:{c}"], dma=True,
                    tag=f"{prefix}_stw{which} e{e} c{c}")
            else:
                dma_in(sc, b, arena[:, c * cw:(c + 1) * cw],
                       hbm_bf.ap()[e, c * 128:(c + 1) * 128, :], ak,
                       keys_r=[ak, f"{an}hbm:{e}:{c}"],
                       tag=f"{prefix}_ldwb{which} e{e} c{c}")

    WAK = [prefix + f"_WA:{c}" for c in range(8)]
    WBK = [prefix + f"_WB:{c}" for c in range(16)]
    for qtr in range(4):
        create = False
        ceps2, kceps2 = b.const_col(sc, 1e-5)
        for i in range(8):
            tg = qtr * 8 + i
            kx = f"{prefix}_XR:{i}"
            klq = prefix + "_lnq"
            dma_in(sc, b, XR[:, i * D:(i + 1) * D],
                   xin.ap()[tg * 128:(tg + 1) * 128, :], kx,
                   keys_r=[kx], tag=f"{prefix}_ldXR q{qtr} i{i}")
            sc.op("V", lambda v, i=i: v.bn_stats(lnq[:, 0:6], XR[:, i * D:i * D + 512]),
                  reads=[kx], writes=[klq + "a"], tag=f"{prefix}_qbn0 q{qtr} i{i}")
            sc.op("V", lambda v, i=i: v.bn_stats(lnq[:, 6:12],
                                                 XR[:, i * D + 512:i * D + 1024]),
                  reads=[kx], writes=[klq + "b"], tag=f"{prefix}_qbn1 q{qtr} i{i}")
            sc.op("V", lambda v: v.bn_aggr(lnq[:, 12:14], lnq[:, 0:12]),
                  reads=[klq + "a", klq + "b"], writes=[klq + "mv"],
                  tag=f"{prefix}_qagg q{qtr} i{i}")
            sc.op("S", lambda s: s.activation(lnq[:, 14:15], lnq[:, 13:14],
                                              AF.Sqrt, bias=ceps2[:]),
                  reads=[klq + "mv", kceps2], writes=[klq + "std"],
                  tag=f"{prefix}_qsqrt q{qtr} i{i}")
            sc.op("V", lambda v: v.reciprocal(lnq[:, 15:16], lnq[:, 14:15]),
                  reads=[klq + "std"], writes=[klq + "rs"],
                  tag=f"{prefix}_qrec q{qtr} i{i}")
            sc.op("V", lambda v: v.tensor_scalar(lnq[:, 15:16], lnq[:, 15:16],
                                                 0.5, None, ALU.mult),
                  reads=[klq + "rs"], writes=[klq + "rh"],
                  tag=f"{prefix}_qhalf q{qtr} i{i}")
            sc.op("V", lambda v, i=i: v.tensor_scalar(
                stage0[:, 0:1024], XR[:, i * D:(i + 1) * D], lnq[:, 12:13],
                lnq[:, 15:16], ALU.subtract, ALU.mult),
                reads=[kx, klq + "mv", klq + "rh", "SCR:0", "SCR:1"],
                writes=["SCR:0", "SCR:1"], tag=f"{prefix}_qln q{qtr} i{i}")
            sc.op("V", lambda v, i=i: v.tensor_tensor(
                XR[:, i * D:(i + 1) * D], XR[:, i * D:(i + 1) * D],
                stage0[:, 0:1024], ALU.add),
                reads=[kx, "SCR:0", "SCR:1"], writes=[kx],
                tag=f"{prefix}_qadd q{qtr} i{i}")
        if qtr == 0:
            guard(G1)
            guard(G2)
            load_w12(1, 0, create)
            load_w12(2, 0, create)
        for e in range(8):
            for tc in range(2):
                for fc in range(16):
                    kq = f"P{fc % 2}"
                    def h1mm(eng, qtr=qtr, tc=tc, fc=fc):
                        pp = pq[fc % 2]
                        for c in range(CT):
                            ins = eng.matmul(
                                pp[:], WA[:, c * 2048 + fc * 128: c * 2048 + fc * 128 + 128],
                                xnf[:, c * T + qtr * 1024 + tc * 512:
                                    c * T + qtr * 1024 + (tc + 1) * 512],
                                start=(c == 0), stop=(c == CT - 1))
                        return ins
                    sc.op("T", h1mm,
                          reads=WAK + xnf_keys(xn_name, ts=range(qtr * 8 + tc * 4,
                                                                 qtr * 8 + tc * 4 + 4)),
                          writes=[kq], tag=f"{prefix}_h1 q{qtr} e{e} tc{tc} f{fc}")
                    sc.op("S", lambda s, tc=tc, fc=fc, e=e: s.activation(
                        aof[:, tc * 8192 + fc * 512: tc * 8192 + (fc + 1) * 512],
                        pq[fc % 2][:], AF.Gelu, bias=b1d[:, e * 16 + fc: e * 16 + fc + 1]),
                        reads=[kq, prefix + "_b1d"],
                        writes=[f"{prefix}_h1g:{tc}:{fc}"],
                        tag=f"{prefix}_h1g q{qtr} e{e} tc{tc} f{fc}")
            if e + 1 < 8:
                load_w12(1, e + 1, create)
            elif qtr + 1 < 4:
                load_w12(1, 0, convert_moe and False)
            for tc in range(2):
                H1K = [f"{prefix}_h1g:{tc}:{fc}" for fc in range(16)]
                for ts in range(4):
                    i = tc * 4 + ts
                    tg = qtr * 8 + i
                    for ch in range(2):
                        kq = f"P{2 + (ts * 2 + ch) % 2}"
                        def h2mm(eng, tc=tc, ts=ts, ch=ch):
                            pp = pA[(ts * 2 + ch) % 2]
                            for fc in range(16):
                                ins = eng.matmul(
                                    pp[:], aof[:, tc * 8192 + fc * 512 + ts * 128:
                                               tc * 8192 + fc * 512 + ts * 128 + 128],
                                    WB[:, fc * 1024 + ch * 512: fc * 1024 + (ch + 1) * 512],
                                    start=(fc == 0), stop=(fc == 15))
                            return ins
                        sc.op("T", h2mm, reads=H1K + WBK, writes=[kq],
                              tag=f"{prefix}_h2 q{qtr} e{e} i{i} c{ch}")
                        sc.op("V", lambda v, i=i, ch=ch, tg=tg, e=e, pi=(ts * 2 + ch) % 2:
                              v.scalar_tensor_tensor(
                                  XR[:, i * D + ch * 512: i * D + (ch + 1) * 512],
                                  pA[pi][:], carena[:, tg * 8 + e: tg * 8 + e + 1],
                                  XR[:, i * D + ch * 512: i * D + (ch + 1) * 512],
                                  ALU.mult, ALU.add),
                              reads=[kq, f"{prefix}_car:{tg}", f"{prefix}_XR:{i}"],
                              writes=[f"{prefix}_XR:{i}"],
                              tag=f"{prefix}_h2a q{qtr} e{e} i{i} c{ch}")
            if e + 1 < 8:
                load_w12(2, e + 1, create)
            elif qtr + 1 < 4:
                load_w12(2, 0, False)
        for i in range(8):
            tg = qtr * 8 + i
            sc.op("Q", lambda q, tg=tg, i=i: q.dma_start(
                xout.ap()[tg * 128:(tg + 1) * 128, :], XR[:, i * D:(i + 1) * D]),
                reads=[f"{prefix}_XR:{i}"], writes=[f"xo:{tg}", f"{prefix}_XR:{i}"],
                dma=True, tag=f"{prefix}_stXR q{qtr} i{i}")


def emit_l1_ffn(sc0, b, W, xin, xout, prefix):
    nc = b.nc
    NQ, NKV = 1024, 2048
    LOGKQ = float(np.log(np.float32(1152.0)))
    LOGKKV = float(np.log(np.float32(2304.0)))
    NIT = 50
    SCL8 = 0.125

    xnlf = b.sbuf(prefix + "_xa", [128, CT * T], BF16)
    xhf = xnlf
    FB = b.sbuf(prefix + "_FB", [128, 2048], F32)
    GB = b.sbuf(prefix + "_GB", [128, 3072], BF16)
    nmr = b.sbuf(prefix + "_nmr", [128, 96], F32)
    ut = b.sbuf(prefix + "_ut", [128, D], F32)
    xnlt = b.sbuf(prefix + "_xnlt", [128, D], BF16)
    xht = b.sbuf(prefix + "_xht", [128, D], BF16)
    rts = b.sbuf(prefix + "_rts", [128, 320], F32)
    pst = b.sbuf(prefix + "_pstb", [128, 16], F32)
    bis = b.sbuf(prefix + "_bis", [1, 96], F32)
    Ebuf = b.sbuf(prefix + "_Ebuf", [128, 32], BF16)
    LT = b.sbuf(prefix + "_LT", [128, 128], BF16)
    wqkvL = [b.sbuf(prefix + f"_wkv{i}", [128, 3 * 1024], BF16) for i in range(2)]
    Q2 = b.sbuf(prefix + "_Q2", [128, T], BF16)
    K2 = b.sbuf(prefix + "_K2", [128, T], BF16)
    v2l = b.sbuf(prefix + "_v2l", [128, 64 * 130], BF16)
    expT = b.sbuf(prefix + "_expT", [128, 2 * 192], BF16)
    exph = b.sbuf(prefix + "_exph", [128, 2 * 512], BF16)
    expn = b.sbuf(prefix + "_expn", [1, 2 * 512], BF16)
    aotL = b.sbuf(prefix + "_aotL", [128, T], BF16)
    aofst = [b.sbuf(prefix + f"_aofst{i}", [128, 128], BF16) for i in range(2)]
    aofld = [b.sbuf(prefix + f"_aofld{i}", [128, 4 * 128], BF16) for i in range(2)]
    wout = b.sbuf(prefix + "_wout", [128, 4 * D], BF16)
    nkcol = b.sbuf(prefix + "_nk", [128, 4], BF16)
    nkf = b.sbuf(prefix + "_nkf", [128, 4], F32)
    nvrow = b.sbuf(prefix + "_nv", [1, 520], BF16)
    nvf = b.sbuf(prefix + "_nvf", [1, 520], F32)


    pq = [b["P0"], b["P1"]]
    pq2 = [b["P0"], b["P1"], b["P2"], b["P3"]]
    pav = [b["P4"], b["P5"]]
    pA = [b["P2"], b["P3"], b["P4"], b["P5"]]

    CQ_S, CK_S, CQ_B, CK_B, CQ_SC, CK_SC, TMP, MQ, QG, NG = (
        0, 32, 64, 96, 128, 160, 192, 224, 256, 288)

    sc = sc0

    def guard(gkey):
        sc.op("V", lambda v: v.engine_nop(), reads=[], writes=[gkey],
              tag="guard_" + gkey)

    rowstage = b["SCR"][0:1, 0:1024]
    dma_in(sc, b, nkf[:], W["nullk_dev"].ap(), prefix + "_nkf", tag="ld_nk")
    dma_in(sc, b, nvf[:], W["nullv_row"].ap(), prefix + "_nvf", tag="ld_nv")
    sc.op("V", lambda v: v.tensor_copy(nkcol[:], nkf[:]),
          reads=[prefix + "_nkf"], writes=[prefix + "_nk"], tag="cv_nk")
    sc.op("V", lambda v: v.tensor_copy(nvrow[:], nvf[:]),
          reads=[prefix + "_nvf"], writes=[prefix + "_nv"], tag="cv_nv")
    def bcast_row(rn, dst, dstslot_lo, dkey, out_bf):
        dma_in(sc, b, rowstage, W[rn].ap(), "SCR:0", keys_r=["SCR:0"],
               tag=f"ld_{rn}")
        for half in range(2):
            kps = f"P{half}"
            sc.op("T", lambda t, half=half: t.matmul(
                pq[half][:], b["ones_row"][:],
                rowstage[:, half * 512:(half + 1) * 512],
                start=True, stop=True),
                reads=["ones_row", "SCR:0"], writes=[kps],
                tag=f"{prefix}_bc_{rn}{half}")
            sc.op("V", lambda v, half=half: v.tensor_copy(
                dst[:, dstslot_lo + half * 512: dstslot_lo + (half + 1) * 512],
                pq[half][:]),
                reads=[kps, "SCR:0"], writes=[dkey], tag=f"{prefix}_bcv_{rn}{half}")

    bcast_row("rt_q", FB, 0, f"{prefix}_FB:0", False)
    bcast_row("rt_kv", FB, 1024, f"{prefix}_FB:1", False)
    bcast_row("ln_g", GB, 0, f"{prefix}_GB:0", True)
    bcast_row("ln_b", GB, 1024, f"{prefix}_GB:1", True)
    bcast_row("rms_g", GB, 2048, f"{prefix}_GB:2", True)
    sc.op("P", lambda g: g.iota(b["SCR"][:, 0:128], pattern=[[1, 128]], base=0,
                                channel_multiplier=-1,
                                allow_small_or_imprecise_dtypes=True),
          reads=["SCR:0"], writes=["SCR:0"], tag=f"{prefix}_ltiota")
    sc.op("P", lambda g: g.tensor_scalar(LT[:], b["SCR"][:, 0:128], 0.0, None,
                                         ALU.is_gt),
          reads=["SCR:0"], writes=[prefix + "_LT"], tag=f"{prefix}_ltge")

    ceps, kceps = b.const_col(sc, 1e-5)

    for t in range(TT):
        par = t % 2
        kx = f"SCR:{par}"
        xtile = b["SCR"][:, par * 1024:(par + 1) * 1024]
        st8 = pst[:]
        kst = prefix + "_pst"
        dma_in(sc, b, xtile, xin.ap()[t * 128:(t + 1) * 128, :], kx,
               keys_r=[kx], tag=f"{prefix}_ldx t{t}")
        sc.op("V", lambda v, xtile=xtile: v.bn_stats(st8[:, 0:6], xtile[:, 0:512]),
              reads=[kx], writes=[kst + "a"], tag=f"{prefix}_bn0 t{t}")
        sc.op("V", lambda v, xtile=xtile: v.bn_stats(st8[:, 6:12], xtile[:, 512:1024]),
              reads=[kx], writes=[kst + "b"], tag=f"{prefix}_bn1 t{t}")
        sc.op("V", lambda v: v.bn_aggr(st8[:, 12:14], st8[:, 0:12]),
              reads=[kst + "a", kst + "b"], writes=[kst + "mv"], tag=f"{prefix}_agg t{t}")
        sc.op("S", lambda s: s.activation(st8[:, 14:15], st8[:, 13:14], AF.Sqrt,
                                          bias=ceps[:]),
              reads=[kst + "mv", kceps], writes=[kst + "sd"], tag=f"{prefix}_sq t{t}")
        sc.op("V", lambda v: v.reciprocal(st8[:, 15:16], st8[:, 14:15]),
              reads=[kst + "sd"], writes=[kst + "rs"], tag=f"{prefix}_rc t{t}")
        sc.op("V", lambda v, xtile=xtile: v.tensor_scalar(
            ut[:], xtile, st8[:, 12:13], st8[:, 15:16], ALU.subtract, ALU.mult),
            reads=[kx, kst + "mv", kst + "rs"], writes=[prefix + "_ut"],
            tag=f"{prefix}_u t{t}")
        sc.op("V", lambda v, t=t: v.scalar_tensor_tensor(
            xht[:], ut[:], 1.0, FB[:, 0:1024], ALU.mult, ALU.mult,
            accum_out=rts[:, CQ_S + t:CQ_S + t + 1]),
            reads=[prefix + "_ut", f"{prefix}_FB:0", prefix + "_xht"],
            writes=[f"{prefix}_sq:{t}", prefix + "_xht"], tag=f"{prefix}_sdq t{t}")
        sc.op("V", lambda v, t=t: v.scalar_tensor_tensor(
            xht[:], ut[:], 1.0, FB[:, 1024:2048], ALU.mult, ALU.mult,
            accum_out=rts[:, CK_S + t:CK_S + t + 1]),
            reads=[prefix + "_ut", f"{prefix}_FB:1", prefix + "_xht"],
            writes=[f"{prefix}_skv:{t}", prefix + "_xht"], tag=f"{prefix}_sdk t{t}")
        sc.op("V", lambda v, t=t: v.tensor_copy(nmr[:, 32 + t:33 + t],
                                                st8[:, 12:13]),
              reads=[kst + "mv"], writes=[prefix + "_nmr"], tag=f"{prefix}_svm t{t}")
        sc.op("V", lambda v, t=t: v.tensor_copy(nmr[:, 64 + t:65 + t],
                                                st8[:, 15:16]),
              reads=[kst + "rs"], writes=[prefix + "_nmr"], tag=f"{prefix}_svr t{t}")
        sc.op("V", lambda v: v.bn_stats(st8[:, 0:6], ut[:, 0:512]),
              reads=[prefix + "_ut", kst + "a"], writes=[kst + "a"],
              tag=f"{prefix}_b20 t{t}")
        sc.op("V", lambda v: v.bn_stats(st8[:, 6:12], ut[:, 512:1024]),
              reads=[prefix + "_ut", kst + "b"], writes=[kst + "b"],
              tag=f"{prefix}_b21 t{t}")
        sc.op("V", lambda v: v.bn_aggr(st8[:, 12:14], st8[:, 0:12]),
              reads=[kst + "a", kst + "b", kst + "mv"], writes=[kst + "mv2"],
              tag=f"{prefix}_ag2 t{t}")
        sc.op("V", lambda v: v.tensor_tensor(st8[:, 2:3], st8[:, 12:13],
                                             st8[:, 12:13], ALU.mult),
              reads=[kst + "mv2"], writes=[kst + "m2s"], tag=f"{prefix}_m2 t{t}")
        sc.op("V", lambda v: v.tensor_tensor(st8[:, 3:4], st8[:, 13:14],
                                             st8[:, 2:3], ALU.add),
              reads=[kst + "mv2", kst + "m2s"], writes=[kst + "ss"],
              tag=f"{prefix}_ss t{t}")
        sc.op("S", lambda s: s.activation(st8[:, 4:5], st8[:, 3:4], AF.Sqrt,
                                          scale=1024.0),
              reads=[kst + "ss"], writes=[kst + "nr"], tag=f"{prefix}_sq2 t{t}")
        sc.op("V", lambda v: v.tensor_scalar(st8[:, 4:5], st8[:, 4:5], 1e-12,
                                             None, ALU.max),
              reads=[kst + "nr"], writes=[kst + "nr2"], tag=f"{prefix}_mx t{t}")
        sc.op("V", lambda v: v.reciprocal(st8[:, 5:6], st8[:, 4:5]),
              reads=[kst + "nr2"], writes=[kst + "nr3"], tag=f"{prefix}_rc2 t{t}")
        sc.op("V", lambda v: v.tensor_scalar(st8[:, 5:6], st8[:, 5:6], 32.0,
                                             None, ALU.mult),
              reads=[kst + "nr3"], writes=[kst + "nr4"], tag=f"{prefix}_m32 t{t}")
        sc.op("V", lambda v, t=t: v.tensor_copy(nmr[:, t:t + 1], st8[:, 5:6]),
              reads=[kst + "nr4"], writes=[prefix + "_nmr"], tag=f"{prefix}_svn t{t}")
        sc.op("S", lambda s: s.activation(st8[:, 14:15], st8[:, 13:14], AF.Sqrt,
                                          bias=ceps[:]),
              reads=[kst + "mv2", kceps, kst + "sd"], writes=[kst + "sd2"],
              tag=f"{prefix}_sq3 t{t}")
        sc.op("V", lambda v: v.reciprocal(st8[:, 15:16], st8[:, 14:15]),
              reads=[kst + "sd2", kst + "rs"], writes=[kst + "rs2"],
              tag=f"{prefix}_rc3 t{t}")
        sc.op("V", lambda v: v.tensor_scalar(
            ut[:], ut[:], st8[:, 12:13], st8[:, 15:16], ALU.subtract, ALU.mult),
            reads=[prefix + "_ut", kst + "mv2", kst + "rs2"],
            writes=[prefix + "_ut"], tag=f"{prefix}_nl1 t{t}")
        sc.op("V", lambda v: v.tensor_tensor(ut[:], ut[:], GB[:, 0:1024], ALU.mult),
              reads=[prefix + "_ut", f"{prefix}_GB:0"], writes=[prefix + "_ut"],
              tag=f"{prefix}_nl2 t{t}")
        sc.op("V", lambda v: v.tensor_tensor(xnlt[:], ut[:], GB[:, 1024:2048],
                                             ALU.add),
              reads=[prefix + "_ut", f"{prefix}_GB:1"], writes=[prefix + "_xnlt"],
              tag=f"{prefix}_nl3 t{t}")
        for c in range(CT):
            pkk = f"PT{(t * CT + c) % 2}"
            sc.op("T", lambda tn, c=c, pkk=pkk: tn.transpose(
                b[pkk][:], xnlt[:, c * 128:(c + 1) * 128], b["ident_bf"][:]),
                reads=[prefix + "_xnlt", "ident_bf"],
                writes=[pkk], tag=f"{prefix}_tpxnl t{t} c{c}")
            sc.op("S", lambda s, t=t, c=c, pkk=pkk: s.copy(
                xnlf[:, c * T + t * 128: c * T + (t + 1) * 128], b[pkk][:]),
                reads=[pkk], writes=[f"{prefix}_xa:{c}:{t}"],
                tag=f"{prefix}_evxnl t{t} c{c}")

    def R(lo, hi):
        ks = [f"{prefix}_rb{i}" for i in range(lo // 32, (hi + 31) // 32)]
        return rts[:, lo:hi], ks

    def R1(lo, hi):
        ap, ks = R(lo, hi)
        return rts[0:1, lo:hi], ks

    def Bc(lo, hi):
        return bis[:, lo:hi], [f"{prefix}_bis{i}" for i in range(lo // 8, (hi + 7) // 8)]

    onescol_f = b.sbuf("onescol_f", [128, 1], F32)
    b.t["onescol_f"] = onescol_f
    sc.op("V", lambda v: v.memset(onescol_f[:], 1.0), writes=["onescol_f"],
          tag="onescol_f")

    def vop(fn, rd, wr, tag):
        sc.op("V", fn, reads=rd, writes=wr, tag=tag)

    def sop(fn, rd, wr, tag):
        sc.op("S", fn, reads=rd, writes=wr, tag=tag)

    def top_op(fn, rd, wr, tag):
        sc.op("T", fn, reads=rd, writes=wr, tag=tag)

    SQ_, SQK = R(0, 32)
    SK_, SKK = R(32, 64)

    def routing(which, s_ap, SRD, sco_lo, logk):
        Bp, BK = R(64, 96)
        Ep, EK = R(96, 128)
        rs1, rsK = R(224, 225)
        ab, abK = R(225, 226)
        sco, scoK = R(sco_lo, sco_lo + 32)
        lse, lseK = Bc(0, 1)
        av, avK = Bc(1, 2)
        vop(lambda v: v.tensor_scalar(Bp, s_ap, -1.0, None, ALU.mult),
            SRD, BK, f"{prefix}_r{which}_init")
        for it in range(NIT):
            vop(lambda v: v.tensor_tensor(Ep, s_ap, Bp, ALU.add),
                SRD + BK, EK, f"{prefix}_r{which}_sb{it}")
            sop(lambda s: s.activation(Ep, Ep, AF.Exp), EK, EK,
                f"{prefix}_r{which}_e{it}")
            vop(lambda v: v.reduce_sum(rs1, Ep, axis=AX.X), EK, rsK,
                f"{prefix}_r{which}_rs{it}")
            top_op(lambda t: t.matmul(b["P5"][0:1, 0:1], rs1, onescol_f[:],
                                      start=True, stop=True),
                   rsK + ["onescol_f"], ["P5"], f"{prefix}_r{which}_tt{it}")
            sop(lambda s: s.activation(lse, b["P5"][0:1, 0:1], AF.Ln),
                ["P5"], lseK, f"{prefix}_r{which}_ln{it}")
            vop(lambda v, logk=logk: v.tensor_scalar(av, lse, -1.0, logk,
                                                     ALU.mult, ALU.add),
                lseK, avK, f"{prefix}_r{which}_a{it}")
            top_op(lambda t: t.matmul(b["P5"][:, 1:2], b["ones_row"][:], av,
                                      start=True, stop=True),
                   avK + ["ones_row"], ["P5"], f"{prefix}_r{which}_ab{it}")
            vop(lambda v: v.tensor_copy(ab, b["P5"][:, 1:2]), ["P5"], abK,
                f"{prefix}_r{which}_abe{it}")
            vop(lambda v: v.tensor_scalar(Ep, s_ap, ab, None, ALU.add),
                SRD + abK, EK, f"{prefix}_r{which}_sa{it}")
            vop(lambda v: v.tensor_scalar(Bp, Ep, 0.0, -1.0, ALU.max, ALU.mult),
                EK, BK, f"{prefix}_r{which}_b{it}")
        vop(lambda v: v.tensor_tensor(sco, Ep, Bp, ALU.add), EK + BK, scoK,
            f"{prefix}_r{which}_f1")
        sop(lambda s: s.activation(sco, sco, AF.Exp), scoK, scoK,
            f"{prefix}_r{which}_f2")
        return sco, scoK

    SRQ = [f"{prefix}_sq:{t}" for t in range(TT)] + SQK
    SRK = [f"{prefix}_skv:{t}" for t in range(TT)] + SKK
    scoQ, scoQK = routing("q", SQ_, SRQ, 128, LOGKQ)
    scoKV, scoKVK = routing("kv", SK_, SRK, 160, LOGKKV)

    def topk_mask(which, sco, scoK, k, m_lo):
        lo, loK = Bc(2, 3)
        hi, hiK = Bc(3, 4)
        mid, midK = Bc(4, 5)
        cnt, cntK = Bc(5, 6)
        pred, predK = Bc(6, 7)
        d1, d1K = Bc(7, 8)
        d2, d2K = Bc(8, 9)
        npred, npK = Bc(9, 10)
        rr, rrK = Bc(10, 11)
        cs_i, csiK = Bc(16, 48)
        cs_e, cseK = Bc(48, 80)
        midb, midbK = R(226, 227)
        lob, lobK = R(227, 228)
        hib, hibK = R(228, 229)
        rbc, rbcK = R(229, 230)
        rs1, rsK = R(224, 225)
        cmp, cmpK = R(96, 128)
        ghi, ghiK = R(192, 224)
        E, EK2 = R(64, 96)
        pfx, pfxK = R(0, 32)
        msk, mskK = R(m_lo, m_lo + 32)
        vop(lambda v: v.memset(lo, 0.0), [], loK, f"{prefix}_t{which}_l0")
        vop(lambda v: v.memset(hi, 1.5), [], hiK, f"{prefix}_t{which}_h0")
        for it in range(30):
            vop(lambda v: v.tensor_tensor(mid, lo, hi, ALU.add), loK + hiK,
                midK, f"{prefix}_t{which}_m{it}")
            vop(lambda v: v.tensor_scalar(mid, mid, 0.5, None, ALU.mult),
                midK, midK, f"{prefix}_t{which}_m2{it}")
            top_op(lambda t: t.matmul(b["P5"][:, 2:3], b["ones_row"][:], mid,
                                      start=True, stop=True),
                   midK + ["ones_row"], ["P5"], f"{prefix}_t{which}_mb{it}")
            vop(lambda v: v.tensor_copy(midb, b["P5"][:, 2:3]), ["P5"], midbK,
                f"{prefix}_t{which}_me{it}")
            vop(lambda v: v.tensor_scalar(cmp, sco, midb, None, ALU.is_gt),
                scoK + midbK, cmpK, f"{prefix}_t{which}_c{it}")
            vop(lambda v: v.reduce_sum(rs1, cmp, axis=AX.X), cmpK, rsK,
                f"{prefix}_t{which}_r{it}")
            top_op(lambda t: t.matmul(b["P5"][0:1, 3:4], rs1, onescol_f[:],
                                      start=True, stop=True),
                   rsK + ["onescol_f"], ["P5"], f"{prefix}_t{which}_t{it}")
            vop(lambda v: v.tensor_copy(cnt, b["P5"][0:1, 3:4]), ["P5"], cntK,
                f"{prefix}_t{which}_te{it}")
            vop(lambda v, k=k: v.tensor_scalar(pred, cnt, float(k), None,
                                               ALU.is_ge), cntK, predK,
                f"{prefix}_t{which}_p{it}")
            vop(lambda v: v.tensor_tensor(d1, mid, lo, ALU.subtract),
                midK + loK, d1K, f"{prefix}_t{which}_d1{it}")
            vop(lambda v: v.tensor_tensor(d1, d1, pred, ALU.mult),
                d1K + predK, d1K, f"{prefix}_t{which}_d1m{it}")
            vop(lambda v: v.tensor_tensor(lo, lo, d1, ALU.add), d1K + loK,
                loK, f"{prefix}_t{which}_lo{it}")
            vop(lambda v: v.tensor_tensor(d2, mid, hi, ALU.subtract),
                midK + hiK, d2K, f"{prefix}_t{which}_d2{it}")
            vop(lambda v: v.tensor_scalar(npred, pred, -1.0, 1.0, ALU.mult,
                                          ALU.add), predK, npK,
                f"{prefix}_t{which}_np{it}")
            vop(lambda v: v.tensor_tensor(d2, d2, npred, ALU.mult),
                d2K + npK, d2K, f"{prefix}_t{which}_d2m{it}")
            vop(lambda v: v.tensor_tensor(hi, hi, d2, ALU.add), d2K + hiK,
                hiK, f"{prefix}_t{which}_hi{it}")
        top_op(lambda t: t.matmul(b["P5"][:, 4:5], b["ones_row"][:], lo,
                                  start=True, stop=True),
               loK + ["ones_row"], ["P5"], f"{prefix}_t{which}_lob")
        vop(lambda v: v.tensor_copy(lob, b["P5"][:, 4:5]), ["P5"], lobK,
            f"{prefix}_t{which}_lobe")
        top_op(lambda t: t.matmul(b["P5"][:, 5:6], b["ones_row"][:], hi,
                                  start=True, stop=True),
               hiK + ["ones_row"], ["P5"], f"{prefix}_t{which}_hib")
        vop(lambda v: v.tensor_copy(hib, b["P5"][:, 5:6]), ["P5"], hibK,
            f"{prefix}_t{which}_hibe")
        vop(lambda v: v.tensor_scalar(ghi, sco, hib, None, ALU.is_gt),
            scoK + hibK, ghiK, f"{prefix}_t{which}_g")
        vop(lambda v: v.reduce_sum(rs1, ghi, axis=AX.X), ghiK, rsK,
            f"{prefix}_t{which}_cg")
        top_op(lambda t: t.matmul(b["P5"][0:1, 3:4], rs1, onescol_f[:],
                                  start=True, stop=True),
               rsK + ["onescol_f"], ["P5"], f"{prefix}_t{which}_cg2")
        vop(lambda v, k=k: v.tensor_scalar(rr, b["P5"][0:1, 3:4], -1.0,
                                           float(k), ALU.mult, ALU.add),
            ["P5"], rrK, f"{prefix}_t{which}_rr")
        top_op(lambda t: t.matmul(b["P5"][:, 6:7], b["ones_row"][:], rr,
                                  start=True, stop=True),
               rrK + ["ones_row"], ["P5"], f"{prefix}_t{which}_rrb")
        vop(lambda v: v.tensor_copy(rbc, b["P5"][:, 6:7]), ["P5"], rbcK,
            f"{prefix}_t{which}_rrbe")
        vop(lambda v: v.tensor_scalar(E, sco, lob, None, ALU.is_gt),
            scoK + lobK, EK2, f"{prefix}_t{which}_gl")
        vop(lambda v: v.tensor_tensor(E, E, ghi, ALU.subtract), EK2 + ghiK,
            EK2, f"{prefix}_t{which}_E")
        Ebf = Ebuf[:]
        vop(lambda v: v.tensor_copy(Ebf, E), EK2, [f"{prefix}_Ebf"],
            f"{prefix}_t{which}_Eb")
        top_op(lambda t: t.matmul(b["P4"][:, 0:32], LT[:], Ebf, start=True,
                                  stop=True),
               [prefix + "_LT", f"{prefix}_Ebf"], ["P4"], f"{prefix}_t{which}_px")
        top_op(lambda t: t.matmul(b["P5"][0:1, 8:40], b["ones_col_bf"][:], Ebf,
                                  start=True, stop=True),
               ["ones_col_bf", f"{prefix}_Ebf"], ["P5"], f"{prefix}_t{which}_cs")
        cs_i_ap = bis[0:1, 16:48]
        cs_e_ap = bis[0:1, 48:80]
        vop(lambda v: v.tensor_copy(cs_i_ap, b["P5"][0:1, 8:40]), ["P5"],
            csiK, f"{prefix}_t{which}_cse")
        vop(lambda v: v.tensor_tensor_scan(cs_e_ap, cs_i_ap, cs_i_ap, 0.0,
                                           ALU.add, ALU.bypass), csiK, cseK,
            f"{prefix}_t{which}_sc")
        vop(lambda v: v.tensor_tensor(cs_e_ap, cs_e_ap, cs_i_ap, ALU.subtract),
            cseK + csiK, cseK, f"{prefix}_t{which}_ex")
        top_op(lambda t: t.matmul(b["P5"][:, 8:40], b["ones_row"][:], cs_e_ap,
                                  start=True, stop=True),
               cseK + ["ones_row"], ["P5"], f"{prefix}_t{which}_cb")
        wcol, wcolK = R(32, 64)
        vop(lambda v: v.tensor_copy(wcol, b["P4"][:, 0:32]), ["P4"], wcolK,
            f"{prefix}_t{which}_pxe")
        vop(lambda v: v.tensor_tensor(pfx, b["P5"][:, 8:40], wcol,
                                      ALU.add), ["P5"] + wcolK, pfxK,
            f"{prefix}_t{which}_tot")
        vop(lambda v: v.tensor_scalar(pfx, pfx, rbc, None, ALU.is_lt),
            pfxK + rbcK, pfxK, f"{prefix}_t{which}_plt")
        vop(lambda v: v.tensor_tensor(pfx, pfx, E, ALU.mult), pfxK + EK2,
            pfxK, f"{prefix}_t{which}_sel")
        vop(lambda v: v.tensor_tensor(msk, pfx, ghi, ALU.add), pfxK + ghiK,
            mskK, f"{prefix}_t{which}_m")
        return msk, mskK

    maskQ, maskQK = topk_mask("q", scoQ, scoQK, NQ, 232)
    maskKV, maskKVK = topk_mask("kv", scoKV, scoKVK, NKV, 264)
    qg, qgK = R(32, 64)
    ng, ngK = R(192, 224)
    kvg, kvgK = R(96, 128)
    tmp3, t3K = R(64, 96)
    vop(lambda v: v.tensor_scalar(tmp3, scoQ, -1.0, 1.0, ALU.mult, ALU.add),
        scoQK, t3K, f"{prefix}_g1")
    vop(lambda v: v.tensor_tensor(tmp3, tmp3, scoQ, ALU.add), t3K + scoQK,
        t3K, f"{prefix}_g2")
    vop(lambda v: v.tensor_tensor(qg, tmp3, maskQ, ALU.mult), t3K + maskQK,
        qgK, f"{prefix}_g3")
    vop(lambda v: v.tensor_scalar(qg, qg, 0.5, None, ALU.mult), qgK, qgK,
        f"{prefix}_g4")
    vop(lambda v: v.tensor_scalar(ng, maskQ, -0.5, 0.5, ALU.mult, ALU.add),
        maskQK, ngK, f"{prefix}_g5")
    vop(lambda v: v.tensor_scalar(tmp3, scoKV, -1.0, 1.0, ALU.mult, ALU.add),
        scoKVK, t3K, f"{prefix}_g6")
    vop(lambda v: v.tensor_tensor(tmp3, tmp3, scoKV, ALU.add), t3K + scoKVK,
        t3K, f"{prefix}_g7")
    vop(lambda v: v.tensor_tensor(kvg, tmp3, maskKV, ALU.mult), t3K + maskKVK,
        kvgK, f"{prefix}_g8")

    XNLK = [f"{prefix}_xa:{c}:{t}" for c in range(CT) for t in range(TT)]
    XHK = XNLK
    recL = b.sbuf(prefix + "_recL", [128, 4], F32)

    def load_pair_w(names, hbm, col_off, p, par):
        for wi, (wn, off) in enumerate(zip(names, col_off)):
            ks = f"SCR:{par}"
            srcap = hbm[wn].ap()[:, off + p * 128: off + (p + 1) * 128].rearrange(
                "(c p) j -> p c j", p=128)
            sc.op("Q", lambda q, srcap=srcap, par=par: q.dma_start(
                b["SCR"][:, par * 1024:(par + 1) * 1024].rearrange(
                    "p (c j) -> p c j", j=128), srcap),
                reads=[ks], writes=[ks], dma=True, tag=f"{prefix}_lw p{p} {wn}")
            sc.op("V", lambda v, par=par, wi=wi: v.tensor_copy(
                wqkvL[par][:, wi * 1024:(wi + 1) * 1024],
                b["SCR"][:, par * 1024:(par + 1) * 1024]),
                reads=[ks], writes=[f"{prefix}_wkv{par}:{wi}"],
                tag=f"{prefix}_cw p{p} {wn}")

    load_convert_bf16(sc, b, prefix + "_wo", W["wout_l"].ap(), 512, D,
                      stage=b["SCR"][:, 0:1024], stage_key="SCR:0", dst=wout)
    lighth = b.dram(prefix + "_lighth", [T, D], F32)
    aofh = b.dram(prefix + "_aofh", [CT * TT, 128 * 128], BF16)

    load_pair_w(("wqkv_l",) * 3, W, (0, 512, 1024), 0, 0)
    for p in range(4):
        par = p % 2
        if p + 1 < 4:
            load_pair_w(("wqkv_l",) * 3, W, (0, 512, 1024), p + 1, (p + 1) % 2)
        WQk = [f"{prefix}_wkv{par}:0"]
        WKk = [f"{prefix}_wkv{par}:1"]
        WVk = [f"{prefix}_wkv{par}:2"]
        for wi, (dst, wk) in enumerate(((Q2, WQk), (K2, WKk))):
            for n in range(8):
                kq = f"P{n % 2}"
                def qmm(eng, par=par, n=n, wi=wi):
                    pp = pq[n % 2]
                    for c in range(CT):
                        ins = eng.matmul(
                            pp[:], wqkvL[par][:, wi * 1024 + c * 128: wi * 1024 + c * 128 + 128],
                            xnlf[:, c * T + n * 512: c * T + (n + 1) * 512],
                            start=(c == 0), stop=(c == CT - 1))
                    return ins
                sc.op("T", qmm, reads=wk + XNLK, writes=[kq],
                      tag=f"{prefix}_lqk p{p} w{wi} n{n}")
                sc.op("S", lambda s, n=n, dst=dst: s.copy(
                    dst[:, n * 512:(n + 1) * 512], pq[n % 2][:]),
                    reads=[kq], writes=[f"{prefix}_{'Q2' if wi == 0 else 'K2'}:{n}"],
                    tag=f"{prefix}_lqke p{p} w{wi} n{n}")
        sc.op("V", lambda v: v.memset(
            v2l[0:64, :].rearrange("p (t h w) -> p t h w", h=2, w=65)[:, :, :, 64:65],
            1.0),
            writes=[f"{prefix}_v2l:{w}" for w in range(64)],
            tag=f"{prefix}_lvone p{p}")
        for w in range(64):
            kk = f"P{2 + w % 2}"
            def vmm(eng, par=par, w=w):
                pp = pA[w % 2]
                for c in range(CT):
                    ins = eng.matmul(
                        pp[0:64, 0:128],
                        xnlf[:, c * T + w * 64: c * T + (w + 1) * 64],
                        wqkvL[par][:, 2 * 1024 + c * 128: 2 * 1024 + c * 128 + 128],
                        start=(c == 0), stop=(c == CT - 1))
                return ins
            sc.op("T", vmm, reads=WVk + XNLK, writes=[kk], tag=f"{prefix}_lv p{p} w{w}")
            for h in range(2):
                sc.op("S", lambda s, w=w, h=h, pi=w % 2: s.copy(
                    v2l[0:64, w * 130 + h * 65: w * 130 + h * 65 + 64],
                    pA[pi][0:64, h * 64:(h + 1) * 64]),
                    reads=[kk], writes=[f"{prefix}_v2l:{w}"],
                    tag=f"{prefix}_lve p{p} w{w} h{h}")
        for h in range(2):
            base = 64 * h
            Q2K = [f"{prefix}_Q2:{n}" for n in range(8)]
            K2K = [f"{prefix}_K2:{n}" for n in range(8)]
            for w in range(64):
                ds = [d for d in (-1, 0, 1) if 0 <= w + d < 64]
                slot0 = (w % 2) * 3
                for di, d in enumerate(ds):
                    kp = f"P{di}"
                    sc.op("T", lambda eng, base=base, w=w, d=d, di=di: eng.matmul(
                        pq[di][0:64, 0:64] if di < 2 else b["P2"][0:64, 0:64],
                        K2[base:base + 64, (w + d) * 64:(w + d + 1) * 64],
                        Q2[base:base + 64, w * 64:(w + 1) * 64],
                        start=True, stop=True),
                        reads=Q2K + K2K, writes=[kp],
                        tag=f"{prefix}_ls p{p} h{h} w{w} d{d}")
                    sc.op("S", lambda s, di=di, slot0=slot0: s.activation(
                        expT[0:64, (slot0 + di) * 64:(slot0 + di + 1) * 64],
                        (pq[di] if di < 2 else b["P2"])[0:64, 0:64],
                        AF.Exp, scale=SCL8),
                        reads=[kp], writes=[f"{prefix}_expT:{slot0 + di}"],
                        tag=f"{prefix}_lse p{p} h{h} w{w} d{d}")
                def lav(eng, w=w, h=h, ds=ds, slot0=slot0):
                    for di, d in enumerate(ds):
                        ins = eng.matmul(
                            b["P3"][0:64, 0:65],
                            expT[0:64, (slot0 + di) * 64:(slot0 + di + 1) * 64],
                            v2l[0:64, (w + d) * 130 + h * 65: (w + d) * 130 + h * 65 + 65],
                            start=(di == 0), stop=(di == len(ds) - 1))
                    return ins
                sc.op("T", lav,
                      reads=[f"{prefix}_expT:{slot0 + di}" for di in range(len(ds))] +
                            [f"{prefix}_v2l:{w + d}" for d in ds],
                      writes=["P3"], tag=f"{prefix}_lav p{p} h{h} w{w}")
                sc.op("V", lambda v, h=h: v.reciprocal(
                    recL[:, h:h + 1][0:64, :], b["P3"][0:64, 64:65]),
                    reads=["P3"], writes=[f"{prefix}_recL:{h}"],
                    tag=f"{prefix}_lrec p{p} h{h} w{w}")
                sc.op("V", lambda v, w=w, h=h: v.tensor_scalar(
                    aotL[(w % 2) * 64:(w % 2) * 64 + 64,
                         (w // 2) * 128 + h * 64: (w // 2) * 128 + h * 64 + 64],
                    b["P3"][0:64, 0:64], recL[:, h:h + 1][0:64, :], None, ALU.mult),
                    reads=["P3", f"{prefix}_recL:{h}"],
                    writes=[f"{prefix}_aotL:{w // 2}:{h}"],
                    tag=f"{prefix}_laev p{p} h{h} w{w}")
        for t in range(TT):
            pkk = f"PT{t % 2}"
            kst = f"{prefix}_aofst{t % 2}"
            sc.op("T", lambda eng, t=t, pkk=pkk: eng.transpose(
                b[pkk][:], aotL[:, t * 128:(t + 1) * 128], b["ident_bf"][:]),
                reads=[f"{prefix}_aotL:{t}:0", f"{prefix}_aotL:{t}:1", "ident_bf"],
                writes=[pkk], tag=f"{prefix}_latr p{p} t{t}")
            sc.op("S", lambda s, t=t, pkk=pkk: s.copy(aofst[t % 2][:], b[pkk][:]),
                  reads=[pkk], writes=[kst], tag=f"{prefix}_latre p{p} t{t}")
            sc.op("Q", lambda q, t=t, p=p: q.dma_start(
                aofh.ap()[p * TT + t:p * TT + t + 1, :].rearrange(
                    "a (p2 f) -> (a p2) f", p2=128), aofst[t % 2][:]),
                reads=[kst], writes=[f"{prefix}_aofh:{p}:{t}"], dma=True,
                tag=f"{prefix}_last p{p} t{t}")
    for t in range(TT):
        par = t % 2
        kld = f"{prefix}_aofld{par}"
        for c in range(4):
            sc.op("Q", lambda q, t=t, c=c, par=par: q.dma_start(
                aofld[par][:, c * 128:(c + 1) * 128],
                aofh.ap()[c * TT + t:c * TT + t + 1, :].rearrange(
                    "a (p2 f) -> (a p2) f", p2=128)),
                reads=[f"{prefix}_aofh:{c}:{t}", kld + f":{c}"],
                writes=[kld + f":{c}"], dma=True, tag=f"{prefix}_lpl t{t} c{c}")
        for half in range(2):
            kp = f"P{half}"
            def lpm(eng, par=par, half=half):
                pp = pq[half]
                for c in range(4):
                    ins = eng.matmul(
                        pp[:], aofld[par][:, c * 128:(c + 1) * 128],
                        wout[:, c * 1024 + half * 512: c * 1024 + half * 512 + 512],
                        start=(c == 0), stop=(c == 3))
                return ins
            sc.op("T", lpm,
                  reads=[kld + f":{c}" for c in range(4)] +
                        [f"{prefix}_wo:{c}" for c in range(4)],
                  writes=[kp], tag=f"{prefix}_lpm t{t} h{half}")
            sc.op("V", lambda v, half=half: v.tensor_copy(
                ut[:, half * 512:(half + 1) * 512], pq[half][:]),
                reads=[kp, prefix + "_ut"], writes=[prefix + "_ut"],
                tag=f"{prefix}_lpe t{t} h{half}")
        sc.op("Q", lambda q, t=t: q.dma_start(
            lighth.ap()[t * 128:(t + 1) * 128, :], ut[:]),
            reads=[prefix + "_ut"], writes=[f"{prefix}_lh:{t}", prefix + "_ut"],
            dma=True, tag=f"{prefix}_lps t{t}")

    for t in range(TT):
        par = t % 2
        kx = f"SCR:{par}"
        xtile = b["SCR"][:, par * 1024:(par + 1) * 1024]
        dma_in(sc, b, xtile, xin.ap()[t * 128:(t + 1) * 128, :], kx,
               keys_r=[kx], tag=f"{prefix}_ldx2 t{t}")
        sc.op("V", lambda v, t=t, xtile=xtile: v.tensor_scalar(
            ut[:], xtile, nmr[:, 32 + t:33 + t], nmr[:, 64 + t:65 + t],
            ALU.subtract, ALU.mult),
            reads=[kx, prefix + "_nmr", prefix + "_ut"], writes=[prefix + "_ut"],
            tag=f"{prefix}_u2 t{t}")
        sc.op("V", lambda v, t=t: v.scalar_tensor_tensor(
            xht[:], ut[:], nmr[:, t:t + 1], GB[:, 2048:3072], ALU.mult, ALU.mult),
            reads=[prefix + "_ut", prefix + "_nmr", f"{prefix}_GB:2",
                   prefix + "_xht"],
            writes=[prefix + "_xht"], tag=f"{prefix}_xh2 t{t}")
        for c in range(CT):
            pkk = f"PT{(t * CT + c) % 2}"
            sc.op("T", lambda tn, c=c, pkk=pkk: tn.transpose(
                b[pkk][:], xht[:, c * 128:(c + 1) * 128], b["ident_bf"][:]),
                reads=[prefix + "_xht", "ident_bf"],
                writes=[pkk], tag=f"{prefix}_tpxh t{t} c{c}")
            sc.op("S", lambda s, t=t, c=c, pkk=pkk: s.copy(
                xhf[:, c * T + t * 128: c * T + (t + 1) * 128], b[pkk][:]),
                reads=[pkk], writes=[f"{prefix}_xa:{c}:{t}"],
                tag=f"{prefix}_evxh t{t} c{c}")

    load_convert_bf16(sc, b, prefix + "_wo", W["wout_h"].ap(), 512, D,
                      stage=b["SCR"][:, 0:1024], stage_key="SCR:0", dst=wout)
    load_pair_w(("wq_h", "wk_h", "wv_h"), W, (0, 0, 0), 0, 0)
    for p in range(4):
        par = p % 2
        if p + 1 < 4:
            load_pair_w(("wq_h", "wk_h", "wv_h"), W, (0, 0, 0), p + 1, (p + 1) % 2)
        WQk = [f"{prefix}_wkv{par}:0"]
        WKk = [f"{prefix}_wkv{par}:1"]
        WVk = [f"{prefix}_wkv{par}:2"]
        for wi, (dst, wk, xk, xf_) in enumerate(((Q2, WQk, XHK, xhf),
                                                 (K2, WKk, XHK, xhf))):
            for n in range(8):
                kq = f"P{n % 2}"
                def qmm(eng, par=par, n=n, wi=wi, xf_=xf_):
                    pp = pq[n % 2]
                    for c in range(CT):
                        ins = eng.matmul(
                            pp[:], wqkvL[par][:, wi * 1024 + c * 128: wi * 1024 + c * 128 + 128],
                            xf_[:, c * T + n * 512: c * T + (n + 1) * 512],
                            start=(c == 0), stop=(c == CT - 1))
                    return ins
                sc.op("T", qmm, reads=wk + xk, writes=[kq],
                      tag=f"{prefix}_hqk p{p} w{wi} n{n}")
                sc.op("S", lambda s, n=n, dst=dst: s.copy(
                    dst[:, n * 512:(n + 1) * 512], pq[n % 2][:]),
                    reads=[kq], writes=[f"{prefix}_{'Q2' if wi == 0 else 'K2'}:{n}"],
                    tag=f"{prefix}_hqke p{p} w{wi} n{n}")
        for t in range(TT):
            kk = f"P{2 + t % 2}"
            def vmm(eng, par=par, t=t):
                pp = pA[t % 2]
                for c in range(CT):
                    ins = eng.matmul(
                        pp[:, 0:128], xhf[:, c * T + t * 128: c * T + (t + 1) * 128],
                        wqkvL[par][:, 2 * 1024 + c * 128: 2 * 1024 + c * 128 + 128],
                        start=(c == 0), stop=(c == CT - 1))
                return ins
            sc.op("T", vmm, reads=WVk + XHK, writes=[kk], tag=f"{prefix}_hv p{p} t{t}")
            for h in range(2):
                sc.op("V", lambda v, t=t, h=h, pi=t % 2: v.tensor_scalar(
                    v2l[:, t * 130 + h * 65: t * 130 + h * 65 + 64],
                    pA[pi][:, h * 64:(h + 1) * 64],
                    rts[:, 96 + t:96 + t + 1], None, ALU.mult),
                    reads=[kk, f"{prefix}_rb3"], writes=[f"{prefix}_v2l:{t}"],
                    tag=f"{prefix}_hve p{p} t{t} h{h}")
                sc.op("V", lambda v, t=t, h=h: v.tensor_copy(
                    v2l[:, t * 130 + h * 65 + 64: t * 130 + h * 65 + 65],
                    rts[:, 96 + t:96 + t + 1]),
                    reads=[f"{prefix}_rb3", f"{prefix}_v2l:{t}"],
                    writes=[f"{prefix}_v2l:{t}"],
                    tag=f"{prefix}_hvo p{p} t{t} h{h}")
        for h in range(2):
            base = 64 * h
            hh = 2 * p + h
            Q2K = [f"{prefix}_Q2:{n}" for n in range(8)]
            K2K = [f"{prefix}_K2:{n}" for n in range(8)]
            for qc in range(16):
                for kc in range(TT):
                    pi = kc % 4
                    ks = f"P{pi}"
                    ke = f"{prefix}_exph:{pi}"
                    sc.op("T", lambda eng, base=base, kc=kc, qc=qc, pi=pi: eng.matmul(
                        pq2[pi][:, 0:256], K2[base:base + 64, kc * 128:(kc + 1) * 128],
                        Q2[base:base + 64, qc * 256:(qc + 1) * 256],
                        start=True, stop=True),
                        reads=Q2K + K2K, writes=[ks],
                        tag=f"{prefix}_hs p{p} h{h} q{qc} k{kc}")
                    sc.op("S", lambda s, pi=pi: s.activation(
                        exph[:, pi * 256:(pi + 1) * 256], pq2[pi][:, 0:256],
                        AF.Exp, scale=SCL8),
                        reads=[ks], writes=[ke], tag=f"{prefix}_hse p{p} h{h} q{qc} k{kc}")
                    def hav(eng, pi=pi, kc=kc, hh=hh):
                        for qs in range(2):
                            ins = eng.matmul(
                                pav[qs][:, 0:65],
                                exph[:, pi * 256 + qs * 128: pi * 256 + qs * 128 + 128],
                                v2l[:, kc * 130 + (hh % 2) * 65: kc * 130 + (hh % 2) * 65 + 65],
                                start=(kc == 0), stop=False)
                        return ins
                    sc.op("T", hav, reads=[ke, f"{prefix}_v2l:{kc}"],
                          writes=["P4", "P5"],
                          tag=f"{prefix}_hav p{p} h{h} q{qc} k{kc}")
                kn = "P0"
                sc.op("T", lambda eng, base=base, p=p, qc=qc: eng.matmul(
                    pq[0][0:1, 0:256], nkcol[base:base + 64, p:p + 1],
                    Q2[base:base + 64, qc * 256:(qc + 1) * 256],
                    start=True, stop=True),
                    reads=Q2K + [prefix + "_nk"], writes=[kn],
                    tag=f"{prefix}_hn p{p} h{h} q{qc}")
                sc.op("S", lambda s: s.activation(
                    expn[0:1, 0:256], pq[0][0:1, 0:256], AF.Exp, scale=SCL8),
                    reads=[kn], writes=[f"{prefix}_expn"],
                    tag=f"{prefix}_hne p{p} h{h} q{qc}")
                def hnav(eng, hh=hh):
                    for qs in range(2):
                        ins = eng.matmul(
                            pav[qs][:, 0:65], expn[0:1, qs * 128:(qs + 1) * 128],
                            nvrow[0:1, (hh) * 65:(hh) * 65 + 65],
                            start=False, stop=True)
                    return ins
                sc.op("T", hnav, reads=[f"{prefix}_expn", prefix + "_nv"],
                      writes=["P4", "P5"],
                      tag=f"{prefix}_hnav p{p} h{h} q{qc}")
                for qs in range(2):
                    tglob = qc * 2 + qs
                    kA = f"P{4 + qs}"
                    sc.op("V", lambda v, qs=qs: v.reciprocal(
                        recL[:, 2 + qs:3 + qs], pav[qs][:, 64:65]),
                        reads=[kA], writes=[f"{prefix}_recH:{qs}"],
                        tag=f"{prefix}_hrec p{p} h{h} q{qc} s{qs}")
                    sc.op("V", lambda v, qs=qs, tglob=tglob, h=h: v.tensor_scalar(
                        aotL[:, tglob * 128 + h * 64: tglob * 128 + h * 64 + 64],
                        pav[qs][:, 0:64], recL[:, 2 + qs:3 + qs], None,
                        ALU.mult),
                        reads=[kA, f"{prefix}_recH:{qs}"],
                        writes=[f"{prefix}_aotL:{tglob}:{h}"],
                        tag=f"{prefix}_haev p{p} h{h} q{qc} s{qs}")
        for t in range(TT):
            pkk = f"PT{t % 2}"
            kst = f"{prefix}_aofst{t % 2}"
            sc.op("T", lambda eng, t=t, pkk=pkk: eng.transpose(
                b[pkk][:], aotL[:, t * 128:(t + 1) * 128], b["ident_bf"][:]),
                reads=[f"{prefix}_aotL:{t}:0", f"{prefix}_aotL:{t}:1", "ident_bf"],
                writes=[pkk], tag=f"{prefix}_hatr p{p} t{t}")
            sc.op("S", lambda s, t=t, pkk=pkk: s.copy(aofst[t % 2][:], b[pkk][:]),
                  reads=[pkk], writes=[kst], tag=f"{prefix}_hatre p{p} t{t}")
            sc.op("Q", lambda q, t=t, p=p: q.dma_start(
                aofh.ap()[p * TT + t:p * TT + t + 1, :].rearrange(
                    "a (p2 f) -> (a p2) f", p2=128), aofst[t % 2][:]),
                reads=[kst], writes=[f"{prefix}_aofh:{p}:{t}"], dma=True,
                tag=f"{prefix}_hast p{p} t{t}")

    guard(prefix + "_FBg")
    dma_in(sc, b, rowstage, W["null_q"].ap(), "SCR:0", keys_r=["SCR:0"],
           tag="ld_null_q")
    for half in range(2):
        kps = f"P{half}"
        sc.op("T", lambda t, half=half: t.matmul(
            pq[half][:], b["ones_row"][:],
            rowstage[:, half * 512:(half + 1) * 512],
            start=True, stop=True),
            reads=["ones_row", "SCR:0", prefix + "_FBg"],
            writes=[kps], tag=f"{prefix}_nqb{half}")
        sc.op("V", lambda v, half=half: v.tensor_copy(
            FB[:, half * 512:(half + 1) * 512], pq[half][:]),
            reads=[kps, prefix + "_FBg", f"{prefix}_FB:0"],
            writes=[f"{prefix}_FB:0"], tag=f"{prefix}_nqbe{half}")
    for t in range(TT):
        par = t % 2
        kxt = f"SCR:{par}"
        klt = f"{prefix}_FB:1"
        xtile = b["SCR"][:, par * 1024:(par + 1) * 1024]
        ltile = FB[:, 1024:2048]
        dma_in(sc, b, xtile, xin.ap()[t * 128:(t + 1) * 128, :], kxt,
               keys_r=[kxt], tag=f"{prefix}_cx t{t}")
        sc.op("Q", lambda q, t=t, ltile=ltile: q.dma_start(
            ltile, lighth.ap()[t * 128:(t + 1) * 128, :]),
            reads=[f"{prefix}_lh:{t}", klt, prefix + "_FBg"], writes=[klt],
            dma=True, tag=f"{prefix}_cl t{t}")
        kld = f"{prefix}_aofld{par}"
        for c in range(4):
            sc.op("Q", lambda q, t=t, c=c, par=par: q.dma_start(
                aofld[par][:, c * 128:(c + 1) * 128],
                aofh.ap()[c * TT + t:c * TT + t + 1, :].rearrange(
                    "a (p2 f) -> (a p2) f", p2=128)),
                reads=[f"{prefix}_aofh:{c}:{t}", kld + f":{c}"],
                writes=[kld + f":{c}"], dma=True, tag=f"{prefix}_chl t{t} c{c}")
        for half in range(2):
            kp = f"P{half}"
            def hpm(eng, par=par, half=half):
                pp = pq[half]
                for c in range(4):
                    ins = eng.matmul(
                        pp[:], aofld[par][:, c * 128:(c + 1) * 128],
                        wout[:, c * 1024 + half * 512: c * 1024 + half * 512 + 512],
                        start=(c == 0), stop=(c == 3))
                return ins
            sc.op("T", hpm,
                  reads=[kld + f":{c}" for c in range(4)] +
                        [f"{prefix}_wo:{c}" for c in range(4)],
                  writes=[kp], tag=f"{prefix}_chm t{t} h{half}")
            hs_ = slice(half * 512, (half + 1) * 512)
            sc.op("V", lambda v, half=half, t=t, hs_=hs_: v.tensor_scalar(
                ut[:, hs_], pq[half][:], rts[:, 32 + t:32 + t + 1], None,
                ALU.mult),
                reads=[kp, f"{prefix}_rb1", prefix + "_ut"],
                writes=[prefix + "_ut"], tag=f"{prefix}_ca t{t} h{half}")
            sc.op("V", lambda v, half=half, t=t, hs_=hs_: v.scalar_tensor_tensor(
                ut[:, hs_], FB[:, hs_], rts[:, 192 + t:192 + t + 1], ut[:, hs_],
                ALU.mult, ALU.add),
                reads=[f"{prefix}_FB:0", f"{prefix}_rb6", prefix + "_ut"],
                writes=[prefix + "_ut"], tag=f"{prefix}_cb t{t} h{half}")
            sc.op("V", lambda v, half=half, ltile=ltile, hs_=hs_: v.scalar_tensor_tensor(
                ut[:, hs_], ltile[:, hs_], 0.5, ut[:, hs_], ALU.mult, ALU.add),
                reads=[klt, prefix + "_ut"], writes=[prefix + "_ut"],
                tag=f"{prefix}_cc t{t} h{half}")
            sc.op("V", lambda v, half=half, xtile=xtile, hs_=hs_: v.tensor_tensor(
                ut[:, hs_], ut[:, hs_], xtile[:, hs_], ALU.add),
                reads=[kxt, prefix + "_ut"], writes=[prefix + "_ut"],
                tag=f"{prefix}_cd t{t} h{half}")
        sc.op("Q", lambda q, t=t: q.dma_start(
            xout.ap()[t * 128:(t + 1) * 128, :], ut[:]),
            reads=[prefix + "_ut"], writes=[f"xo:{t}", prefix + "_ut"], dma=True,
            tag=f"{prefix}_cs t{t}")


def _prep_l0a(g):
    wqkv = g["L0_attn_w_qkv"]
    return {
        "wqT": np.ascontiguousarray(wqkv[0:1024].T),
        "wkT": np.ascontiguousarray(wqkv[1024:2048].T),
        "wvT": np.ascontiguousarray(wqkv[2048:3072].T),
        "w_projT": np.ascontiguousarray(g["L0_attn_w_proj"].T),
        "b_proj": np.ascontiguousarray(g["L0_attn_b_proj"]),
    }


_L0A_SHAPES = {"wqT": [D, D], "wkT": [D, D], "wvT": [D, D], "w_projT": [D, D],
               "b_proj": [D]}


def _prep_l0f(g):
    qw = g["L0_ffn_q_w"].reshape(1024, 16, 64)[:, :, :32].reshape(1024, 512)
    qb = g["L0_ffn_q_b"].reshape(16, 64)[:, :32].reshape(512)
    return {
        "qw": np.ascontiguousarray(qw),
        "kw": np.ascontiguousarray(g["L0_ffn_k_w"]),
        "vw": np.ascontiguousarray(g["L0_ffn_v_w"]),
        "qb_dev": np.ascontiguousarray(qb.reshape(4, 128).T),
        "kb_dev": np.ascontiguousarray(g["L0_ffn_k_b"].reshape(4, 128).T),
        "vb_row": np.ascontiguousarray(g["L0_ffn_v_b"][None, :]),
        "ow": np.ascontiguousarray(g["L0_ffn_o_w"][:512]),
        "ob_dev": np.ascontiguousarray(g["L0_ffn_o_b"].reshape(8, 128).T),
        "rw": np.ascontiguousarray(g["L0_ffn_router_w"]),
        "rb_row": np.ascontiguousarray(g["L0_ffn_router_b"][None, :]),
        "w1": np.ascontiguousarray(g["L0_ffn_w1"]),
        "b1_dev": np.ascontiguousarray(
            g["L0_ffn_b1"].reshape(8, 16, 128).transpose(2, 0, 1).reshape(128, 128)),
        "w2": np.ascontiguousarray(g["L0_ffn_w2"]),
    }


_L0F_SHAPES = {"qw": [D, 512], "kw": [D, 512], "vw": [D, 512],
               "qb_dev": [128, 4], "kb_dev": [128, 4], "vb_row": [1, 512],
               "ow": [512, D], "ob_dev": [128, 8], "rw": [D, 8],
               "rb_row": [1, 8], "w1": [8, D, 2048], "b1_dev": [128, 128],
               "w2": [8, 2048, D]}


def _prep_l1t(g):
    Ws, Wq, Wk = g["L1_attn_W_share"], g["L1_attn_Wq"], g["L1_attn_Wk"]
    bs, bq, bk = g["L1_attn_b_share"], g["L1_attn_bq"], g["L1_attn_bk"]
    Wq_big = np.zeros((D, D), np.float32)
    Wk_big = np.zeros((D, D), np.float32)
    bq_big = np.zeros(D, np.float32)
    bk_big = np.zeros(D, np.float32)
    for h in range(16):
        Wq_big[:, h * 64:h * 64 + 16] = Ws
        Wq_big[:, h * 64 + 16:(h + 1) * 64] = Wq[:, h * 48:(h + 1) * 48]
        Wk_big[:, h * 64:h * 64 + 16] = Ws
        Wk_big[:, h * 64 + 16:(h + 1) * 64] = Wk[:, h * 48:(h + 1) * 48]
        bq_big[h * 64:h * 64 + 16] = bs
        bq_big[h * 64 + 16:(h + 1) * 64] = bq[h * 48:(h + 1) * 48]
        bk_big[h * 64:h * 64 + 16] = bs
        bk_big[h * 64 + 16:(h + 1) * 64] = bk[h * 48:(h + 1) * 48]
    return {
        "Wq_big": np.ascontiguousarray(Wq_big),
        "Wk_big": np.ascontiguousarray(Wk_big),
        "Wv": np.ascontiguousarray(g["L1_attn_Wv"]),
        "Wo": np.ascontiguousarray(g["L1_attn_Wo"]),
        "bq_dev": np.ascontiguousarray(bq_big.reshape(8, 128).T),
        "bk_dev": np.ascontiguousarray(bk_big.reshape(8, 128).T),
        "bv_row": np.ascontiguousarray(g["L1_attn_bv"][None, :]),
        "bo": np.ascontiguousarray(g["L1_attn_bo"]),
    }


_L1T_SHAPES = {"Wq_big": [D, D], "Wk_big": [D, D], "Wv": [D, D], "Wo": [D, D],
               "bq_dev": [128, 8], "bk_dev": [128, 8], "bv_row": [1, D],
               "bo": [D]}


def _prep_l1f(g):
    wkv = g["L1_ffn_w_kv_h"]
    nk = g["L1_ffn_null_kv"][0]
    nv = g["L1_ffn_null_kv"][1]
    nkd = np.zeros((128, 4), np.float32)
    for p in range(4):
        nkd[0:64, p] = nk[2 * p]
        nkd[64:128, p] = nk[2 * p + 1]
    nvr = np.zeros((1, 520), np.float32)
    for h in range(8):
        nvr[0, h * 65:h * 65 + 64] = nv[h]
        nvr[0, h * 65 + 64] = 1.0
    return {
        "wqkv_l": np.ascontiguousarray(g["L1_ffn_w_qkv_l"]),
        "wout_l": np.ascontiguousarray(g["L1_ffn_w_out_l"]),
        "wq_h": np.ascontiguousarray(g["L1_ffn_w_q_h"]),
        "wk_h": np.ascontiguousarray(wkv[:, :, :64].reshape(1024, 512)),
        "wv_h": np.ascontiguousarray(wkv[:, :, 64:].reshape(1024, 512)),
        "wout_h": np.ascontiguousarray(g["L1_ffn_w_out_h"]),
        "ln_g": np.ascontiguousarray(g["L1_ffn_ln_g"][None, :]),
        "ln_b": np.ascontiguousarray(g["L1_ffn_ln_b"][None, :]),
        "rms_g": np.ascontiguousarray(g["L1_ffn_rms_g"][None, :]),
        "null_q": np.ascontiguousarray(g["L1_ffn_null_q"][None, :]),
        "rt_q": np.ascontiguousarray(g["L1_ffn_rt_q"]),
        "rt_kv": np.ascontiguousarray(g["L1_ffn_rt_kv"]),
        "nullk_dev": nkd,
        "nullv_row": nvr,
    }


_L1F_SHAPES = {"wqkv_l": [D, 1536], "wout_l": [512, D], "wq_h": [D, 512],
               "wk_h": [D, 512], "wv_h": [D, 512], "wout_h": [512, D],
               "ln_g": [1, D], "ln_b": [1, D], "rms_g": [1, D],
               "null_q": [1, D], "rt_q": [1, D], "rt_kv": [1, D],
               "nullk_dev": [128, 4], "nullv_row": [1, 520]}


def _build_sub(kind):
    nc = bass.Bass("TRN2", target_bir_lowering=False)
    es = ExitStack()
    b = Bufs(nc, es)
    sc = Sched()
    xin = b.dram("x_in", [T, D], F32, kind="ExternalInput")
    xout = b.dram("x_out", [T, D], F32, kind="ExternalOutput")
    shapes = {"l0a": _L0A_SHAPES, "l0f": _L0F_SHAPES, "l1t": _L1T_SHAPES,
              "l1f": _L1F_SHAPES}[kind]
    W = {n: b.dram(kind + "_" + n, s, F32, kind="ExternalInput")
         for n, s in shapes.items()}
    with es:
        emit_setup(sc, b)
        if kind == "l0a":
            emit_ln_xnf(sc, b, xin, "xnf", "ln0")
            emit_l0_attn(sc, b, W, "xnf", xin, xout)
        elif kind == "l0f":
            w1bf = b.dram("w1bf", [8, D, 2048], BF16)
            w2bf = b.dram("w2bf", [8, 2048, D], BF16)
            emit_ln_xnf(sc, b, xin, "xnf", "ln0")
            emit_l0_ffn(sc, b, W, "xnf", xin, xout, "l0f",
                        w1bf=w1bf, w2bf=w2bf, convert_moe=True)
        elif kind == "l1t":
            emit_ln_xnf(sc, b, xin, "xnf", "ln0")
            emit_l1_attn(sc, b, W, "xnf", xin, xout)
        elif kind == "l1f":
            emit_l1_ffn(sc, b, W, xin, xout, "l1f")
        sc.build(nc, es)
        sc.emit(nc)
    return nc


_NC_CACHE = {}


def kernel(**inputs):
    g = {k: np.asarray(v, dtype=np.float32) for k, v in inputs.items()}
    Wl0f = _prep_l0f(g)
    Wl0a = _prep_l0a(g)
    Wl1f = _prep_l1f(g)
    Wl1t = _prep_l1t(g)
    for kind in ("l0f", "l0a", "l1t", "l1f"):
        if kind not in _NC_CACHE:
            _NC_CACHE[kind] = _build_sub(kind)
    x = np.ascontiguousarray(g["L0_attn_x"])
    shards = [np.ascontiguousarray(x[i]) for i in range(8)]
    seq = [("l0f", Wl0f), ("l0a", Wl0a), ("l0f", Wl0f),
           ("l1f", Wl1f), ("l1t", Wl1t), ("l1f", Wl1f)]
    for kind, Wm in seq:
        nc = _NC_CACHE[kind]
        wmap = {kind + "_" + k: v for k, v in Wm.items()}
        in_maps = [{"x_in": shards[i], **wmap} for i in range(8)]
        res = run_bass_kernel_spmd(nc, in_maps, core_ids=list(range(8)))
        shards = [np.ascontiguousarray(r["x_out"]) for r in res.results]
    return np.stack(shards, axis=0).astype(np.float32)


_REMAP = {
    "xnf": ("A8", 0),
    "l1f_xa": ("A8", 0),
    "l0a_wp": ("PB1", 0), "l0a_wqkv0": ("PB1", 8192), "l0a_wqkv1": ("PB1", 11264),
    "l0a_aofst0": ("PB1", 14336), "l0a_aofst1": ("PB1", 14464),
    "l0a_qf": ("PB2", 0), "l0a_ktv": ("PB2", 4096),
    "l0a_aot0": ("PB2", 12288), "l0a_aot1": ("PB3", 0),
    "l0a_aofld0": ("PB3", 4096), "l0a_aofld1": ("PB3", 5120),
    "l0a_stage0": ("PF1", 0), "l0a_stage1": ("PF1", 1024),
    "l0a_xt0": ("PF1", 2048), "l0a_xt1": ("PF1", 3072),
    "l0a_yt0": ("PF1", 4096), "l0a_yt1": ("PF1", 5120),
    "l0a_bproj_bc": ("PF1", 6144), "l0a_bproj_row": ("PF1", 7168),
    "l0a_c2b": ("PB1", 14592),
    "l1t_wo": ("PB1", 0), "l1t_wqkv0": ("PB1", 8192), "l1t_wqkv1": ("PB1", 11264),
    "l1t_aofst0": ("PB1", 14336), "l1t_aofst1": ("PB1", 14464),
    "l1t_qf": ("PB2", 0), "l1t_kf": ("PB2", 4096), "l1t_vt2": ("PB2", 8192),
    "l1t_aot0": ("PB3", 0), "l1t_aot1": ("PB3", 4096),
    "l1t_expb0": ("PB3", 8192), "l1t_expb1": ("PB3", 8704),
    "l1t_aofld0": ("PB3", 9216), "l1t_aofld1": ("PB3", 10240),
    "l1t_masks": ("PB3", 11264),
    "l1t_stage0": ("PF1", 0), "l1t_stage1": ("PF1", 1024),
    "l1t_xt0": ("PF1", 2048), "l1t_xt1": ("PF1", 3072),
    "l1t_bo_bc": ("PF1", 6144), "l1t_bo_row": ("PF1", 7168),
    "l1t_bvrow": ("PF1", 4096), "l1t_XR:0": ("PF1", 5120),
    "l1t_bqk": ("PF1", 5250), "l1t_recb": ("PF1", 5270),
    "l0f_AR1": ("PB1", 0), "l0f_AR2": ("PB2", 0), "l0f_aof": ("PB3", 0),
    "l0f_XR": ("PF1", 0),
    "l1f_v2l": ("PB1", 0), "l1f_wkv0": ("PB1", 8320), "l1f_wkv1": ("PB1", 11392),
    "l1f_Q2": ("PB2", 0), "l1f_K2": ("PB2", 4096), "l1f_aotL": ("PB2", 8192),
    "l1f_wout": ("PB2", 12288),
    "l1f_exph": ("PB3", 0), "l1f_expT": ("PB3", 1024), "l1f_GB": ("PB3", 1408),
    "l1f_xnlt": ("PB3", 4480), "l1f_xht": ("PB3", 5504),
    "l1f_aofst0": ("PB3", 6528), "l1f_aofst1": ("PB3", 6656),
    "l1f_aofld0": ("PB3", 6784), "l1f_aofld1": ("PB3", 7296),
    "l1f_Ebuf": ("PB3", 7808), "l1f_LT": ("PB3", 7840), "l1f_expn": ("PB3", 7968),
    "l1f_FB": ("PF1", 0), "l1f_ut": ("PF1", 2048),
    "l1f_nmr": ("PF1", 3072), "l1f_rts": ("PF1", 3200),
    "l1f_pstb": ("PF1", 3552), "l1f_bis": ("PF1", 3584),
    "l1f_recL": ("PF1", 3680), "l1f_nkf": ("PF1", 3690),
    "l1f_nvf": ("PF1", 3800), "l1f_nv": ("PB3", 9000),
    "l1f_nk": ("PB3", 9600),
}


def build_fused():
    nc = bass.Bass("TRN2", target_bir_lowering=False)
    es = ExitStack()
    b = Bufs(nc, es)
    sc = Sched()
    b.remap = dict(_REMAP)
    xin = b.dram("x_in", [T, D], F32, kind="ExternalInput")
    xout = b.dram("x_out", [T, D], F32, kind="ExternalOutput")
    xs = [xin] + [b.dram(f"xs{i}", [T, D], F32) for i in range(1, 6)] + [xout]
    Wl0a = {n: b.dram("l0a_" + n, s, F32, kind="ExternalInput")
            for n, s in _L0A_SHAPES.items()}
    Wl0f = {n: b.dram("l0f_" + n, s, F32, kind="ExternalInput")
            for n, s in _L0F_SHAPES.items()}
    Wl1t = {n: b.dram("l1t_" + n, s, F32, kind="ExternalInput")
            for n, s in _L1T_SHAPES.items()}
    Wl1f = {n: b.dram("l1f_" + n, s, F32, kind="ExternalInput")
            for n, s in _L1F_SHAPES.items()}
    w1bf = b.dram("w1bf", [8, D, 2048], BF16)
    w2bf = b.dram("w2bf", [8, 2048, D], BF16)
    with es:
        emit_setup(sc, b)
        b.sbuf("A8", [128, 32768], BF16)
        b.sbuf("PB1", [128, 16384], BF16)
        b.sbuf("PB2", [128, 16384], BF16)
        b.sbuf("PB3", [128, 16384], BF16)
        b.sbuf("PF1", [128, 8192], F32)
        emit_ln_xnf(sc, b, xs[0], "xnf", "ln0")
        emit_l0_ffn(sc, b, Wl0f, "xnf", xs[0], xs[1], "l0f",
                    w1bf=w1bf, w2bf=w2bf, convert_moe=True)
        sc.fence()
        emit_ln_xnf(sc, b, xs[1], "xnf", "ln0")
        emit_l0_attn(sc, b, Wl0a, "xnf", xs[1], xs[2])
        sc.fence()
        emit_ln_xnf(sc, b, xs[2], "xnf", "ln0")
        emit_l0_ffn(sc, b, Wl0f, "xnf", xs[2], xs[3], "l0f",
                    w1bf=w1bf, w2bf=w2bf, convert_moe=False)
        sc.fence()
        emit_l1_ffn(sc, b, Wl1f, xs[3], xs[4], "l1f")
        sc.fence()
        emit_ln_xnf(sc, b, xs[4], "xnf", "ln0")
        emit_l1_attn(sc, b, Wl1t, "xnf", xs[4], xs[5])
        sc.fence()
        emit_l1_ffn(sc, b, Wl1f, xs[5], xs[6], "l1f")
        sc.build(nc, es)
        sc.emit(nc)
    return nc, len(sc.ops)


_FUSED_CACHE = []


def kernel(**inputs):
    g = {k: np.asarray(v, dtype=np.float32) for k, v in inputs.items()}
    wmap = {}
    for kind, prep in (("l0a", _prep_l0a), ("l0f", _prep_l0f),
                       ("l1t", _prep_l1t), ("l1f", _prep_l1f)):
        for k, v in prep(g).items():
            wmap[kind + "_" + k] = v
    if not _FUSED_CACHE:
        nc, nops = build_fused()
        _FUSED_CACHE.append(nc)
    nc = _FUSED_CACHE[0]
    x = np.ascontiguousarray(g["L0_attn_x"])
    in_maps = [{"x_in": np.ascontiguousarray(x[i]), **wmap} for i in range(8)]
    res = run_bass_kernel_spmd(nc, in_maps, core_ids=list(range(8)))
    return np.stack([r["x_out"] for r in res.results], axis=0).astype(np.float32)


# version 04
# speedup vs baseline: 1.6121x; 1.6121x over previous
import numpy as np
from contextlib import ExitStack
import concourse.bass as bass
import concourse.mybir as mybir
from concourse.bass_utils import run_bass_kernel_spmd

F32 = mybir.dt.float32
BF16 = mybir.dt.bfloat16
U8 = mybir.dt.uint8
AF = mybir.ActivationFunctionType
ALU = mybir.AluOpType
AX = mybir.AxisListType

T = 4096
D = 1024
TT = T // 128
CT = D // 128
NEG = -1e30


class Op:
    __slots__ = ("idx", "stream", "fn", "deps", "dma", "inc", "count_after",
                 "waits", "tag", "dma_ch")


class Sched:

    STREAMS = ("T", "V", "S", "P", "Q")
    NCH = 16

    def __init__(self):
        self.ops = []
        self.last_writer = {}
        self.readers = {}
        self._dma_rr = 0
        self._dma_last = [None] * self.NCH
        self._dma_g2 = {}
        self._last_op = {}
        self._fence_idx = None
        self._fence_seen = set()

    def op(self, stream, fn, reads=(), writes=(), dma=False, tag="", dma_group=None):
        o = Op()
        o.idx = len(self.ops)
        o.stream = stream
        o.fn = fn
        o.dma = dma
        o.inc = False
        o.waits = []
        o.tag = tag
        deps = set()
        if dma:
            if dma_group is None:
                ch = self._dma_rr % 12
                self._dma_rr += 1
            else:
                cnt = self._dma_g2.get(dma_group, 0)
                self._dma_g2[dma_group] = cnt + 1
                ch = 12 + cnt % 4
            o.dma_ch = ch
            if self._dma_last[ch] is not None:
                deps.add(self._dma_last[ch])
            self._dma_last[ch] = o.idx
        for b in reads:
            w = self.last_writer.get(b)
            if w is not None:
                deps.add(w)
        for b in writes:
            w = self.last_writer.get(b)
            if w is not None:
                deps.add(w)
            for r in self.readers.get(b, ()):
                deps.add(r)
        if self._fence_idx is not None and stream not in self._fence_seen:
            deps.add(self._fence_idx)
            self._fence_seen.add(stream)
        deps.discard(o.idx)
        o.deps = sorted(d for d in deps
                        if not (stream == "T" and not dma
                                and self.ops[d].stream == "T"
                                and not self.ops[d].dma))
        for b in reads:
            self.readers.setdefault(b, []).append(o.idx)
        for b in writes:
            self.last_writer[b] = o.idx
            self.readers[b] = []
        self._last_op[stream] = o.idx
        self.ops.append(o)
        return o

    def fence(self):
        deps = set()
        for s, i in self._last_op.items():
            deps.add(i)
        for i in self._dma_last:
            if i is not None:
                deps.add(i)
        o = Op()
        o.idx = len(self.ops)
        o.stream = "V"
        o.fn = lambda v: v.engine_nop()
        o.dma = False
        o.inc = False
        o.waits = []
        o.tag = "FENCE"
        o.deps = sorted(deps)
        self.ops.append(o)
        self._last_op["V"] = o.idx
        self.last_writer = {}
        self.readers = {}
        self._fence_idx = o.idx
        self._fence_seen = {"V"}

    def build(self, nc, es):
        sems = {}
        for s in self.STREAMS:
            sems["c" + s] = es.enter_context(nc.semaphore("sem_c" + s))
        for i in range(self.NCH):
            sems[f"m{i}"] = es.enter_context(nc.semaphore(f"sem_m{i}"))
        for o in self.ops:
            for d in o.deps:
                self.ops[d].inc = True
            if o.dma:
                o.inc = True
        counters = {}
        for o in self.ops:
            key = f"m{o.dma_ch}" if o.dma else ("c" + o.stream)
            if o.inc:
                counters[key] = counters.get(key, 0) + (16 if o.dma else 1)
                o.count_after = (key, counters[key])
            else:
                o.count_after = None
        know = {s: {} for s in self.STREAMS}
        snaps = {}
        for o in self.ops:
            k = know[o.stream]
            for d in o.deps:
                do = self.ops[d]
                sem, c = do.count_after
                if k.get(sem, 0) < c:
                    o.waits.append((sem, c))
                    for s2, c2 in snaps[d].items():
                        if k.get(s2, 0) < c2:
                            k[s2] = c2
            if o.inc:
                snap = dict(k)
                sem, c = o.count_after
                snap[sem] = c
                snaps[o.idx] = snap
        self._sems = sems
        self._final = dict(counters)
        return sems

    def emit(self, nc):
        sems = self._sems
        per_stream = {s: [] for s in self.STREAMS}
        for o in self.ops:
            per_stream[o.stream].append(o)

        def mk_body(ops):
            def body(eng):
                for o in ops:
                    for (semkey, c) in o.waits:
                        eng.wait_ge(sems[semkey], c)
                    if o.fn is None:
                        continue
                    ins = o.fn(eng)
                    if o.inc:
                        semkey, c = o.count_after
                        ins.then_inc(sems[semkey], 16 if o.dma else 1)
            return body

        with nc.Block() as block:
            block.tensor(mk_body(per_stream["T"]))
            block.vector(mk_body(per_stream["V"]))
            block.scalar(mk_body(per_stream["S"]))
            block.gpsimd(mk_body(per_stream["P"]))

            def sync_body(eng):
                mk_body(per_stream["Q"])(eng)
                for key, c in self._final.items():
                    eng.wait_ge(sems[key], c)
            block.sync(sync_body)
        return nc


class Bufs:

    def __init__(self, nc, es):
        self.nc = nc
        self.es = es
        self.t = {}
        self.remap = {}

    def sbuf(self, name, shape, dtype):
        if name in self.t:
            return self.t[name]
        if name in self.remap:
            pool, lo = self.remap[name]
            ph = self.t[pool]
            ap = ph[:, lo:lo + shape[1]] if shape[0] == 128 else \
                ph[0:shape[0], lo:lo + shape[1]]
            self.t[name] = ap
            return ap
        h = self.es.enter_context(self.nc.sbuf_tensor(name, list(shape), dtype))
        self.t[name] = h
        return h

    def psum(self, name, shape, dtype=F32):
        h = self.es.enter_context(self.nc.psum_tensor(name, list(shape), dtype))
        self.t[name] = h
        return h

    def dram(self, name, shape, dtype=F32, kind="Internal"):
        if name in self.t:
            return self.t[name]
        h = self.nc.dram_tensor(name, list(shape), dtype, kind=kind)
        self.t[name] = h
        return h

    def __getitem__(self, name):
        return self.t[name]

    def const_col(self, sc, value, dtype=F32):
        key = f"cc_{dtype}_{value!r}"
        if key not in self.t:
            h = self.sbuf(key, [128, 1], dtype)
            sc.op("V", lambda v, h=h, value=value: v.memset(h[:], float(value)),
                  writes=[key], tag=key)
        return self.t[key], key


def emit_setup(sc, b):
    nc = b.nc
    for i in range(6):
        b.psum(f"P{i}", [128, 512], F32)
    b.psum("PT0", [128, 128], BF16)
    b.psum("PT1", [128, 128], BF16)
    b.sbuf("SCR", [128, 2048], F32)
    ident = b.sbuf("ident_bf", [128, 128], BF16)
    onesrow = b.sbuf("ones_row", [1, 128], F32)
    onescol_bf = b.sbuf("ones_col_bf", [128, 1], BF16)
    iden_i = b.sbuf("ident_i32", [128, 128], F32)
    sc.op("P", lambda g: g.iota(iden_i[:], pattern=[[-1, 128]], base=0,
                                channel_multiplier=1,
                                allow_small_or_imprecise_dtypes=True),
          writes=["iden_i"], tag="iota")
    sc.op("P", lambda g: g.tensor_scalar(ident[:], iden_i[:], 0.0, None,
                                         ALU.is_equal),
          reads=["iden_i"], writes=["ident_bf"], tag="ident")
    sc.op("V", lambda v: v.memset(onesrow[:], 1.0), writes=["ones_row"])
    sc.op("V", lambda v: v.memset(onescol_bf[:], 1.0), writes=["ones_col_bf"])


def dma_in(sc, b, dst_ap, src_ap, key_w, keys_r=(), tag=""):
    sc.op("Q", lambda q: q.dma_start(dst_ap, src_ap), reads=keys_r,
          writes=[key_w], dma=True, tag=tag)


def load_convert_bf16(sc, b, name, hbm_ap, rows, cols, stage=None, stage_key=None,
                      dst=None):
    nchunks = rows // 128
    if stage is None:
        stage = b.sbuf(name + "_f32", [128, cols], F32)
        stage_key = f"{name}_f32"
    kfs = [stage_key] if isinstance(stage_key, str) else list(stage_key)
    wbf = dst if dst is not None else b.sbuf(name, [128, nchunks * cols], BF16)
    for c in range(nchunks):
        sc.op("Q", lambda q, c=c: q.dma_start(
            stage[:, 0:cols], hbm_ap[c * 128:(c + 1) * 128, :]),
            reads=kfs, writes=kfs[:1] if len(kfs) == 1 else kfs, dma=True,
            tag=f"ld_{name}_{c}")
        sc.op("V", lambda v, c=c: v.tensor_copy(
            wbf[:, c * cols:(c + 1) * cols], stage[:, 0:cols]),
            reads=kfs, writes=[f"{name}:{c}"], tag=f"cvt_{name}_{c}")
    return wbf


def bcast_bias(sc, b, name, hbm_ap, cols, scale=None):
    row = b.sbuf(name + "_row", [1, cols], F32)
    out = b.sbuf(name + "_bc", [128, cols], F32)
    dma_in(sc, b, row[:], hbm_ap, name + "_row", tag="ld_" + name)
    nch = (cols + 511) // 512
    for c in range(nch):
        lo, hi = c * 512, min(cols, (c + 1) * 512)
        pn = f"P{c % 2}"
        p = b[pn]
        sc.op("T", lambda t, lo=lo, hi=hi, p=p: t.matmul(
            p[:, 0:hi - lo], b["ones_row"][:], row[:, lo:hi], start=True, stop=True),
            reads=["ones_row", name + "_row"], writes=[pn])
        if scale is None:
            sc.op("V", lambda v, lo=lo, hi=hi, p=p: v.tensor_copy(
                out[:, lo:hi], p[:, 0:hi - lo]),
                reads=[pn], writes=[name])
        else:
            sc.op("V", lambda v, lo=lo, hi=hi, p=p: v.tensor_scalar(
                out[:, lo:hi], p[:, 0:hi - lo], float(scale), None, ALU.mult),
                reads=[pn], writes=[name])
    return out


def emit_ln_xnf(sc, b, xin, xn_name, prefix, want_u=None, extra_tile_cb=None):
    nc = b.nc
    xnf = b.t.get(xn_name) or b.sbuf(xn_name, [128, CT * T], BF16)
    xt = [b["SCR"][:, 0:1024], b["SCR"][:, 1024:2048]]
    stat_ = b.t.get(prefix + "_stat") or b.sbuf(prefix + "_stat", [128, 32], F32)
    xnt = [b.t.get(f"{prefix}_xnt{i}") or b.sbuf(f"{prefix}_xnt{i}", [128, D], BF16)
           for i in range(2)]
    ut = (b.t.get(prefix + "_ut") or b.sbuf(prefix + "_ut", [128, D], F32)) \
        if want_u is not None else None
    ptr = [b["PT0"], b["PT1"]]
    for t in range(TT):
        par = t % 2
        kx = f"SCR:{par}"
        kst = prefix + f"_stat{par}"
        kxn = f"{prefix}_xnt{par}"
        stat = stat_[:, par * 16:(par + 1) * 16]
        dma_in(sc, b, xt[par][:], xin.ap()[t * 128:(t + 1) * 128, :], kx,
               keys_r=[kx], tag=f"{prefix}_ldx{t}")
        sc.op("V", lambda v, par=par: v.bn_stats(stat[:, 0:6], xt[par][:, 0:512]),
              reads=[kx], writes=[kst + "a"], tag=f"{prefix}_bn0_{t}")
        sc.op("V", lambda v, par=par: v.bn_stats(stat[:, 6:12], xt[par][:, 512:1024]),
              reads=[kx], writes=[kst + "b"], tag=f"{prefix}_bn1_{t}")
        sc.op("V", lambda v: v.bn_aggr(stat[:, 12:14], stat[:, 0:12]),
              reads=[kst + "a", kst + "b"], writes=[kst + "mv"], tag=f"{prefix}_agg{t}")
        ceps, kceps = b.const_col(sc, 1e-5)
        sc.op("S", lambda s, ceps=ceps: s.activation(
            stat[:, 14:15], stat[:, 13:14], AF.Sqrt, bias=ceps[:]),
            reads=[kst + "mv", kceps], writes=[kst + "std"], tag=f"{prefix}_sqrt{t}")
        sc.op("V", lambda v: v.reciprocal(stat[:, 15:16], stat[:, 14:15]),
              reads=[kst + "std"], writes=[kst + "rstd"], tag=f"{prefix}_rstd{t}")
        if want_u is not None:
            sc.op("V", lambda v, par=par: v.tensor_scalar(
                ut[:], xt[par][:], stat[:, 12:13], stat[:, 15:16],
                ALU.subtract, ALU.mult),
                reads=[kx, kst + "mv", kst + "rstd"], writes=[prefix + "_ut"],
                tag=f"{prefix}_applyu{t}")
            want_u(sc, t, ut, prefix + "_ut")
            sc.op("V", lambda v, par=par: v.tensor_copy(xnt[par][:], ut[:]),
                  reads=[prefix + "_ut"], writes=[kxn], tag=f"{prefix}_cvt{t}")
        else:
            sc.op("V", lambda v, par=par: v.tensor_scalar(
                xnt[par][:], xt[par][:], stat[:, 12:13], stat[:, 15:16],
                ALU.subtract, ALU.mult),
                reads=[kx, kst + "mv", kst + "rstd"], writes=[kxn],
                tag=f"{prefix}_apply{t}")
        for c in range(CT):
            pk = f"PT{(t * CT + c) % 2}"
            sc.op("T", lambda tn, par=par, c=c, pk=pk: tn.transpose(
                b[pk][:], xnt[par][:, c * 128:(c + 1) * 128], b["ident_bf"][:]),
                reads=[kxn, "ident_bf"], writes=[pk], tag=f"{prefix}_tp{t}_{c}")
            sc.op("S", lambda s, t=t, c=c, pk=pk: s.copy(
                xnf[:, c * T + t * 128: c * T + (t + 1) * 128], b[pk][:]),
                reads=[pk], writes=[f"{xn_name}:{c}:{t}"], tag=f"{prefix}_ev{t}_{c}")
        if extra_tile_cb is not None:
            extra_tile_cb(sc, t, xt[par], kx)
    return xnf


def xnf_keys(xn_name, cs=range(CT), ts=range(TT)):
    return [f"{xn_name}:{c}:{t}" for c in cs for t in ts]


def emit_l0_attn(sc, b, W, xn_name, xin, xout, prefix="l0a"):
    nc = b.nc
    xnf = b[xn_name]
    NP = 8
    EPSL0 = 1.1920929e-07

    stage = [b.sbuf(prefix + f"_stage{i}", [128, D], F32) for i in range(2)]
    wp = load_convert_bf16(sc, b, prefix + "_wp", W["w_projT"].ap(), D, D,
                           stage=stage[0], stage_key=prefix + "_stage0")
    bproj = bcast_bias(sc, b, prefix + "_bproj", W["b_proj"].ap(), D)
    aofh = b.dram(prefix + "_aofh", [CT * TT, 128 * 128], BF16)

    wqkv = [b.sbuf(prefix + f"_wqkv{i}", [128, 3 * D], BF16) for i in range(2)]
    qf = b.sbuf(prefix + "_qf", [128, T], BF16)
    ktv = b.sbuf(prefix + "_ktv", [128, 2 * T], BF16)
    c2both = b.sbuf(prefix + "_c2b", [128, 65], BF16)
    aot = [b.sbuf(prefix + f"_aot{i}", [128, T], BF16) for i in range(2)]
    aofst = [b.sbuf(prefix + f"_aofst{i}", [128, 128], BF16) for i in range(2)]
    aofld = [b.sbuf(prefix + f"_aofld{i}", [128, CT * 128], BF16) for i in range(2)]
    recb = b.sbuf(prefix + "_recb", [128, 2], F32)
    xt = [b.sbuf(prefix + f"_xt{i}", [128, D], F32) for i in range(2)]
    yt = [b.sbuf(prefix + f"_yt{i}", [128, D], F32) for i in range(2)]

    pq = [b["P0"], b["P1"]]
    pkv = [b["P2"], b["P3"]]
    pc2b = b["P4"]
    pks = b["P5"]
    pout = [b["P4"], b["P5"]]
    ptr = [b["PT0"], b["PT1"]]

    XNK = xnf_keys(xn_name)

    def load_pair_weights(p):
        par = p % 2
        for wi, wn in enumerate(("wqT", "wkT", "wvT")):
            ks = f"{prefix}_stage{par}"
            src = W[wn].ap()[:, p * 128:(p + 1) * 128].rearrange(
                "(c p) j -> p c j", p=128)
            dma_in(sc, b, stage[par][:].rearrange("p (c j) -> p c j", j=128),
                   src, ks, keys_r=[ks], tag=f"{prefix}_ldw p{p} {wn}")
            sc.op("V", lambda v, par=par, wi=wi: v.tensor_copy(
                wqkv[par][:, wi * D:(wi + 1) * D], stage[par][:]),
                reads=[ks], writes=[f"{prefix}_wqkv{par}:{wi}"],
                tag=f"{prefix}_cvtw p{p} {wi}")

    load_pair_weights(0)
    for p in range(NP):
        par = p % 2
        if p + 1 < NP:
            load_pair_weights(p + 1)
        WQ = [f"{prefix}_wqkv{par}:0"]
        WK = [f"{prefix}_wqkv{par}:1"]
        WV = [f"{prefix}_wqkv{par}:2"]
        for n in range(T // 512):
            kq = f"P{n % 2}"
            def qmm(eng, par=par, n=n):
                pp = pq[n % 2]
                for c in range(CT):
                    ins = eng.matmul(
                        pp[:], wqkv[par][:, 0 * D + c * 128: 0 * D + c * 128 + 128],
                        xnf[:, c * T + n * 512: c * T + (n + 1) * 512],
                        start=(c == 0), stop=(c == CT - 1))
                return ins
            sc.op("T", qmm, reads=WQ + XNK, writes=[kq], tag=f"{prefix}_qmm p{p} n{n}")
            sc.op("S", lambda s, n=n: s.activation(
                qf[:, n * 512:(n + 1) * 512], pq[n % 2][:], AF.Relu),
                reads=[kq], writes=[f"{prefix}_qf:{n}"], tag=f"{prefix}_qev p{p} n{n}")
        for t in range(TT):
            kk = f"P{2 + t % 2}"
            def kmm(eng, par=par, t=t):
                pp = pkv[t % 2]
                for c in range(CT):
                    ins = eng.matmul(
                        pp[:, 0:128], xnf[:, c * T + t * 128: c * T + (t + 1) * 128],
                        wqkv[par][:, 1 * D + c * 128: 1 * D + c * 128 + 128],
                        start=(c == 0), stop=(c == CT - 1))
                return ins
            sc.op("T", kmm, reads=WK + XNK, writes=[kk], tag=f"{prefix}_kmm p{p} t{t}")
            sc.op("S", lambda s, t=t: s.activation(
                ktv[:, t * 128:(t + 1) * 128], pkv[t % 2][:, 0:128], AF.Relu),
                reads=[kk], writes=[f"{prefix}_kt:{t}"], tag=f"{prefix}_kev p{p} t{t}")
            def vmm(eng, par=par, t=t):
                pp = pkv[t % 2]
                for c in range(CT):
                    ins = eng.matmul(
                        pp[:, 0:128], xnf[:, c * T + t * 128: c * T + (t + 1) * 128],
                        wqkv[par][:, 2 * D + c * 128: 2 * D + c * 128 + 128],
                        start=(c == 0), stop=(c == CT - 1))
                return ins
            sc.op("T", vmm, reads=WV + XNK, writes=[kk], tag=f"{prefix}_vmm p{p} t{t}")
            sc.op("S", lambda s, t=t: s.copy(
                ktv[:, T + t * 128: T + (t + 1) * 128], pkv[t % 2][:, 0:128]),
                reads=[kk], writes=[f"{prefix}_vt:{t}"], tag=f"{prefix}_vev p{p} t{t}")
        KT = [f"{prefix}_kt:{t}" for t in range(TT)]
        VT = [f"{prefix}_vt:{t}" for t in range(TT)]
        def ksmm(eng):
            for t in range(TT):
                ins = eng.matmul(pks[:, 0:1], ktv[:, t * 128:(t + 1) * 128],
                                 b["ones_col_bf"][:], start=(t == 0),
                                 stop=(t == TT - 1))
            return ins
        sc.op("T", ksmm, reads=KT + ["ones_col_bf"], writes=["P5"],
              tag=f"{prefix}_ksum p{p}")
        sc.op("V", lambda v: v.tensor_copy(c2both[:, 64:65], pks[:, 0:1]),
              reads=["P5"], writes=[f"{prefix}_c2b:ks"],
              tag=f"{prefix}_ksev p{p}")
        for h in range(2):
            def c2mm(eng, h=h):
                for t in range(TT):
                    ins = eng.matmul(
                        pc2b[h * 64:(h + 1) * 64, 0:64],
                        ktv[:, t * 128 + h * 64: t * 128 + h * 64 + 64],
                        ktv[:, T + t * 128 + h * 64: T + t * 128 + h * 64 + 64],
                        start=(t == 0), stop=(t == TT - 1),
                        tile_position=(0, h * 64))
                return ins
            sc.op("T", c2mm, reads=KT + VT, writes=["P4"],
                  tag=f"{prefix}_c2 p{p} h{h}")
            sc.op("S", lambda s, h=h: s.copy(
                c2both[h * 64:(h + 1) * 64, 0:64], pc2b[h * 64:(h + 1) * 64, 0:64]),
                reads=["P4"], writes=[f"{prefix}_c2b:{h}"],
                tag=f"{prefix}_c2ev p{p} h{h}")
        for t in range(TT):
            for h in range(2):
                pi = (t * 2 + h) % 2
                ko = f"P{4 + pi}"
                sc.op("T", lambda eng, t=t, h=h, pi=pi: eng.matmul(
                    pout[pi][:, 0:65], qf[h * 64:(h + 1) * 64, t * 128:(t + 1) * 128],
                    c2both[h * 64:(h + 1) * 64, :], start=True, stop=True),
                    reads=[f"{prefix}_qf:{t // 4}", f"{prefix}_c2b:{h}",
                           f"{prefix}_c2b:ks"],
                    writes=[ko], tag=f"{prefix}_omm p{p} t{t} h{h}")
                sc.op("V", lambda v, h=h, pi=pi: v.tensor_scalar(
                    recb[:, h:h + 1], pout[pi][:, 64:65], EPSL0, None, ALU.add),
                    reads=[ko], writes=[f"{prefix}_eps:{h}"],
                    tag=f"{prefix}_zeps p{p} t{t} h{h}")
                sc.op("V", lambda v, h=h: v.reciprocal(
                    recb[:, h:h + 1], recb[:, h:h + 1]),
                    reads=[f"{prefix}_eps:{h}"], writes=[f"{prefix}_rec:{h}"],
                    tag=f"{prefix}_zrec p{p} t{t} h{h}")
                sc.op("V", lambda v, t=t, h=h, par=par, pi=pi: v.tensor_scalar(
                    aot[par][:, t * 128 + h * 64: t * 128 + h * 64 + 64],
                    pout[pi][:, 0:64], recb[:, h:h + 1], None, ALU.mult),
                    reads=[ko, f"{prefix}_rec:{h}"],
                    writes=[f"{prefix}_aot{par}:{t}:{h}"],
                    tag=f"{prefix}_oev p{p} t{t} h{h}")
        emit_aot_transpose_store(sc, b, prefix, aofh, aot, par, p, aofst, ptr)

    emit_out_proj_residual(sc, b, prefix, aofh, wp, f"{prefix}_wp", bproj,
                           prefix + "_bproj", xin, xout, xt, yt, aofld, pq)


def emit_out_proj_residual(sc, b, prefix, aofh, wfull, wkeyroot, bias_bcast,
                           bias_key, xin, xout, xt, yt, aofld, pq,
                           yt_keys=None):
    for t in range(TT):
        par = t % 2
        kxt = f"{prefix}_ext{par}"
        dma_in(sc, b, xt[par][:], xin.ap()[t * 128:(t + 1) * 128, :], kxt,
               keys_r=[kxt], tag=f"{prefix}_eldx t{t}")
        kld = f"{prefix}_aofld{par}"
        for c in range(CT):
            sc.op("Q", lambda q, t=t, c=c, par=par: q.dma_start(
                aofld[par][:, c * 128:(c + 1) * 128],
                aofh.ap()[c * TT + t:c * TT + t + 1, :].rearrange(
                    "a (p2 f) -> (a p2) f", p2=128)),
                reads=[f"{prefix}_aofh:{c}:{t}", kld + f":{c}"],
                writes=[kld + f":{c}"], dma=True, tag=f"{prefix}_eldao t{t} c{c}")
        for half in range(2):
            kp = f"P{half}"
            def pmm(eng, par=par, half=half):
                pp = pq[half]
                for c in range(CT):
                    ins = eng.matmul(
                        pp[:], aofld[par][:, c * 128:(c + 1) * 128],
                        wfull[:, c * D + half * 512: c * D + half * 512 + 512],
                        start=(c == 0), stop=(c == CT - 1))
                return ins
            sc.op("T", pmm,
                  reads=[kld + f":{c}" for c in range(CT)] +
                        [f"{wkeyroot}:{c}" for c in range(CT)],
                  writes=[kp], tag=f"{prefix}_pmm t{t} h{half}")
            ytk = yt_keys[par] if yt_keys else f"{prefix}_yt{par}"
            sc.op("V", lambda v, par=par, half=half: v.tensor_tensor(
                yt[par][:, half * 512:(half + 1) * 512], pq[half][:],
                bias_bcast[:, half * 512:(half + 1) * 512], ALU.add),
                reads=[kp, bias_key, ytk], writes=[f"{prefix}_ya{par}:{half}", ytk],
                tag=f"{prefix}_ebias t{t} h{half}")
            sc.op("V", lambda v, par=par, half=half: v.tensor_tensor(
                yt[par][:, half * 512:(half + 1) * 512],
                yt[par][:, half * 512:(half + 1) * 512],
                xt[par][:, half * 512:(half + 1) * 512], ALU.add),
                reads=[f"{prefix}_ya{par}:{half}", kxt],
                writes=[f"{prefix}_yb{par}:{half}", ytk],
                tag=f"{prefix}_eres t{t} h{half}")
        ytk = yt_keys[par] if yt_keys else f"{prefix}_yt{par}"
        sc.op("Q", lambda q, t=t, par=par: q.dma_start(
            xout.ap()[t * 128:(t + 1) * 128, :], yt[par][:, 0:D]),
            reads=[f"{prefix}_yb{par}:0", f"{prefix}_yb{par}:1", ytk],
            writes=[f"xo:{t}", ytk], dma=True, tag=f"{prefix}_esto t{t}")


def emit_aot_transpose_store(sc, b, prefix, aofh, aot, par, cchunk, aofst, ptr,
                             extra_reads=()):
    for t in range(TT):
        pkk = f"PT{t % 2}"
        kst = f"{prefix}_aofst{t % 2}"
        sc.op("T", lambda eng, t=t, par=par, pkk=pkk: eng.transpose(
            b[pkk][:], aot[par][:, t * 128:(t + 1) * 128], b["ident_bf"][:]),
            reads=[f"{prefix}_aot{par}:{t}:0", f"{prefix}_aot{par}:{t}:1",
                   "ident_bf"] + list(extra_reads),
            writes=[pkk], tag=f"{prefix}_aotr c{cchunk} t{t}")
        sc.op("S", lambda s, t=t, pkk=pkk: s.copy(aofst[t % 2][:], b[pkk][:]),
              reads=[pkk], writes=[kst], tag=f"{prefix}_aotrev c{cchunk} t{t}")
        sc.op("Q", lambda q, t=t, cchunk=cchunk: q.dma_start(
            aofh.ap()[cchunk * TT + t:cchunk * TT + t + 1, :].rearrange(
                "a (p2 f) -> (a p2) f", p2=128),
            aofst[t % 2][:]),
            reads=[kst], writes=[f"{prefix}_aofh:{cchunk}:{t}"], dma=True,
            tag=f"{prefix}_aofst c{cchunk} t{t}")


def host_prep(inputs):
    g = {k: np.asarray(v) for k, v in inputs.items()}
    W = {}
    wqkv = g["L0_attn_w_qkv"]
    W["l0a_wqT"] = np.ascontiguousarray(wqkv[0:1024].T)
    W["l0a_wkT"] = np.ascontiguousarray(wqkv[1024:2048].T)
    W["l0a_wvT"] = np.ascontiguousarray(wqkv[2048:3072].T)
    W["l0a_wpT"] = np.ascontiguousarray(g["L0_attn_w_proj"].T)
    W["l0a_bp"] = np.ascontiguousarray(g["L0_attn_b_proj"])
    return W


def _unused_build_l0a(with_input_x=True):
    nc = bass.Bass("TRN2", target_bir_lowering=False)
    es = ExitStack()
    b = Bufs(nc, es)
    sc = Sched()
    xin = b.dram("x_in", [T, D], F32, kind="ExternalInput")
    xout = b.dram("x_out", [T, D], F32, kind="ExternalOutput")
    W = {
        "wqT": b.dram("l0a_wqT", [D, D], F32, kind="ExternalInput"),
        "wkT": b.dram("l0a_wkT", [D, D], F32, kind="ExternalInput"),
        "wvT": b.dram("l0a_wvT", [D, D], F32, kind="ExternalInput"),
        "w_projT": b.dram("l0a_wpT", [D, D], F32, kind="ExternalInput"),
        "b_proj": b.dram("l0a_bp", [D], F32, kind="ExternalInput"),
    }
    with es:
        emit_setup(sc, b)
        emit_ln_xnf(sc, b, xin, "xnf", "ln0")
        emit_l0_attn(sc, b, W, "xnf", xin, xout)
        sc.build(nc, es)
        sc.emit(nc)
    return nc





def emit_l1_attn(sc, b, W, xn_name, xin, xout, prefix="l1t"):
    nc = b.nc
    xnf = b[xn_name]
    NP = 8
    stage = [b.sbuf(prefix + f"_stage{i}", [128, D], F32) for i in range(2)]
    wo = load_convert_bf16(sc, b, prefix + "_wo", W["Wo"].ap(), D, D,
                           stage=stage[0], stage_key=prefix + "_stage0")
    bo = bcast_bias(sc, b, prefix + "_bo", W["bo"].ap(), D)
    aofh = b.dram(prefix + "_aofh", [CT * TT, 128 * 128], BF16)

    wqkv = [b.sbuf(prefix + f"_wqkv{i}", [128, 3 * D], BF16) for i in range(2)]
    qf = b.sbuf(prefix + "_qf", [128, T], BF16)
    kf = b.sbuf(prefix + "_kf", [128, T], BF16)
    vt2 = b.sbuf(prefix + "_vt2", [128, TT * 130], BF16)
    expb = b.sbuf(prefix + "_expb0", [128, 1024], BF16)
    aot = [b.sbuf(prefix + f"_aot{i}", [128, T], BF16) for i in range(2)]
    aofst = [b.sbuf(prefix + f"_aofst{i}", [128, 128], BF16) for i in range(2)]
    aofld = [b.sbuf(prefix + f"_aofld{i}", [128, CT * 128], BF16) for i in range(2)]
    bqk = b.sbuf(prefix + "_bqk", [128, 16], F32)
    bvrow = b.sbuf(prefix + "_bvrow", [1, D], F32)
    bvb = b.sbuf(f"{prefix}_XR:0", [128, 128], F32)
    masks = b.sbuf(prefix + "_masks", [128, 4 * 512], BF16)
    recb = b.sbuf(prefix + "_recb", [128, 4], F32)
    xt = [b.sbuf(prefix + f"_xt{i}", [128, D], F32) for i in range(2)]
    yt = stage

    pq = [b["P0"], b["P1"]]
    pq2 = [b["P0"], b["P1"], b["P2"], b["P3"]]
    pav = [b["P4"], b["P5"]]
    pkv = [b["P2"], b["P3"]]
    pA = [b["P2"], b["P3"], b["P4"], b["P5"]]
    ptr = [b["PT0"], b["PT1"]]
    XNK = xnf_keys(xn_name)

    dma_in(sc, b, bqk[:, 0:8], W["bq_dev"].ap(), prefix + "_bq", tag="ld_bq")
    dma_in(sc, b, bqk[:, 8:16], W["bk_dev"].ap(), prefix + "_bk", tag="ld_bk")
    dma_in(sc, b, bvrow[:], W["bv_row"].ap(), prefix + "_bvrow", tag="ld_bv")
    for jj in range(2):
        sc.op("P", lambda g, jj=jj: g.iota(
            stage[1][:, 0:256], pattern=[[1, 256]],
            base=-128 * jj, channel_multiplier=-1,
            allow_small_or_imprecise_dtypes=True),
            reads=[prefix + "_stage1"], writes=[prefix + "_stage1"],
            tag=f"{prefix}_miota{jj}")
        sc.op("P", lambda g, jj=jj: g.tensor_scalar(
            masks[:, jj * 256:(jj + 1) * 256], stage[1][:, 0:256],
            0.0, None, ALU.is_ge),
            reads=[prefix + "_stage1"], writes=[f"{prefix}_mask:{jj}",
                                                prefix + "_stage1"],
            tag=f"{prefix}_mge{jj}")

    def load_pair_weights(p):
        par = p % 2
        for wi, wn in enumerate(("Wq_big", "Wk_big", "Wv")):
            ks = f"{prefix}_stage{par}"
            src = W[wn].ap()[:, p * 128:(p + 1) * 128].rearrange(
                "(c p) j -> p c j", p=128)
            dma_in(sc, b, stage[par][:].rearrange("p (c j) -> p c j", j=128),
                   src, ks, keys_r=[ks], tag=f"{prefix}_ldw p{p} {wn}")
            sc.op("V", lambda v, par=par, wi=wi: v.tensor_copy(
                wqkv[par][:, wi * D:(wi + 1) * D], stage[par][:]),
                reads=[ks], writes=[f"{prefix}_wqkv{par}:{wi}"],
                tag=f"{prefix}_cvtw p{p} {wi}")

    load_pair_weights(0)
    for p in range(NP):
        par = p % 2
        if p + 1 < NP:
            load_pair_weights(p + 1)
        WQ = [f"{prefix}_wqkv{par}:0"]
        WK = [f"{prefix}_wqkv{par}:1"]
        WV = [f"{prefix}_wqkv{par}:2"]
        sc.op("T", lambda eng, p=p: eng.matmul(
            b["P5"][:, 0:128], b["ones_row"][:], bvrow[:, p * 128:(p + 1) * 128],
            start=True, stop=True),
            reads=["ones_row", prefix + "_bvrow"], writes=["P5"],
            tag=f"{prefix}_bvb p{p}")
        sc.op("V", lambda v: v.tensor_copy(bvb[:], b["P5"][:, 0:128]),
              reads=["P5"], writes=[f"{prefix}_XR:0"], tag=f"{prefix}_bvbev p{p}")
        for n in range(T // 512):
            for wi, (dst, bcol) in enumerate(((qf, 0), (kf, 1))):
                kq = f"P{n % 2}"
                def qkmm(eng, par=par, n=n, wi=wi):
                    pp = pq[n % 2]
                    for c in range(CT):
                        ins = eng.matmul(
                            pp[:], wqkv[par][:, wi * D + c * 128: wi * D + c * 128 + 128],
                            xnf[:, c * T + n * 512: c * T + (n + 1) * 512],
                            start=(c == 0), stop=(c == CT - 1))
                    return ins
                sc.op("T", qkmm, reads=(WQ if wi == 0 else WK) + XNK,
                      writes=[kq], tag=f"{prefix}_qkmm p{p} n{n} w{wi}")
                sc.op("V", lambda v, n=n, wi=wi, dst=dst, bcol=bcol, p=p: v.tensor_scalar(
                    dst[:, n * 512:(n + 1) * 512], pq[n % 2][:],
                    bqk[:, bcol * 8 + p: bcol * 8 + p + 1], None, ALU.add),
                    reads=[kq, prefix + ("_bq" if wi == 0 else "_bk")],
                    writes=[f"{prefix}_{'qf' if wi == 0 else 'kf'}:{n}"],
                    tag=f"{prefix}_qkev p{p} n{n} w{wi}")
        sc.op("V", lambda v: v.memset(
            vt2[:].rearrange("p (t c) -> p t c", c=130)[:, :, 64:65], 1.0),
            writes=[f"{prefix}_vones_a"], tag=f"{prefix}_vones p{p}")
        sc.op("V", lambda v: v.memset(
            vt2[:].rearrange("p (t c) -> p t c", c=130)[:, :, 129:130], 1.0),
            writes=[f"{prefix}_vones_b"], tag=f"{prefix}_vones2 p{p}")
        for t in range(TT):
            kk = f"P{2 + t % 2}"
            def vmm(eng, par=par, t=t):
                pp = pkv[t % 2]
                for c in range(CT):
                    ins = eng.matmul(
                        pp[:, 0:128], xnf[:, c * T + t * 128: c * T + (t + 1) * 128],
                        wqkv[par][:, 2 * D + c * 128: 2 * D + c * 128 + 128],
                        start=(c == 0), stop=(c == CT - 1))
                return ins
            sc.op("T", vmm, reads=WV + XNK, writes=[kk], tag=f"{prefix}_vmm p{p} t{t}")
            for h in range(2):
                sc.op("V", lambda v, t=t, h=h, pi=t % 2: v.tensor_tensor(
                    vt2[:, t * 130 + h * 65: t * 130 + h * 65 + 64],
                    pkv[pi][:, h * 64:(h + 1) * 64], bvb[:, h * 64:(h + 1) * 64],
                    ALU.add),
                    reads=[kk, f"{prefix}_XR:0", f"{prefix}_vones_a"],
                    writes=[f"{prefix}_vt:{t}:{h}"], tag=f"{prefix}_vev p{p} t{t} h{h}")
        for h in range(2):
            base = 64 * h
            QK = [f"{prefix}_qf:{n}" for n in range(8)]
            KK = [f"{prefix}_kf:{n}" for n in range(8)]
            for qc in range(16):
                nk = 2 * qc + 2

                def emit_sc(kc, base=base, qc=qc):
                    pi = kc % 4
                    ks = f"P{pi}"
                    ke = f"{prefix}_expb:{pi}"
                    sc.op("T", lambda eng, base=base, kc=kc, qc=qc, pi=pi: eng.matmul(
                        pq2[pi][:, 0:256], kf[base:base + 64, kc * 128:(kc + 1) * 128],
                        qf[base:base + 64, qc * 256:(qc + 1) * 256],
                        start=True, stop=True),
                        reads=QK + KK, writes=[ks],
                        tag=f"{prefix}_smm q{qc} k{kc}")
                    sc.op("S", lambda s, pi=pi: s.activation(
                        expb[:, pi * 256:(pi + 1) * 256], pq2[pi][:, 0:256],
                        AF.Exp, scale=0.125),
                        reads=[ks], writes=[ke], tag=f"{prefix}_sexp q{qc} k{kc}")
                    if kc >= 2 * qc:
                        jj = kc - 2 * qc
                        sc.op("V", lambda v, pi=pi, jj=jj: v.tensor_tensor(
                            expb[:, pi * 256:(pi + 1) * 256],
                            expb[:, pi * 256:(pi + 1) * 256],
                            masks[:, jj * 256:(jj + 1) * 256], ALU.mult),
                            reads=[ke, f"{prefix}_mask:{jj}"], writes=[ke],
                            tag=f"{prefix}_smask q{qc} k{kc}")

                def emit_av(kc, h=h, qc=qc, nk=nk):
                    pi = kc % 4
                    ke = f"{prefix}_expb:{pi}"
                    def avmm(eng, pi=pi, kc=kc, h=h, nk=nk):
                        for qs in range(2):
                            ins = eng.matmul(
                                pav[qs][:, 0:65],
                                expb[:, pi * 256 + qs * 128: pi * 256 + qs * 128 + 128],
                                vt2[:, kc * 130 + h * 65: kc * 130 + h * 65 + 65],
                                start=(kc == 0), stop=(kc == nk - 1))
                        return ins
                    sc.op("T", avmm,
                          reads=[ke, f"{prefix}_vt:{kc}:{h}"],
                          writes=["P4", "P5"],
                          tag=f"{prefix}_av q{qc} k{kc}")

                for kc in range(nk):
                    emit_sc(kc)
                    if kc >= 3:
                        emit_av(kc - 3)
                for kc in range(max(0, nk - 3), nk):
                    emit_av(kc)
                for qs in range(2):
                    tglob = qc * 2 + qs
                    kA = f"P{4 + qs}"
                    sc.op("V", lambda v, qs=qs: v.reciprocal(
                        recb[:, qs:qs + 1], pav[qs][:, 64:65]),
                        reads=[kA], writes=[f"{prefix}_rec:{qs}"],
                        tag=f"{prefix}_arec p{p} h{h} q{qc} s{qs}")
                    sc.op("V", lambda v, qs=qs, tglob=tglob, h=h, par=par: v.tensor_scalar(
                        aot[par][:, tglob * 128 + h * 64: tglob * 128 + h * 64 + 64],
                        pav[qs][:, 0:64], recb[:, qs:qs + 1], None, ALU.mult),
                        reads=[kA, f"{prefix}_rec:{qs}"],
                        writes=[f"{prefix}_aot{par}:{tglob}:{h}"],
                        tag=f"{prefix}_aev p{p} h{h} q{qc} s{qs}")
        emit_aot_transpose_store(sc, b, prefix, aofh, aot, par, p, aofst, ptr)

    emit_out_proj_residual(sc, b, prefix, aofh, wo, f"{prefix}_wo", bo,
                           prefix + "_bo", xin, xout, xt, yt, aofld, pq,
                           yt_keys=[prefix + "_stage0", prefix + "_stage1"])


def emit_l0_ffn(sc, b, W, xn_name, xin, xout, prefix, w1bf=None, w2bf=None,
                convert_moe=True):
    nc = b.nc
    xnf = b[xn_name]
    SCL = 1.0 / 5.656854249492381

    AR1 = b.sbuf(prefix + "_AR1", [128, 16384], BF16)
    AR2 = b.sbuf(prefix + "_AR2", [128, 16384], BF16)
    stage0 = b["SCR"]
    stage = [stage0[:, 0:1024], stage0[:, 1024:2048]]
    q4 = AR1[:, 0:4096]
    k4 = AR1[:, 4096:8192]
    vt4 = AR1[:, 8192:12416]
    expb = AR1[:, 12416:13440]
    aot = [AR2[:, 0:4096], AR2[:, 4096:8192]]
    wqkv = [AR2[:, 8192:11264], AR2[:, 11264:14336]]
    ow = AR2[:, 8192:12288]
    aof = b.sbuf(prefix + "_aof", [128, 16384], BF16)
    WA = AR1
    WB = AR2
    rw = b.sbuf(prefix + "_rwS", [128, 64], BF16)
    qkb = b.sbuf(prefix + "_qkb", [128, 8], F32)
    obd = b.sbuf(prefix + "_obd", [128, 8], F32)
    b1d = b.sbuf(prefix + "_b1d", [128, 128], F32)
    carena = b.sbuf(prefix + "_car", [128, 8 * TT], F32)
    rt = b.sbuf(prefix + "_rt", [128, 80], F32)
    lnq = b.sbuf(prefix + "_lnq", [128, 16], F32)
    recb = b.sbuf(prefix + "_recb", [128, 4], F32)
    XR = b.sbuf(prefix + "_XR", [128, 8 * D], F32)
    vbrow = XR[0:1, 0:512]
    rbrow = XR[0:1, 512:520]
    rbb = XR[:, 640:648]
    bvb = XR[:, 768:896]

    pq = [b["P0"], b["P1"]]
    pq2 = [b["P0"], b["P1"], b["P2"], b["P3"]]
    pav = [b["P4"], b["P5"]]
    pA = [b["P2"], b["P3"], b["P4"], b["P5"]]
    XNK = xnf_keys(xn_name)

    G1 = prefix + "_AR1g"
    G2 = prefix + "_AR2g"
    _g1pref = (f"{prefix}_q4", f"{prefix}_k4", f"{prefix}_vt", f"{prefix}_expb",
               f"{prefix}_WA", f"{prefix}_vones")
    _g2pref = (f"{prefix}_aot", f"{prefix}_wqkv", f"{prefix}_ow:", f"{prefix}_WB")

    _orig_op = sc.op

    def _op(stream, fn, reads=(), writes=(), **kw):
        reads = list(reads)
        ks = list(reads) + list(writes)
        if any(k.startswith(_g1pref) for k in ks):
            reads.append(G1)
        if any(k.startswith(_g2pref) for k in ks):
            reads.append(G2)
        return _orig_op(stream, fn, reads=reads, writes=writes, **kw)

    class _SC:
        op = staticmethod(_op)
    sc2 = sc
    sc = _SC()

    def guard(gkey):
        _orig_op("V", lambda v: v.engine_nop(), reads=[], writes=[gkey],
                 tag="guard_" + gkey)

    cvb = [AR1[:, 13440:15488], AR2[:, 14336:16384]]
    _g1pref = _g1pref + (f"{prefix}_cvA",)
    _g2pref = _g2pref + (f"{prefix}_cvB",)
    CVK = [f"{prefix}_cvA", f"{prefix}_cvB"]

    def preconvert_moe():
        idx = 0
        for which, hbm_f32, hbm_bf, nch, cw in ((1, W["w1"], w1bf, 8, 2048),
                                                (2, W["w2"], w2bf, 16, 1024)):
            an = prefix + ("_WA" if which == 1 else "_WB")
            for e in range(8):
                for c in range(nch):
                    par = idx % 2
                    idx += 1
                    slots = [f"{prefix}_XR:{1 + 2 * par}", f"{prefix}_XR:{2 + 2 * par}"]
                    stg = XR[:, (1 + 2 * par) * 1024: (1 + 2 * par) * 1024 + cw]
                    sc.op("P", lambda q, stg=stg, hbm_f32=hbm_f32, e=e, c=c: q.dma_start(
                        stg, hbm_f32.ap()[e, c * 128:(c + 1) * 128, :]),
                        reads=slots, writes=slots, dma=True, dma_group="pc",
                        tag=f"{prefix}_pcl{which} e{e} c{c}")
                    sc.op("P", lambda v, stg=stg, par=par, cw=cw: v.tensor_copy(
                        cvb[par][:, 0:cw], stg),
                        reads=slots, writes=[CVK[par]],
                        tag=f"{prefix}_pcc{which} e{e} c{c}")
                    sc.op("P", lambda q, par=par, cw=cw, hbm_bf=hbm_bf, e=e, c=c: q.dma_start(
                        hbm_bf.ap()[e, c * 128:(c + 1) * 128, :], cvb[par][:, 0:cw]),
                        reads=[CVK[par]], writes=[f"{an}hbm:{e}:{c}", CVK[par]],
                        dma=True, dma_group="pc",
                        tag=f"{prefix}_pcs{which} e{e} c{c}")

    if convert_moe:
        preconvert_moe()

    dma_in(sc, b, qkb[:, 0:4], W["qb_dev"].ap(), prefix + "_qb", tag="ld_qb")
    dma_in(sc, b, qkb[:, 4:8], W["kb_dev"].ap(), prefix + "_kb", tag="ld_kb")
    dma_in(sc, b, obd[:], W["ob_dev"].ap(), prefix + "_obd", tag="ld_ob")
    dma_in(sc, b, b1d[:], W["b1_dev"].ap(), prefix + "_b1d", tag="ld_b1")
    dma_in(sc, b, vbrow[:], W["vb_row"].ap(), f"{prefix}_XR:0", tag="ld_vb")
    dma_in(sc, b, rbrow[:], W["rb_row"].ap(), f"{prefix}_XR:0", tag="ld_rb")
    sc.op("T", lambda t: t.matmul(b["P5"][:, 0:8], b["ones_row"][:], rbrow[:],
                                  start=True, stop=True),
          reads=["ones_row", f"{prefix}_XR:0"], writes=["P5"], tag="rbb_mm")
    sc.op("V", lambda v: v.tensor_copy(rbb[:], b["P5"][:, 0:8]),
          reads=["P5"], writes=[f"{prefix}_XR:0"], tag="rbb_ev")


    def load_group_weights(g):
        par = g % 2
        for wi, wn in enumerate(("qw", "kw", "vw")):
            ks = f"SCR:{par}"
            srcap = W[wn].ap()[:, g * 128:(g + 1) * 128].rearrange(
                "(c p) j -> p c j", p=128)
            dma_in(sc, b, stage[par].rearrange("p (c j) -> p c j", j=128),
                   srcap, ks, keys_r=[ks], tag=f"{prefix}_ldw g{g} {wn}")
            sc.op("V", lambda v, par=par, wi=wi: v.tensor_copy(
                wqkv[par][:, wi * 1024:(wi + 1) * 1024], stage[par]),
                reads=[ks], writes=[f"{prefix}_wqkv{par}:{wi}"],
                tag=f"{prefix}_cvtw g{g} {wi}")

    load_group_weights(0)
    for g in range(4):
        par = g % 2
        if g + 1 < 4:
            load_group_weights(g + 1)
        WQ = [f"{prefix}_wqkv{par}:0"]
        WK = [f"{prefix}_wqkv{par}:1"]
        WV = [f"{prefix}_wqkv{par}:2"]
        sc.op("T", lambda eng, g=g: eng.matmul(
            b["P5"][:, 0:128], b["ones_row"][:], vbrow[:, g * 128:(g + 1) * 128],
            start=True, stop=True),
            reads=["ones_row", f"{prefix}_XR:0"], writes=["P5"],
            tag=f"{prefix}_bvb g{g}")
        sc.op("V", lambda v: v.tensor_copy(bvb[:], b["P5"][:, 0:128]),
              reads=["P5"], writes=[f"{prefix}_XR:0"], tag=f"{prefix}_bvbev g{g}")
        for wi, (dst, bc) in enumerate(((q4, 0), (k4, 1))):
            for n in range(8):
                kq = f"P{n % 2}"
                def qmm(eng, par=par, n=n, wi=wi):
                    pp = pq[n % 2]
                    for c in range(CT):
                        ins = eng.matmul(
                            pp[:], wqkv[par][:, wi * 1024 + c * 128: wi * 1024 + c * 128 + 128],
                            xnf[:, c * T + n * 512: c * T + (n + 1) * 512],
                            start=(c == 0), stop=(c == CT - 1))
                    return ins
                sc.op("T", qmm, reads=(WQ if wi == 0 else WK) + XNK, writes=[kq],
                      tag=f"{prefix}_qkmm g{g} w{wi} n{n}")
                sc.op("V", lambda v, n=n, wi=wi, dst=dst, bc=bc, g=g: v.tensor_scalar(
                    dst[:, n * 512:(n + 1) * 512], pq[n % 2][:],
                    qkb[:, bc * 4 + g: bc * 4 + g + 1], None, ALU.add),
                    reads=[kq, prefix + ("_qb" if wi == 0 else "_kb")],
                    writes=[f"{prefix}_{'q4' if wi == 0 else 'k4'}:{n}"],
                    tag=f"{prefix}_qkev g{g} w{wi} n{n}")
        sc.op("V", lambda v: v.memset(
            vt4[:].rearrange("p (t h w) -> p t h w", h=4, w=33)[:, :, :, 32:33], 1.0),
            writes=[f"{prefix}_vones"], tag=f"{prefix}_vones g{g}")
        for t in range(TT):
            kk = f"P{2 + t % 2}"
            def vmm(eng, par=par, t=t):
                pp = pA[t % 2]
                for c in range(CT):
                    ins = eng.matmul(
                        pp[:, 0:128], xnf[:, c * T + t * 128: c * T + (t + 1) * 128],
                        wqkv[par][:, 2 * 1024 + c * 128: 2 * 1024 + c * 128 + 128],
                        start=(c == 0), stop=(c == CT - 1))
                return ins
            sc.op("T", vmm, reads=WV + XNK, writes=[kk], tag=f"{prefix}_vmm g{g} t{t}")
            for hh in range(4):
                sc.op("V", lambda v, t=t, hh=hh, pi=t % 2: v.tensor_tensor(
                    vt4[:, t * 132 + hh * 33: t * 132 + hh * 33 + 32],
                    pA[pi][:, hh * 32:(hh + 1) * 32], bvb[:, hh * 32:(hh + 1) * 32],
                    ALU.add),
                    reads=[kk, f"{prefix}_XR:0", f"{prefix}_vones"],
                    writes=[f"{prefix}_vt:{t}:{hh}"], tag=f"{prefix}_vev g{g} t{t} h{hh}")
        QK = [f"{prefix}_q4:{n}" for n in range(8)]
        KK = [f"{prefix}_k4:{n}" for n in range(8)]
        for hh in range(4):
            base = 32 * hh
            for qc in range(16):
                def emit_sc(kc, base=base, qc=qc):
                    pi = kc % 4
                    ks = f"P{pi}"
                    ke = f"{prefix}_expb:{pi}"
                    sc.op("T", lambda eng, base=base, kc=kc, qc=qc, pi=pi: eng.matmul(
                        pq2[pi][:, 0:256], k4[base:base + 32, kc * 128:(kc + 1) * 128],
                        q4[base:base + 32, qc * 256:(qc + 1) * 256],
                        start=True, stop=True, tile_position=(base, 0)),
                        reads=QK + KK, writes=[ks],
                        tag=f"{prefix}_smm q{qc} k{kc}")
                    sc.op("S", lambda s, pi=pi: s.activation(
                        expb[:, pi * 256:(pi + 1) * 256],
                        pq2[pi][:, 0:256], AF.Exp, scale=SCL),
                        reads=[ks], writes=[ke],
                        tag=f"{prefix}_sexp q{qc} k{kc}")

                def emit_av(kc, hh=hh, qc=qc):
                    pi = kc % 4
                    ke = f"{prefix}_expb:{pi}"
                    def avmm(eng, kc=kc, hh=hh, pi=pi):
                        for qs in range(2):
                            ins = eng.matmul(
                                pav[qs][:, 0:33],
                                expb[:, pi * 256 + qs * 128: pi * 256 + qs * 128 + 128],
                                vt4[:, kc * 132 + hh * 33: kc * 132 + hh * 33 + 33],
                                start=(kc == 0), stop=(kc == TT - 1))
                        return ins
                    sc.op("T", avmm,
                          reads=[ke, f"{prefix}_vt:{kc}:{hh}"],
                          writes=["P4", "P5"],
                          tag=f"{prefix}_av q{qc} k{kc}")

                for kc in range(TT):
                    emit_sc(kc)
                    if kc >= 3:
                        emit_av(kc - 3)
                for kc in range(TT - 3, TT):
                    emit_av(kc)
                for qs in range(2):
                    tglob = qc * 2 + qs
                    kA = f"P{4 + qs}"
                    sc.op("V", lambda v, qs=qs: v.reciprocal(
                        recb[:, qs:qs + 1], pav[qs][:, 32:33]),
                        reads=[kA], writes=[f"{prefix}_rec:{qs}"],
                        tag=f"{prefix}_arec g{g} h{hh} q{qc} s{qs}")
                    sc.op("V", lambda v, qs=qs, tglob=tglob, hh=hh, par=par: v.tensor_scalar(
                        aot[par][:, tglob * 128 + hh * 32: tglob * 128 + hh * 32 + 32],
                        pav[qs][:, 0:32],
                        recb[:, qs:qs + 1], None, ALU.mult),
                        reads=[kA, f"{prefix}_rec:{qs}"],
                        writes=[f"{prefix}_aot{par}:{tglob}:{hh}"],
                        tag=f"{prefix}_aev g{g} h{hh} q{qc} s{qs}")
        for t in range(TT):
            pkk = f"PT{t % 2}"
            sc.op("T", lambda eng, t=t, par=par, pkk=pkk: eng.transpose(
                b[pkk][:], aot[par][:, t * 128:(t + 1) * 128], b["ident_bf"][:]),
                reads=[f"{prefix}_aot{par}:{t}:{hh2}" for hh2 in range(4)] + ["ident_bf"],
                writes=[pkk], tag=f"{prefix}_aotr g{g} t{t}")
            sc.op("S", lambda s, t=t, g=g, pkk=pkk: s.copy(
                aof[:, g * 4096 + t * 128: g * 4096 + (t + 1) * 128], b[pkk][:]),
                reads=[pkk], writes=[f"{prefix}_aofA:{g}:{t}"],
                tag=f"{prefix}_aotrev g{g} t{t}")

    AOK = [f"{prefix}_aofA:{g}:{t}" for g in range(4) for t in range(TT)]
    OWK = [f"{prefix}_ow:{c}" for c in range(4)]
    guard(G2)
    load_convert_bf16(sc, b, prefix + "_ow", W["ow"].ap(), 512, D,
                      stage=stage[0], stage_key="SCR:0", dst=ow)
    load_convert_bf16(sc, b, prefix + "_rw", W["rw"].ap(), D, 8,
                      stage=stage[0], stage_key="SCR:0", dst=rw)
    for c in range(CT):
        for n in range(8):
            kq = f"P{n % 2}"
            def pmm(eng, c=c, n=n):
                pp = pq[n % 2]
                for a in range(4):
                    ins = eng.matmul(
                        pp[:], ow[:, a * 1024 + c * 128: a * 1024 + c * 128 + 128],
                        aof[:, a * 4096 + n * 512: a * 4096 + (n + 1) * 512],
                        start=(a == 0), stop=(a == 3))
                return ins
            sc.op("T", pmm, reads=AOK + OWK, writes=[kq],
                  tag=f"{prefix}_mlamm c{c} n{n}")
            sc.op("V", lambda v, c=c, n=n: v.tensor_scalar(
                xnf[:, c * T + n * 512: c * T + (n + 1) * 512], pq[n % 2][:],
                obd[:, c:c + 1], None, ALU.add),
                reads=[kq, prefix + "_obd"],
                writes=[f"{xn_name}:{c}:{4 * n + i}" for i in range(4)],
                tag=f"{prefix}_mlaev c{c} n{n}")

    RWK = [f"{prefix}_rw:{c}" for c in range(CT)]
    for t in range(TT):
        kr = f"P{4 + t % 2}"
        def rmm(eng, t=t):
            pp = pA[2 + t % 2]
            for c in range(CT):
                ins = eng.matmul(
                    pp[:, 0:8], xnf[:, c * T + t * 128: c * T + (t + 1) * 128],
                    rw[:, c * 8:(c + 1) * 8], start=(c == 0), stop=(c == CT - 1))
            return ins
        sc.op("T", rmm, reads=RWK + xnf_keys(xn_name, ts=[t]), writes=[kr],
              tag=f"{prefix}_rmm t{t}")
        RT = prefix + "_rt"
        sc.op("V", lambda v, t=t: v.tensor_tensor(
            rt[:, 0:8], pA[2 + t % 2][:, 0:8], rbb[:], ALU.add),
            reads=[kr, f"{prefix}_XR:0"], writes=[RT + ":l"], tag=f"{prefix}_rb t{t}")
        sc.op("S", lambda s: s.activation(rt[:, 8:16], rt[:, 0:8], AF.Exp,
                                          accum_out=rt[:, 16:17]),
              reads=[RT + ":l"], writes=[RT + ":e", RT + ":s"], tag=f"{prefix}_rexp t{t}")
        sc.op("V", lambda v: v.reciprocal(rt[:, 17:18], rt[:, 16:17]),
              reads=[RT + ":s"], writes=[RT + ":ri"], tag=f"{prefix}_rrec t{t}")
        sc.op("V", lambda v: v.tensor_scalar(rt[:, 24:32], rt[:, 8:16],
                                             rt[:, 17:18], None, ALU.mult),
              reads=[RT + ":e", RT + ":ri"], writes=[RT + ":p"], tag=f"{prefix}_rp t{t}")
        sc.op("V", lambda v: v.max(out=rt[:, 32:40], in_=rt[:, 24:32]),
              reads=[RT + ":p"], writes=[RT + ":m8"], tag=f"{prefix}_rmax t{t}")
        sc.op("V", lambda v: v.tensor_scalar(rt[:, 40:48], rt[:, 24:32],
                                             rt[:, 32:33], None, ALU.is_equal),
              reads=[RT + ":p", RT + ":m8"], writes=[RT + ":e1"], tag=f"{prefix}_req1 t{t}")
        sc.op("V", lambda v: v.tensor_tensor_scan(rt[:, 48:56], rt[:, 40:48],
                                                  rt[:, 40:48], 0.0, ALU.add, ALU.bypass),
              reads=[RT + ":e1"], writes=[RT + ":c1"], tag=f"{prefix}_rc1 t{t}")
        sc.op("V", lambda v: v.tensor_scalar(rt[:, 56:64], rt[:, 48:56], 1.0,
                                             None, ALU.is_equal),
              reads=[RT + ":c1"], writes=[RT + ":t1"], tag=f"{prefix}_rt1 t{t}")
        sc.op("V", lambda v: v.tensor_tensor(rt[:, 40:48], rt[:, 40:48],
                                             rt[:, 56:64], ALU.mult),
              reads=[RT + ":e1", RT + ":t1"], writes=[RT + ":x1"], tag=f"{prefix}_rx1 t{t}")
        sc.op("V", lambda v: v.tensor_tensor(rt[:, 18:19], rt[:, 32:33],
                                             rt[:, 33:34], ALU.is_equal),
              reads=[RT + ":m8"], writes=[RT + ":em"], tag=f"{prefix}_rem t{t}")
        sc.op("V", lambda v: v.tensor_scalar(rt[:, 19:20], rt[:, 18:19], 1.0,
                                             None, ALU.add),
              reads=[RT + ":em"], writes=[RT + ":th"], tag=f"{prefix}_rth t{t}")
        sc.op("V", lambda v: v.tensor_scalar(rt[:, 64:72], rt[:, 24:32],
                                             rt[:, 33:34], None, ALU.is_equal),
              reads=[RT + ":p", RT + ":m8"], writes=[RT + ":e2"], tag=f"{prefix}_req2 t{t}")
        sc.op("V", lambda v: v.tensor_tensor_scan(rt[:, 72:80], rt[:, 64:72],
                                                  rt[:, 64:72], 0.0, ALU.add, ALU.bypass),
              reads=[RT + ":e2"], writes=[RT + ":c2"], tag=f"{prefix}_rc2 t{t}")
        sc.op("V", lambda v: v.tensor_scalar(rt[:, 72:80], rt[:, 72:80],
                                             rt[:, 19:20], None, ALU.is_equal),
              reads=[RT + ":c2", RT + ":th"], writes=[RT + ":t2"], tag=f"{prefix}_rt2 t{t}")
        sc.op("V", lambda v: v.tensor_tensor(rt[:, 64:72], rt[:, 64:72],
                                             rt[:, 72:80], ALU.mult),
              reads=[RT + ":e2", RT + ":t2"], writes=[RT + ":x2"], tag=f"{prefix}_rx2 t{t}")
        sc.op("V", lambda v: v.tensor_tensor(rt[:, 20:21], rt[:, 32:33],
                                             rt[:, 33:34], ALU.add),
              reads=[RT + ":m8"], writes=[RT + ":ws"], tag=f"{prefix}_rws t{t}")
        sc.op("V", lambda v: v.tensor_scalar(rt[:, 20:21], rt[:, 20:21], 1e-8,
                                             None, ALU.add),
              reads=[RT + ":ws"], writes=[RT + ":ws2"], tag=f"{prefix}_rws2 t{t}")
        sc.op("V", lambda v: v.reciprocal(rt[:, 21:22], rt[:, 20:21]),
              reads=[RT + ":ws2"], writes=[RT + ":rw"], tag=f"{prefix}_rrw t{t}")
        sc.op("V", lambda v: v.tensor_scalar(rt[:, 22:23], rt[:, 32:33],
                                             rt[:, 21:22], 0.15, ALU.mult, ALU.mult),
              reads=[RT + ":m8", RT + ":rw"], writes=[RT + ":w1"], tag=f"{prefix}_rw1 t{t}")
        sc.op("V", lambda v: v.tensor_scalar(rt[:, 23:24], rt[:, 33:34],
                                             rt[:, 21:22], 0.15, ALU.mult, ALU.mult),
              reads=[RT + ":m8", RT + ":rw"], writes=[RT + ":w2"], tag=f"{prefix}_rw2 t{t}")
        sc.op("V", lambda v, t=t: v.tensor_scalar(rt[:, 40:48], rt[:, 40:48],
                                                  rt[:, 22:23], None, ALU.mult),
              reads=[RT + ":x1", RT + ":w1"], writes=[RT + ":y1"], tag=f"{prefix}_ry1 t{t}")
        sc.op("V", lambda v, t=t: v.tensor_scalar(rt[:, 64:72], rt[:, 64:72],
                                                  rt[:, 23:24], None, ALU.mult),
              reads=[RT + ":x2", RT + ":w2"], writes=[RT + ":y2"], tag=f"{prefix}_ry2 t{t}")
        sc.op("V", lambda v, t=t: v.tensor_tensor(carena[:, t * 8:(t + 1) * 8],
                                                  rt[:, 40:48], rt[:, 64:72], ALU.add),
              reads=[RT + ":y1", RT + ":y2"], writes=[f"{prefix}_car:{t}"],
              tag=f"{prefix}_rc t{t}")

    def load_w12(which, e, create):
        arena, hbm_bf, hbm_f32, nch, cw = (
            (WA, w1bf, W["w1"], 8, 2048) if which == 1 else
            (WB, w2bf, W["w2"], 16, 1024))
        an = prefix + ("_WA" if which == 1 else "_WB")
        for c in range(nch):
            ak = f"{an}:{c}"
            if create:
                kss = ["SCR:0", "SCR:1"] if cw == 2048 else [f"SCR:{c % 2}"]
                stg = stage0[:, 0:cw] if cw == 2048 else stage[c % 2]
                sc.op("Q", lambda q, c=c, stg=stg, e=e: q.dma_start(
                    stg, hbm_f32.ap()[e, c * 128:(c + 1) * 128, :]),
                    reads=kss, writes=kss, dma=True,
                    tag=f"{prefix}_ldwf{which} e{e} c{c}")
                sc.op("V", lambda v, c=c, cw=cw, arena=arena, stg=stg: v.tensor_copy(
                    arena[:, c * cw:(c + 1) * cw], stg),
                    reads=kss, writes=[ak], tag=f"{prefix}_cvw{which} e{e} c{c}")
                sc.op("Q", lambda q, c=c, cw=cw, arena=arena, e=e: q.dma_start(
                    hbm_bf.ap()[e, c * 128:(c + 1) * 128, :],
                    arena[:, c * cw:(c + 1) * cw]),
                    reads=[ak], writes=[f"{an}hbm:{e}:{c}"], dma=True,
                    tag=f"{prefix}_stw{which} e{e} c{c}")
            else:
                dma_in(sc, b, arena[:, c * cw:(c + 1) * cw],
                       hbm_bf.ap()[e, c * 128:(c + 1) * 128, :], ak,
                       keys_r=[ak, f"{an}hbm:{e}:{c}"],
                       tag=f"{prefix}_ldwb{which} e{e} c{c}")

    WAK = [prefix + f"_WA:{c}" for c in range(8)]
    WBK = [prefix + f"_WB:{c}" for c in range(16)]
    for qtr in range(4):
        create = False
        ceps2, kceps2 = b.const_col(sc, 1e-5)
        for i in range(8):
            tg = qtr * 8 + i
            kx = f"{prefix}_XR:{i}"
            klq = prefix + "_lnq"
            dma_in(sc, b, XR[:, i * D:(i + 1) * D],
                   xin.ap()[tg * 128:(tg + 1) * 128, :], kx,
                   keys_r=[kx], tag=f"{prefix}_ldXR q{qtr} i{i}")
            sc.op("V", lambda v, i=i: v.bn_stats(lnq[:, 0:6], XR[:, i * D:i * D + 512]),
                  reads=[kx], writes=[klq + "a"], tag=f"{prefix}_qbn0 q{qtr} i{i}")
            sc.op("V", lambda v, i=i: v.bn_stats(lnq[:, 6:12],
                                                 XR[:, i * D + 512:i * D + 1024]),
                  reads=[kx], writes=[klq + "b"], tag=f"{prefix}_qbn1 q{qtr} i{i}")
            sc.op("V", lambda v: v.bn_aggr(lnq[:, 12:14], lnq[:, 0:12]),
                  reads=[klq + "a", klq + "b"], writes=[klq + "mv"],
                  tag=f"{prefix}_qagg q{qtr} i{i}")
            sc.op("S", lambda s: s.activation(lnq[:, 14:15], lnq[:, 13:14],
                                              AF.Sqrt, bias=ceps2[:]),
                  reads=[klq + "mv", kceps2], writes=[klq + "std"],
                  tag=f"{prefix}_qsqrt q{qtr} i{i}")
            sc.op("V", lambda v: v.reciprocal(lnq[:, 15:16], lnq[:, 14:15]),
                  reads=[klq + "std"], writes=[klq + "rs"],
                  tag=f"{prefix}_qrec q{qtr} i{i}")
            sc.op("V", lambda v: v.tensor_scalar(lnq[:, 15:16], lnq[:, 15:16],
                                                 0.5, None, ALU.mult),
                  reads=[klq + "rs"], writes=[klq + "rh"],
                  tag=f"{prefix}_qhalf q{qtr} i{i}")
            sc.op("V", lambda v, i=i: v.tensor_scalar(
                stage0[:, 0:1024], XR[:, i * D:(i + 1) * D], lnq[:, 12:13],
                lnq[:, 15:16], ALU.subtract, ALU.mult),
                reads=[kx, klq + "mv", klq + "rh", "SCR:0", "SCR:1"],
                writes=["SCR:0", "SCR:1"], tag=f"{prefix}_qln q{qtr} i{i}")
            sc.op("V", lambda v, i=i: v.tensor_tensor(
                XR[:, i * D:(i + 1) * D], XR[:, i * D:(i + 1) * D],
                stage0[:, 0:1024], ALU.add),
                reads=[kx, "SCR:0", "SCR:1"], writes=[kx],
                tag=f"{prefix}_qadd q{qtr} i{i}")
        if qtr == 0:
            guard(G1)
            guard(G2)
            load_w12(1, 0, create)
            load_w12(2, 0, create)
        for e in range(8):
            for tc in range(2):
                for fc in range(16):
                    kq = f"P{fc % 2}"
                    def h1mm(eng, qtr=qtr, tc=tc, fc=fc):
                        pp = pq[fc % 2]
                        for c in range(CT):
                            ins = eng.matmul(
                                pp[:], WA[:, c * 2048 + fc * 128: c * 2048 + fc * 128 + 128],
                                xnf[:, c * T + qtr * 1024 + tc * 512:
                                    c * T + qtr * 1024 + (tc + 1) * 512],
                                start=(c == 0), stop=(c == CT - 1))
                        return ins
                    sc.op("T", h1mm,
                          reads=WAK + xnf_keys(xn_name, ts=range(qtr * 8 + tc * 4,
                                                                 qtr * 8 + tc * 4 + 4)),
                          writes=[kq], tag=f"{prefix}_h1 q{qtr} e{e} tc{tc} f{fc}")
                    sc.op("S", lambda s, tc=tc, fc=fc, e=e: s.activation(
                        aof[:, tc * 8192 + fc * 512: tc * 8192 + (fc + 1) * 512],
                        pq[fc % 2][:], AF.Gelu, bias=b1d[:, e * 16 + fc: e * 16 + fc + 1]),
                        reads=[kq, prefix + "_b1d"],
                        writes=[f"{prefix}_h1g:{tc}:{fc}"],
                        tag=f"{prefix}_h1g q{qtr} e{e} tc{tc} f{fc}")
            if e + 1 < 8:
                load_w12(1, e + 1, create)
            elif qtr + 1 < 4:
                load_w12(1, 0, convert_moe and False)
            for tc in range(2):
                H1K = [f"{prefix}_h1g:{tc}:{fc}" for fc in range(16)]
                for ts in range(4):
                    i = tc * 4 + ts
                    tg = qtr * 8 + i
                    for ch in range(2):
                        kq = f"P{2 + (ts * 2 + ch) % 2}"
                        def h2mm(eng, tc=tc, ts=ts, ch=ch):
                            pp = pA[(ts * 2 + ch) % 2]
                            for fc in range(16):
                                ins = eng.matmul(
                                    pp[:], aof[:, tc * 8192 + fc * 512 + ts * 128:
                                               tc * 8192 + fc * 512 + ts * 128 + 128],
                                    WB[:, fc * 1024 + ch * 512: fc * 1024 + (ch + 1) * 512],
                                    start=(fc == 0), stop=(fc == 15))
                            return ins
                        sc.op("T", h2mm, reads=H1K + WBK, writes=[kq],
                              tag=f"{prefix}_h2 q{qtr} e{e} i{i} c{ch}")
                        sc.op("V", lambda v, i=i, ch=ch, tg=tg, e=e, pi=(ts * 2 + ch) % 2:
                              v.scalar_tensor_tensor(
                                  XR[:, i * D + ch * 512: i * D + (ch + 1) * 512],
                                  pA[pi][:], carena[:, tg * 8 + e: tg * 8 + e + 1],
                                  XR[:, i * D + ch * 512: i * D + (ch + 1) * 512],
                                  ALU.mult, ALU.add),
                              reads=[kq, f"{prefix}_car:{tg}", f"{prefix}_XR:{i}"],
                              writes=[f"{prefix}_XR:{i}"],
                              tag=f"{prefix}_h2a q{qtr} e{e} i{i} c{ch}")
            if e + 1 < 8:
                load_w12(2, e + 1, create)
            elif qtr + 1 < 4:
                load_w12(2, 0, False)
        for i in range(8):
            tg = qtr * 8 + i
            sc.op("Q", lambda q, tg=tg, i=i: q.dma_start(
                xout.ap()[tg * 128:(tg + 1) * 128, :], XR[:, i * D:(i + 1) * D]),
                reads=[f"{prefix}_XR:{i}"], writes=[f"xo:{tg}", f"{prefix}_XR:{i}"],
                dma=True, tag=f"{prefix}_stXR q{qtr} i{i}")


def emit_l1_ffn(sc0, b, W, xin, xout, prefix):
    nc = b.nc
    NQ, NKV = 1024, 2048
    LOGKQ = float(np.log(np.float32(1152.0)))
    LOGKKV = float(np.log(np.float32(2304.0)))
    NIT = 50
    SCL8 = 0.125

    xnlf = b.sbuf(prefix + "_xa", [128, CT * T], BF16)
    xhf = xnlf
    FB = b.sbuf(prefix + "_FB", [128, 2048], F32)
    GB = b.sbuf(prefix + "_GB", [128, 3072], BF16)
    nmr = b.sbuf(prefix + "_nmr", [128, 96], F32)
    ut = b.sbuf(prefix + "_ut", [128, D], F32)
    xnlt = b.sbuf(prefix + "_xnlt", [128, D], BF16)
    xht = b.sbuf(prefix + "_xht", [128, D], BF16)
    rts = b.sbuf(prefix + "_rts", [128, 320], F32)
    pst = b.sbuf(prefix + "_pstb", [128, 16], F32)
    bis = b.sbuf(prefix + "_bis", [1, 96], F32)
    Ebuf = b.sbuf(prefix + "_Ebuf", [128, 32], BF16)
    LT = b.sbuf(prefix + "_LT", [128, 128], BF16)
    wqkvL = [b.sbuf(prefix + f"_wkv{i}", [128, 3 * 1024], BF16) for i in range(2)]
    Q2 = b.sbuf(prefix + "_Q2", [128, T], BF16)
    K2 = b.sbuf(prefix + "_K2", [128, T], BF16)
    v2l = b.sbuf(prefix + "_v2l", [128, 64 * 130], BF16)
    expT = b.sbuf(prefix + "_expT", [128, 2 * 192], BF16)
    exph = b.sbuf(prefix + "_exph", [128, 2 * 512], BF16)
    expn = b.sbuf(prefix + "_expn", [1, 2 * 512], BF16)
    aotL = b.sbuf(prefix + "_aotL", [128, T], BF16)
    aofst = [b.sbuf(prefix + f"_aofst{i}", [128, 128], BF16) for i in range(2)]
    aofld = [b.sbuf(prefix + f"_aofld{i}", [128, 4 * 128], BF16) for i in range(2)]
    wout = b.sbuf(prefix + "_wout", [128, 4 * D], BF16)
    nkcol = b.sbuf(prefix + "_nk", [128, 4], BF16)
    nkf = b.sbuf(prefix + "_nkf", [128, 4], F32)
    nvrow = b.sbuf(prefix + "_nv", [1, 520], BF16)
    nvf = b.sbuf(prefix + "_nvf", [1, 520], F32)


    pq = [b["P0"], b["P1"]]
    pq2 = [b["P0"], b["P1"], b["P2"], b["P3"]]
    pav = [b["P4"], b["P5"]]
    pA = [b["P2"], b["P3"], b["P4"], b["P5"]]

    CQ_S, CK_S, CQ_B, CK_B, CQ_SC, CK_SC, TMP, MQ, QG, NG = (
        0, 32, 64, 96, 128, 160, 192, 224, 256, 288)

    sc = sc0

    def guard(gkey):
        sc.op("V", lambda v: v.engine_nop(), reads=[], writes=[gkey],
              tag="guard_" + gkey)

    rowstage = b["SCR"][0:1, 0:1024]
    dma_in(sc, b, nkf[:], W["nullk_dev"].ap(), prefix + "_nkf", tag="ld_nk")
    dma_in(sc, b, nvf[:], W["nullv_row"].ap(), prefix + "_nvf", tag="ld_nv")
    sc.op("V", lambda v: v.tensor_copy(nkcol[:], nkf[:]),
          reads=[prefix + "_nkf"], writes=[prefix + "_nk"], tag="cv_nk")
    sc.op("V", lambda v: v.tensor_copy(nvrow[:], nvf[:]),
          reads=[prefix + "_nvf"], writes=[prefix + "_nv"], tag="cv_nv")
    def bcast_row(rn, dst, dstslot_lo, dkey, out_bf):
        dma_in(sc, b, rowstage, W[rn].ap(), "SCR:0", keys_r=["SCR:0"],
               tag=f"ld_{rn}")
        for half in range(2):
            kps = f"P{half}"
            sc.op("T", lambda t, half=half: t.matmul(
                pq[half][:], b["ones_row"][:],
                rowstage[:, half * 512:(half + 1) * 512],
                start=True, stop=True),
                reads=["ones_row", "SCR:0"], writes=[kps],
                tag=f"{prefix}_bc_{rn}{half}")
            sc.op("V", lambda v, half=half: v.tensor_copy(
                dst[:, dstslot_lo + half * 512: dstslot_lo + (half + 1) * 512],
                pq[half][:]),
                reads=[kps, "SCR:0"], writes=[dkey], tag=f"{prefix}_bcv_{rn}{half}")

    bcast_row("rt_q", FB, 0, f"{prefix}_FB:0", False)
    bcast_row("rt_kv", FB, 1024, f"{prefix}_FB:1", False)
    bcast_row("ln_g", GB, 0, f"{prefix}_GB:0", True)
    bcast_row("ln_b", GB, 1024, f"{prefix}_GB:1", True)
    bcast_row("rms_g", GB, 2048, f"{prefix}_GB:2", True)
    sc.op("P", lambda g: g.iota(b["SCR"][:, 0:128], pattern=[[1, 128]], base=0,
                                channel_multiplier=-1,
                                allow_small_or_imprecise_dtypes=True),
          reads=["SCR:0"], writes=["SCR:0"], tag=f"{prefix}_ltiota")
    sc.op("P", lambda g: g.tensor_scalar(LT[:], b["SCR"][:, 0:128], 0.0, None,
                                         ALU.is_gt),
          reads=["SCR:0"], writes=[prefix + "_LT"], tag=f"{prefix}_ltge")

    ceps, kceps = b.const_col(sc, 1e-5)

    for t in range(TT):
        par = t % 2
        kx = f"SCR:{par}"
        xtile = b["SCR"][:, par * 1024:(par + 1) * 1024]
        st8 = pst[:]
        kst = prefix + "_pst"
        dma_in(sc, b, xtile, xin.ap()[t * 128:(t + 1) * 128, :], kx,
               keys_r=[kx], tag=f"{prefix}_ldx t{t}")
        sc.op("V", lambda v, xtile=xtile: v.bn_stats(st8[:, 0:6], xtile[:, 0:512]),
              reads=[kx], writes=[kst + "a"], tag=f"{prefix}_bn0 t{t}")
        sc.op("V", lambda v, xtile=xtile: v.bn_stats(st8[:, 6:12], xtile[:, 512:1024]),
              reads=[kx], writes=[kst + "b"], tag=f"{prefix}_bn1 t{t}")
        sc.op("V", lambda v: v.bn_aggr(st8[:, 12:14], st8[:, 0:12]),
              reads=[kst + "a", kst + "b"], writes=[kst + "mv"], tag=f"{prefix}_agg t{t}")
        sc.op("S", lambda s: s.activation(st8[:, 14:15], st8[:, 13:14], AF.Sqrt,
                                          bias=ceps[:]),
              reads=[kst + "mv", kceps], writes=[kst + "sd"], tag=f"{prefix}_sq t{t}")
        sc.op("V", lambda v: v.reciprocal(st8[:, 15:16], st8[:, 14:15]),
              reads=[kst + "sd"], writes=[kst + "rs"], tag=f"{prefix}_rc t{t}")
        sc.op("V", lambda v, xtile=xtile: v.tensor_scalar(
            ut[:], xtile, st8[:, 12:13], st8[:, 15:16], ALU.subtract, ALU.mult),
            reads=[kx, kst + "mv", kst + "rs"], writes=[prefix + "_ut"],
            tag=f"{prefix}_u t{t}")
        sc.op("V", lambda v, t=t: v.scalar_tensor_tensor(
            xht[:], ut[:], 1.0, FB[:, 0:1024], ALU.mult, ALU.mult,
            accum_out=rts[:, CQ_S + t:CQ_S + t + 1]),
            reads=[prefix + "_ut", f"{prefix}_FB:0", prefix + "_xht"],
            writes=[f"{prefix}_sq:{t}", prefix + "_xht"], tag=f"{prefix}_sdq t{t}")
        sc.op("V", lambda v, t=t: v.scalar_tensor_tensor(
            xht[:], ut[:], 1.0, FB[:, 1024:2048], ALU.mult, ALU.mult,
            accum_out=rts[:, CK_S + t:CK_S + t + 1]),
            reads=[prefix + "_ut", f"{prefix}_FB:1", prefix + "_xht"],
            writes=[f"{prefix}_skv:{t}", prefix + "_xht"], tag=f"{prefix}_sdk t{t}")
        sc.op("V", lambda v, t=t: v.tensor_copy(nmr[:, 32 + t:33 + t],
                                                st8[:, 12:13]),
              reads=[kst + "mv"], writes=[prefix + "_nmr"], tag=f"{prefix}_svm t{t}")
        sc.op("V", lambda v, t=t: v.tensor_copy(nmr[:, 64 + t:65 + t],
                                                st8[:, 15:16]),
              reads=[kst + "rs"], writes=[prefix + "_nmr"], tag=f"{prefix}_svr t{t}")
        sc.op("V", lambda v: v.bn_stats(st8[:, 0:6], ut[:, 0:512]),
              reads=[prefix + "_ut", kst + "a"], writes=[kst + "a"],
              tag=f"{prefix}_b20 t{t}")
        sc.op("V", lambda v: v.bn_stats(st8[:, 6:12], ut[:, 512:1024]),
              reads=[prefix + "_ut", kst + "b"], writes=[kst + "b"],
              tag=f"{prefix}_b21 t{t}")
        sc.op("V", lambda v: v.bn_aggr(st8[:, 12:14], st8[:, 0:12]),
              reads=[kst + "a", kst + "b", kst + "mv"], writes=[kst + "mv2"],
              tag=f"{prefix}_ag2 t{t}")
        sc.op("V", lambda v: v.tensor_tensor(st8[:, 2:3], st8[:, 12:13],
                                             st8[:, 12:13], ALU.mult),
              reads=[kst + "mv2"], writes=[kst + "m2s"], tag=f"{prefix}_m2 t{t}")
        sc.op("V", lambda v: v.tensor_tensor(st8[:, 3:4], st8[:, 13:14],
                                             st8[:, 2:3], ALU.add),
              reads=[kst + "mv2", kst + "m2s"], writes=[kst + "ss"],
              tag=f"{prefix}_ss t{t}")
        sc.op("S", lambda s: s.activation(st8[:, 4:5], st8[:, 3:4], AF.Sqrt,
                                          scale=1024.0),
              reads=[kst + "ss"], writes=[kst + "nr"], tag=f"{prefix}_sq2 t{t}")
        sc.op("V", lambda v: v.tensor_scalar(st8[:, 4:5], st8[:, 4:5], 1e-12,
                                             None, ALU.max),
              reads=[kst + "nr"], writes=[kst + "nr2"], tag=f"{prefix}_mx t{t}")
        sc.op("V", lambda v: v.reciprocal(st8[:, 5:6], st8[:, 4:5]),
              reads=[kst + "nr2"], writes=[kst + "nr3"], tag=f"{prefix}_rc2 t{t}")
        sc.op("V", lambda v: v.tensor_scalar(st8[:, 5:6], st8[:, 5:6], 32.0,
                                             None, ALU.mult),
              reads=[kst + "nr3"], writes=[kst + "nr4"], tag=f"{prefix}_m32 t{t}")
        sc.op("V", lambda v, t=t: v.tensor_copy(nmr[:, t:t + 1], st8[:, 5:6]),
              reads=[kst + "nr4"], writes=[prefix + "_nmr"], tag=f"{prefix}_svn t{t}")
        sc.op("S", lambda s: s.activation(st8[:, 14:15], st8[:, 13:14], AF.Sqrt,
                                          bias=ceps[:]),
              reads=[kst + "mv2", kceps, kst + "sd"], writes=[kst + "sd2"],
              tag=f"{prefix}_sq3 t{t}")
        sc.op("V", lambda v: v.reciprocal(st8[:, 15:16], st8[:, 14:15]),
              reads=[kst + "sd2", kst + "rs"], writes=[kst + "rs2"],
              tag=f"{prefix}_rc3 t{t}")
        sc.op("V", lambda v: v.tensor_scalar(
            ut[:], ut[:], st8[:, 12:13], st8[:, 15:16], ALU.subtract, ALU.mult),
            reads=[prefix + "_ut", kst + "mv2", kst + "rs2"],
            writes=[prefix + "_ut"], tag=f"{prefix}_nl1 t{t}")
        sc.op("V", lambda v: v.tensor_tensor(ut[:], ut[:], GB[:, 0:1024], ALU.mult),
              reads=[prefix + "_ut", f"{prefix}_GB:0"], writes=[prefix + "_ut"],
              tag=f"{prefix}_nl2 t{t}")
        sc.op("V", lambda v: v.tensor_tensor(xnlt[:], ut[:], GB[:, 1024:2048],
                                             ALU.add),
              reads=[prefix + "_ut", f"{prefix}_GB:1"], writes=[prefix + "_xnlt"],
              tag=f"{prefix}_nl3 t{t}")
        for c in range(CT):
            pkk = f"PT{(t * CT + c) % 2}"
            sc.op("T", lambda tn, c=c, pkk=pkk: tn.transpose(
                b[pkk][:], xnlt[:, c * 128:(c + 1) * 128], b["ident_bf"][:]),
                reads=[prefix + "_xnlt", "ident_bf"],
                writes=[pkk], tag=f"{prefix}_tpxnl t{t} c{c}")
            sc.op("S", lambda s, t=t, c=c, pkk=pkk: s.copy(
                xnlf[:, c * T + t * 128: c * T + (t + 1) * 128], b[pkk][:]),
                reads=[pkk], writes=[f"{prefix}_xa:{c}:{t}"],
                tag=f"{prefix}_evxnl t{t} c{c}")

    def R(lo, hi):
        ks = [f"{prefix}_rb{i}" for i in range(lo // 32, (hi + 31) // 32)]
        return rts[:, lo:hi], ks

    def R1(lo, hi):
        ap, ks = R(lo, hi)
        return rts[0:1, lo:hi], ks

    def Bc(lo, hi):
        return bis[:, lo:hi], [f"{prefix}_bis{i}" for i in range(lo // 8, (hi + 7) // 8)]

    onescol_f = b.sbuf("onescol_f", [128, 1], F32)
    b.t["onescol_f"] = onescol_f
    sc.op("V", lambda v: v.memset(onescol_f[:], 1.0), writes=["onescol_f"],
          tag="onescol_f")

    def vop(fn, rd, wr, tag):
        sc.op("V", fn, reads=rd, writes=wr, tag=tag)

    def sop(fn, rd, wr, tag):
        sc.op("S", fn, reads=rd, writes=wr, tag=tag)

    def top_op(fn, rd, wr, tag):
        sc.op("T", fn, reads=rd, writes=wr, tag=tag)

    SQ_, SQK = R(0, 32)
    SK_, SKK = R(32, 64)

    def routing(which, s_ap, SRD, sco_lo, logk):
        Bp, BK = R(64, 96)
        Ep, EK = R(96, 128)
        rs1, rsK = R(224, 225)
        ab, abK = R(225, 226)
        sco, scoK = R(sco_lo, sco_lo + 32)
        lse, lseK = Bc(0, 1)
        av, avK = Bc(1, 2)
        vop(lambda v: v.tensor_scalar(Bp, s_ap, -1.0, None, ALU.mult),
            SRD, BK, f"{prefix}_r{which}_init")
        for it in range(NIT):
            vop(lambda v: v.tensor_tensor(Ep, s_ap, Bp, ALU.add),
                SRD + BK, EK, f"{prefix}_r{which}_sb{it}")
            sop(lambda s: s.activation(Ep, Ep, AF.Exp), EK, EK,
                f"{prefix}_r{which}_e{it}")
            vop(lambda v: v.reduce_sum(rs1, Ep, axis=AX.X), EK, rsK,
                f"{prefix}_r{which}_rs{it}")
            top_op(lambda t: t.matmul(b["P5"][0:1, 0:1], rs1, onescol_f[:],
                                      start=True, stop=True),
                   rsK + ["onescol_f"], ["P5"], f"{prefix}_r{which}_tt{it}")
            sop(lambda s: s.activation(lse, b["P5"][0:1, 0:1], AF.Ln),
                ["P5"], lseK, f"{prefix}_r{which}_ln{it}")
            vop(lambda v, logk=logk: v.tensor_scalar(av, lse, -1.0, logk,
                                                     ALU.mult, ALU.add),
                lseK, avK, f"{prefix}_r{which}_a{it}")
            top_op(lambda t: t.matmul(b["P5"][:, 1:2], b["ones_row"][:], av,
                                      start=True, stop=True),
                   avK + ["ones_row"], ["P5"], f"{prefix}_r{which}_ab{it}")
            vop(lambda v: v.tensor_copy(ab, b["P5"][:, 1:2]), ["P5"], abK,
                f"{prefix}_r{which}_abe{it}")
            vop(lambda v: v.tensor_scalar(Ep, s_ap, ab, None, ALU.add),
                SRD + abK, EK, f"{prefix}_r{which}_sa{it}")
            vop(lambda v: v.tensor_scalar(Bp, Ep, 0.0, -1.0, ALU.max, ALU.mult),
                EK, BK, f"{prefix}_r{which}_b{it}")
        vop(lambda v: v.tensor_tensor(sco, Ep, Bp, ALU.add), EK + BK, scoK,
            f"{prefix}_r{which}_f1")
        sop(lambda s: s.activation(sco, sco, AF.Exp), scoK, scoK,
            f"{prefix}_r{which}_f2")
        return sco, scoK

    SRQ = [f"{prefix}_sq:{t}" for t in range(TT)] + SQK
    SRK = [f"{prefix}_skv:{t}" for t in range(TT)] + SKK
    scoQ, scoQK = routing("q", SQ_, SRQ, 128, LOGKQ)
    scoKV, scoKVK = routing("kv", SK_, SRK, 160, LOGKKV)

    def topk_mask(which, sco, scoK, k, m_lo):
        lo, loK = Bc(2, 3)
        hi, hiK = Bc(3, 4)
        mid, midK = Bc(4, 5)
        cnt, cntK = Bc(5, 6)
        pred, predK = Bc(6, 7)
        d1, d1K = Bc(7, 8)
        d2, d2K = Bc(8, 9)
        npred, npK = Bc(9, 10)
        rr, rrK = Bc(10, 11)
        cs_i, csiK = Bc(16, 48)
        cs_e, cseK = Bc(48, 80)
        midb, midbK = R(226, 227)
        lob, lobK = R(227, 228)
        hib, hibK = R(228, 229)
        rbc, rbcK = R(229, 230)
        rs1, rsK = R(224, 225)
        cmp, cmpK = R(96, 128)
        ghi, ghiK = R(192, 224)
        E, EK2 = R(64, 96)
        pfx, pfxK = R(0, 32)
        msk, mskK = R(m_lo, m_lo + 32)
        vop(lambda v: v.memset(lo, 0.0), [], loK, f"{prefix}_t{which}_l0")
        vop(lambda v: v.memset(hi, 1.5), [], hiK, f"{prefix}_t{which}_h0")
        for it in range(30):
            vop(lambda v: v.tensor_tensor(mid, lo, hi, ALU.add), loK + hiK,
                midK, f"{prefix}_t{which}_m{it}")
            vop(lambda v: v.tensor_scalar(mid, mid, 0.5, None, ALU.mult),
                midK, midK, f"{prefix}_t{which}_m2{it}")
            top_op(lambda t: t.matmul(b["P5"][:, 2:3], b["ones_row"][:], mid,
                                      start=True, stop=True),
                   midK + ["ones_row"], ["P5"], f"{prefix}_t{which}_mb{it}")
            vop(lambda v: v.tensor_copy(midb, b["P5"][:, 2:3]), ["P5"], midbK,
                f"{prefix}_t{which}_me{it}")
            vop(lambda v: v.tensor_scalar(cmp, sco, midb, None, ALU.is_gt),
                scoK + midbK, cmpK, f"{prefix}_t{which}_c{it}")
            vop(lambda v: v.reduce_sum(rs1, cmp, axis=AX.X), cmpK, rsK,
                f"{prefix}_t{which}_r{it}")
            top_op(lambda t: t.matmul(b["P5"][0:1, 3:4], rs1, onescol_f[:],
                                      start=True, stop=True),
                   rsK + ["onescol_f"], ["P5"], f"{prefix}_t{which}_t{it}")
            vop(lambda v: v.tensor_copy(cnt, b["P5"][0:1, 3:4]), ["P5"], cntK,
                f"{prefix}_t{which}_te{it}")
            vop(lambda v, k=k: v.tensor_scalar(pred, cnt, float(k), None,
                                               ALU.is_ge), cntK, predK,
                f"{prefix}_t{which}_p{it}")
            vop(lambda v: v.tensor_tensor(d1, mid, lo, ALU.subtract),
                midK + loK, d1K, f"{prefix}_t{which}_d1{it}")
            vop(lambda v: v.tensor_tensor(d1, d1, pred, ALU.mult),
                d1K + predK, d1K, f"{prefix}_t{which}_d1m{it}")
            vop(lambda v: v.tensor_tensor(lo, lo, d1, ALU.add), d1K + loK,
                loK, f"{prefix}_t{which}_lo{it}")
            vop(lambda v: v.tensor_tensor(d2, mid, hi, ALU.subtract),
                midK + hiK, d2K, f"{prefix}_t{which}_d2{it}")
            vop(lambda v: v.tensor_scalar(npred, pred, -1.0, 1.0, ALU.mult,
                                          ALU.add), predK, npK,
                f"{prefix}_t{which}_np{it}")
            vop(lambda v: v.tensor_tensor(d2, d2, npred, ALU.mult),
                d2K + npK, d2K, f"{prefix}_t{which}_d2m{it}")
            vop(lambda v: v.tensor_tensor(hi, hi, d2, ALU.add), d2K + hiK,
                hiK, f"{prefix}_t{which}_hi{it}")
        top_op(lambda t: t.matmul(b["P5"][:, 4:5], b["ones_row"][:], lo,
                                  start=True, stop=True),
               loK + ["ones_row"], ["P5"], f"{prefix}_t{which}_lob")
        vop(lambda v: v.tensor_copy(lob, b["P5"][:, 4:5]), ["P5"], lobK,
            f"{prefix}_t{which}_lobe")
        top_op(lambda t: t.matmul(b["P5"][:, 5:6], b["ones_row"][:], hi,
                                  start=True, stop=True),
               hiK + ["ones_row"], ["P5"], f"{prefix}_t{which}_hib")
        vop(lambda v: v.tensor_copy(hib, b["P5"][:, 5:6]), ["P5"], hibK,
            f"{prefix}_t{which}_hibe")
        vop(lambda v: v.tensor_scalar(ghi, sco, hib, None, ALU.is_gt),
            scoK + hibK, ghiK, f"{prefix}_t{which}_g")
        vop(lambda v: v.reduce_sum(rs1, ghi, axis=AX.X), ghiK, rsK,
            f"{prefix}_t{which}_cg")
        top_op(lambda t: t.matmul(b["P5"][0:1, 3:4], rs1, onescol_f[:],
                                  start=True, stop=True),
               rsK + ["onescol_f"], ["P5"], f"{prefix}_t{which}_cg2")
        vop(lambda v, k=k: v.tensor_scalar(rr, b["P5"][0:1, 3:4], -1.0,
                                           float(k), ALU.mult, ALU.add),
            ["P5"], rrK, f"{prefix}_t{which}_rr")
        top_op(lambda t: t.matmul(b["P5"][:, 6:7], b["ones_row"][:], rr,
                                  start=True, stop=True),
               rrK + ["ones_row"], ["P5"], f"{prefix}_t{which}_rrb")
        vop(lambda v: v.tensor_copy(rbc, b["P5"][:, 6:7]), ["P5"], rbcK,
            f"{prefix}_t{which}_rrbe")
        vop(lambda v: v.tensor_scalar(E, sco, lob, None, ALU.is_gt),
            scoK + lobK, EK2, f"{prefix}_t{which}_gl")
        vop(lambda v: v.tensor_tensor(E, E, ghi, ALU.subtract), EK2 + ghiK,
            EK2, f"{prefix}_t{which}_E")
        Ebf = Ebuf[:]
        vop(lambda v: v.tensor_copy(Ebf, E), EK2, [f"{prefix}_Ebf"],
            f"{prefix}_t{which}_Eb")
        top_op(lambda t: t.matmul(b["P4"][:, 0:32], LT[:], Ebf, start=True,
                                  stop=True),
               [prefix + "_LT", f"{prefix}_Ebf"], ["P4"], f"{prefix}_t{which}_px")
        top_op(lambda t: t.matmul(b["P5"][0:1, 8:40], b["ones_col_bf"][:], Ebf,
                                  start=True, stop=True),
               ["ones_col_bf", f"{prefix}_Ebf"], ["P5"], f"{prefix}_t{which}_cs")
        cs_i_ap = bis[0:1, 16:48]
        cs_e_ap = bis[0:1, 48:80]
        vop(lambda v: v.tensor_copy(cs_i_ap, b["P5"][0:1, 8:40]), ["P5"],
            csiK, f"{prefix}_t{which}_cse")
        vop(lambda v: v.tensor_tensor_scan(cs_e_ap, cs_i_ap, cs_i_ap, 0.0,
                                           ALU.add, ALU.bypass), csiK, cseK,
            f"{prefix}_t{which}_sc")
        vop(lambda v: v.tensor_tensor(cs_e_ap, cs_e_ap, cs_i_ap, ALU.subtract),
            cseK + csiK, cseK, f"{prefix}_t{which}_ex")
        top_op(lambda t: t.matmul(b["P5"][:, 8:40], b["ones_row"][:], cs_e_ap,
                                  start=True, stop=True),
               cseK + ["ones_row"], ["P5"], f"{prefix}_t{which}_cb")
        wcol, wcolK = R(32, 64)
        vop(lambda v: v.tensor_copy(wcol, b["P4"][:, 0:32]), ["P4"], wcolK,
            f"{prefix}_t{which}_pxe")
        vop(lambda v: v.tensor_tensor(pfx, b["P5"][:, 8:40], wcol,
                                      ALU.add), ["P5"] + wcolK, pfxK,
            f"{prefix}_t{which}_tot")
        vop(lambda v: v.tensor_scalar(pfx, pfx, rbc, None, ALU.is_lt),
            pfxK + rbcK, pfxK, f"{prefix}_t{which}_plt")
        vop(lambda v: v.tensor_tensor(pfx, pfx, E, ALU.mult), pfxK + EK2,
            pfxK, f"{prefix}_t{which}_sel")
        vop(lambda v: v.tensor_tensor(msk, pfx, ghi, ALU.add), pfxK + ghiK,
            mskK, f"{prefix}_t{which}_m")
        return msk, mskK

    maskQ, maskQK = topk_mask("q", scoQ, scoQK, NQ, 232)
    maskKV, maskKVK = topk_mask("kv", scoKV, scoKVK, NKV, 264)
    qg, qgK = R(32, 64)
    ng, ngK = R(192, 224)
    kvg, kvgK = R(96, 128)
    tmp3, t3K = R(64, 96)
    vop(lambda v: v.tensor_scalar(tmp3, scoQ, -1.0, 1.0, ALU.mult, ALU.add),
        scoQK, t3K, f"{prefix}_g1")
    vop(lambda v: v.tensor_tensor(tmp3, tmp3, scoQ, ALU.add), t3K + scoQK,
        t3K, f"{prefix}_g2")
    vop(lambda v: v.tensor_tensor(qg, tmp3, maskQ, ALU.mult), t3K + maskQK,
        qgK, f"{prefix}_g3")
    vop(lambda v: v.tensor_scalar(qg, qg, 0.5, None, ALU.mult), qgK, qgK,
        f"{prefix}_g4")
    vop(lambda v: v.tensor_scalar(ng, maskQ, -0.5, 0.5, ALU.mult, ALU.add),
        maskQK, ngK, f"{prefix}_g5")
    vop(lambda v: v.tensor_scalar(tmp3, scoKV, -1.0, 1.0, ALU.mult, ALU.add),
        scoKVK, t3K, f"{prefix}_g6")
    vop(lambda v: v.tensor_tensor(tmp3, tmp3, scoKV, ALU.add), t3K + scoKVK,
        t3K, f"{prefix}_g7")
    vop(lambda v: v.tensor_tensor(kvg, tmp3, maskKV, ALU.mult), t3K + maskKVK,
        kvgK, f"{prefix}_g8")

    XNLK = [f"{prefix}_xa:{c}:{t}" for c in range(CT) for t in range(TT)]
    XHK = XNLK
    recL = b.sbuf(prefix + "_recL", [128, 4], F32)

    def load_pair_w(names, hbm, col_off, p, par):
        for wi, (wn, off) in enumerate(zip(names, col_off)):
            ks = f"SCR:{par}"
            srcap = hbm[wn].ap()[:, off + p * 128: off + (p + 1) * 128].rearrange(
                "(c p) j -> p c j", p=128)
            sc.op("Q", lambda q, srcap=srcap, par=par: q.dma_start(
                b["SCR"][:, par * 1024:(par + 1) * 1024].rearrange(
                    "p (c j) -> p c j", j=128), srcap),
                reads=[ks], writes=[ks], dma=True, tag=f"{prefix}_lw p{p} {wn}")
            sc.op("V", lambda v, par=par, wi=wi: v.tensor_copy(
                wqkvL[par][:, wi * 1024:(wi + 1) * 1024],
                b["SCR"][:, par * 1024:(par + 1) * 1024]),
                reads=[ks], writes=[f"{prefix}_wkv{par}:{wi}"],
                tag=f"{prefix}_cw p{p} {wn}")

    load_convert_bf16(sc, b, prefix + "_wo", W["wout_l"].ap(), 512, D,
                      stage=b["SCR"][:, 0:1024], stage_key="SCR:0", dst=wout)
    lighth = b.dram(prefix + "_lighth", [T, D], F32)
    aofh = b.dram(prefix + "_aofh", [CT * TT, 128 * 128], BF16)

    load_pair_w(("wqkv_l",) * 3, W, (0, 512, 1024), 0, 0)
    for p in range(4):
        par = p % 2
        if p + 1 < 4:
            load_pair_w(("wqkv_l",) * 3, W, (0, 512, 1024), p + 1, (p + 1) % 2)
        WQk = [f"{prefix}_wkv{par}:0"]
        WKk = [f"{prefix}_wkv{par}:1"]
        WVk = [f"{prefix}_wkv{par}:2"]
        for wi, (dst, wk) in enumerate(((Q2, WQk), (K2, WKk))):
            for n in range(8):
                kq = f"P{n % 2}"
                def qmm(eng, par=par, n=n, wi=wi):
                    pp = pq[n % 2]
                    for c in range(CT):
                        ins = eng.matmul(
                            pp[:], wqkvL[par][:, wi * 1024 + c * 128: wi * 1024 + c * 128 + 128],
                            xnlf[:, c * T + n * 512: c * T + (n + 1) * 512],
                            start=(c == 0), stop=(c == CT - 1))
                    return ins
                sc.op("T", qmm, reads=wk + XNLK, writes=[kq],
                      tag=f"{prefix}_lqk p{p} w{wi} n{n}")
                sc.op("S", lambda s, n=n, dst=dst: s.copy(
                    dst[:, n * 512:(n + 1) * 512], pq[n % 2][:]),
                    reads=[kq], writes=[f"{prefix}_{'Q2' if wi == 0 else 'K2'}:{n}"],
                    tag=f"{prefix}_lqke p{p} w{wi} n{n}")
        sc.op("V", lambda v: v.memset(
            v2l[0:64, :].rearrange("p (t h w) -> p t h w", h=2, w=65)[:, :, :, 64:65],
            1.0),
            writes=[f"{prefix}_v2l:{w}" for w in range(64)],
            tag=f"{prefix}_lvone p{p}")
        for w in range(64):
            kk = f"P{2 + w % 2}"
            def vmm(eng, par=par, w=w):
                pp = pA[w % 2]
                for c in range(CT):
                    ins = eng.matmul(
                        pp[0:64, 0:128],
                        xnlf[:, c * T + w * 64: c * T + (w + 1) * 64],
                        wqkvL[par][:, 2 * 1024 + c * 128: 2 * 1024 + c * 128 + 128],
                        start=(c == 0), stop=(c == CT - 1))
                return ins
            sc.op("T", vmm, reads=WVk + XNLK, writes=[kk], tag=f"{prefix}_lv p{p} w{w}")
            for h in range(2):
                sc.op("S", lambda s, w=w, h=h, pi=w % 2: s.copy(
                    v2l[0:64, w * 130 + h * 65: w * 130 + h * 65 + 64],
                    pA[pi][0:64, h * 64:(h + 1) * 64]),
                    reads=[kk], writes=[f"{prefix}_v2l:{w}"],
                    tag=f"{prefix}_lve p{p} w{w} h{h}")
        for h in range(2):
            base = 64 * h
            Q2K = [f"{prefix}_Q2:{n}" for n in range(8)]
            K2K = [f"{prefix}_K2:{n}" for n in range(8)]
            for w in range(64):
                ds = [d for d in (-1, 0, 1) if 0 <= w + d < 64]
                slot0 = (w % 2) * 3
                for di, d in enumerate(ds):
                    kp = f"P{di}"
                    sc.op("T", lambda eng, base=base, w=w, d=d, di=di: eng.matmul(
                        pq[di][0:64, 0:64] if di < 2 else b["P2"][0:64, 0:64],
                        K2[base:base + 64, (w + d) * 64:(w + d + 1) * 64],
                        Q2[base:base + 64, w * 64:(w + 1) * 64],
                        start=True, stop=True),
                        reads=Q2K + K2K, writes=[kp],
                        tag=f"{prefix}_ls p{p} h{h} w{w} d{d}")
                    sc.op("S", lambda s, di=di, slot0=slot0: s.activation(
                        expT[0:64, (slot0 + di) * 64:(slot0 + di + 1) * 64],
                        (pq[di] if di < 2 else b["P2"])[0:64, 0:64],
                        AF.Exp, scale=SCL8),
                        reads=[kp], writes=[f"{prefix}_expT:{slot0 + di}"],
                        tag=f"{prefix}_lse p{p} h{h} w{w} d{d}")
                def lav(eng, w=w, h=h, ds=ds, slot0=slot0):
                    for di, d in enumerate(ds):
                        ins = eng.matmul(
                            b["P3"][0:64, 0:65],
                            expT[0:64, (slot0 + di) * 64:(slot0 + di + 1) * 64],
                            v2l[0:64, (w + d) * 130 + h * 65: (w + d) * 130 + h * 65 + 65],
                            start=(di == 0), stop=(di == len(ds) - 1))
                    return ins
                sc.op("T", lav,
                      reads=[f"{prefix}_expT:{slot0 + di}" for di in range(len(ds))] +
                            [f"{prefix}_v2l:{w + d}" for d in ds],
                      writes=["P3"], tag=f"{prefix}_lav p{p} h{h} w{w}")
                sc.op("V", lambda v, h=h: v.reciprocal(
                    recL[:, h:h + 1][0:64, :], b["P3"][0:64, 64:65]),
                    reads=["P3"], writes=[f"{prefix}_recL:{h}"],
                    tag=f"{prefix}_lrec p{p} h{h} w{w}")
                sc.op("V", lambda v, w=w, h=h: v.tensor_scalar(
                    aotL[(w % 2) * 64:(w % 2) * 64 + 64,
                         (w // 2) * 128 + h * 64: (w // 2) * 128 + h * 64 + 64],
                    b["P3"][0:64, 0:64], recL[:, h:h + 1][0:64, :], None, ALU.mult),
                    reads=["P3", f"{prefix}_recL:{h}"],
                    writes=[f"{prefix}_aotL:{w // 2}:{h}"],
                    tag=f"{prefix}_laev p{p} h{h} w{w}")
        for t in range(TT):
            pkk = f"PT{t % 2}"
            kst = f"{prefix}_aofst{t % 2}"
            sc.op("T", lambda eng, t=t, pkk=pkk: eng.transpose(
                b[pkk][:], aotL[:, t * 128:(t + 1) * 128], b["ident_bf"][:]),
                reads=[f"{prefix}_aotL:{t}:0", f"{prefix}_aotL:{t}:1", "ident_bf"],
                writes=[pkk], tag=f"{prefix}_latr p{p} t{t}")
            sc.op("S", lambda s, t=t, pkk=pkk: s.copy(aofst[t % 2][:], b[pkk][:]),
                  reads=[pkk], writes=[kst], tag=f"{prefix}_latre p{p} t{t}")
            sc.op("Q", lambda q, t=t, p=p: q.dma_start(
                aofh.ap()[p * TT + t:p * TT + t + 1, :].rearrange(
                    "a (p2 f) -> (a p2) f", p2=128), aofst[t % 2][:]),
                reads=[kst], writes=[f"{prefix}_aofh:{p}:{t}"], dma=True,
                tag=f"{prefix}_last p{p} t{t}")
    for t in range(TT):
        par = t % 2
        kld = f"{prefix}_aofld{par}"
        for c in range(4):
            sc.op("Q", lambda q, t=t, c=c, par=par: q.dma_start(
                aofld[par][:, c * 128:(c + 1) * 128],
                aofh.ap()[c * TT + t:c * TT + t + 1, :].rearrange(
                    "a (p2 f) -> (a p2) f", p2=128)),
                reads=[f"{prefix}_aofh:{c}:{t}", kld + f":{c}"],
                writes=[kld + f":{c}"], dma=True, tag=f"{prefix}_lpl t{t} c{c}")
        for half in range(2):
            kp = f"P{half}"
            def lpm(eng, par=par, half=half):
                pp = pq[half]
                for c in range(4):
                    ins = eng.matmul(
                        pp[:], aofld[par][:, c * 128:(c + 1) * 128],
                        wout[:, c * 1024 + half * 512: c * 1024 + half * 512 + 512],
                        start=(c == 0), stop=(c == 3))
                return ins
            sc.op("T", lpm,
                  reads=[kld + f":{c}" for c in range(4)] +
                        [f"{prefix}_wo:{c}" for c in range(4)],
                  writes=[kp], tag=f"{prefix}_lpm t{t} h{half}")
            sc.op("V", lambda v, half=half: v.tensor_copy(
                ut[:, half * 512:(half + 1) * 512], pq[half][:]),
                reads=[kp, prefix + "_ut"], writes=[prefix + "_ut"],
                tag=f"{prefix}_lpe t{t} h{half}")
        sc.op("Q", lambda q, t=t: q.dma_start(
            lighth.ap()[t * 128:(t + 1) * 128, :], ut[:]),
            reads=[prefix + "_ut"], writes=[f"{prefix}_lh:{t}", prefix + "_ut"],
            dma=True, tag=f"{prefix}_lps t{t}")

    for t in range(TT):
        par = t % 2
        kx = f"SCR:{par}"
        xtile = b["SCR"][:, par * 1024:(par + 1) * 1024]
        dma_in(sc, b, xtile, xin.ap()[t * 128:(t + 1) * 128, :], kx,
               keys_r=[kx], tag=f"{prefix}_ldx2 t{t}")
        sc.op("V", lambda v, t=t, xtile=xtile: v.tensor_scalar(
            ut[:], xtile, nmr[:, 32 + t:33 + t], nmr[:, 64 + t:65 + t],
            ALU.subtract, ALU.mult),
            reads=[kx, prefix + "_nmr", prefix + "_ut"], writes=[prefix + "_ut"],
            tag=f"{prefix}_u2 t{t}")
        sc.op("V", lambda v, t=t: v.scalar_tensor_tensor(
            xht[:], ut[:], nmr[:, t:t + 1], GB[:, 2048:3072], ALU.mult, ALU.mult),
            reads=[prefix + "_ut", prefix + "_nmr", f"{prefix}_GB:2",
                   prefix + "_xht"],
            writes=[prefix + "_xht"], tag=f"{prefix}_xh2 t{t}")
        for c in range(CT):
            pkk = f"PT{(t * CT + c) % 2}"
            sc.op("T", lambda tn, c=c, pkk=pkk: tn.transpose(
                b[pkk][:], xht[:, c * 128:(c + 1) * 128], b["ident_bf"][:]),
                reads=[prefix + "_xht", "ident_bf"],
                writes=[pkk], tag=f"{prefix}_tpxh t{t} c{c}")
            sc.op("S", lambda s, t=t, c=c, pkk=pkk: s.copy(
                xhf[:, c * T + t * 128: c * T + (t + 1) * 128], b[pkk][:]),
                reads=[pkk], writes=[f"{prefix}_xa:{c}:{t}"],
                tag=f"{prefix}_evxh t{t} c{c}")

    load_convert_bf16(sc, b, prefix + "_wo", W["wout_h"].ap(), 512, D,
                      stage=b["SCR"][:, 0:1024], stage_key="SCR:0", dst=wout)
    load_pair_w(("wq_h", "wk_h", "wv_h"), W, (0, 0, 0), 0, 0)
    for p in range(4):
        par = p % 2
        if p + 1 < 4:
            load_pair_w(("wq_h", "wk_h", "wv_h"), W, (0, 0, 0), p + 1, (p + 1) % 2)
        WQk = [f"{prefix}_wkv{par}:0"]
        WKk = [f"{prefix}_wkv{par}:1"]
        WVk = [f"{prefix}_wkv{par}:2"]
        for wi, (dst, wk, xk, xf_) in enumerate(((Q2, WQk, XHK, xhf),
                                                 (K2, WKk, XHK, xhf))):
            for n in range(8):
                kq = f"P{n % 2}"
                def qmm(eng, par=par, n=n, wi=wi, xf_=xf_):
                    pp = pq[n % 2]
                    for c in range(CT):
                        ins = eng.matmul(
                            pp[:], wqkvL[par][:, wi * 1024 + c * 128: wi * 1024 + c * 128 + 128],
                            xf_[:, c * T + n * 512: c * T + (n + 1) * 512],
                            start=(c == 0), stop=(c == CT - 1))
                    return ins
                sc.op("T", qmm, reads=wk + xk, writes=[kq],
                      tag=f"{prefix}_hqk p{p} w{wi} n{n}")
                sc.op("S", lambda s, n=n, dst=dst: s.copy(
                    dst[:, n * 512:(n + 1) * 512], pq[n % 2][:]),
                    reads=[kq], writes=[f"{prefix}_{'Q2' if wi == 0 else 'K2'}:{n}"],
                    tag=f"{prefix}_hqke p{p} w{wi} n{n}")
        for t in range(TT):
            kk = f"P{2 + t % 2}"
            def vmm(eng, par=par, t=t):
                pp = pA[t % 2]
                for c in range(CT):
                    ins = eng.matmul(
                        pp[:, 0:128], xhf[:, c * T + t * 128: c * T + (t + 1) * 128],
                        wqkvL[par][:, 2 * 1024 + c * 128: 2 * 1024 + c * 128 + 128],
                        start=(c == 0), stop=(c == CT - 1))
                return ins
            sc.op("T", vmm, reads=WVk + XHK, writes=[kk], tag=f"{prefix}_hv p{p} t{t}")
            for h in range(2):
                sc.op("V", lambda v, t=t, h=h, pi=t % 2: v.tensor_scalar(
                    v2l[:, t * 130 + h * 65: t * 130 + h * 65 + 64],
                    pA[pi][:, h * 64:(h + 1) * 64],
                    rts[:, 96 + t:96 + t + 1], None, ALU.mult),
                    reads=[kk, f"{prefix}_rb3"], writes=[f"{prefix}_v2l:{t}"],
                    tag=f"{prefix}_hve p{p} t{t} h{h}")
                sc.op("V", lambda v, t=t, h=h: v.tensor_copy(
                    v2l[:, t * 130 + h * 65 + 64: t * 130 + h * 65 + 65],
                    rts[:, 96 + t:96 + t + 1]),
                    reads=[f"{prefix}_rb3", f"{prefix}_v2l:{t}"],
                    writes=[f"{prefix}_v2l:{t}"],
                    tag=f"{prefix}_hvo p{p} t{t} h{h}")
        for h in range(2):
            base = 64 * h
            hh = 2 * p + h
            Q2K = [f"{prefix}_Q2:{n}" for n in range(8)]
            K2K = [f"{prefix}_K2:{n}" for n in range(8)]
            for qc in range(16):
                def emit_sc(kc, base=base, qc=qc):
                    pi = kc % 4
                    ks = f"P{pi}"
                    ke = f"{prefix}_exph:{pi}"
                    sc.op("T", lambda eng, base=base, kc=kc, qc=qc, pi=pi: eng.matmul(
                        pq2[pi][:, 0:256], K2[base:base + 64, kc * 128:(kc + 1) * 128],
                        Q2[base:base + 64, qc * 256:(qc + 1) * 256],
                        start=True, stop=True),
                        reads=Q2K + K2K, writes=[ks],
                        tag=f"{prefix}_hs q{qc} k{kc}")
                    sc.op("S", lambda s, pi=pi: s.activation(
                        exph[:, pi * 256:(pi + 1) * 256], pq2[pi][:, 0:256],
                        AF.Exp, scale=SCL8),
                        reads=[ks], writes=[ke], tag=f"{prefix}_hse q{qc} k{kc}")

                def emit_av(kc, hh=hh, qc=qc):
                    pi = kc % 4
                    ke = f"{prefix}_exph:{pi}"
                    def hav(eng, pi=pi, kc=kc, hh=hh):
                        for qs in range(2):
                            ins = eng.matmul(
                                pav[qs][:, 0:65],
                                exph[:, pi * 256 + qs * 128: pi * 256 + qs * 128 + 128],
                                v2l[:, kc * 130 + (hh % 2) * 65: kc * 130 + (hh % 2) * 65 + 65],
                                start=(kc == 0), stop=False)
                        return ins
                    sc.op("T", hav, reads=[ke, f"{prefix}_v2l:{kc}"],
                          writes=["P4", "P5"],
                          tag=f"{prefix}_hav q{qc} k{kc}")

                for kc in range(TT):
                    emit_sc(kc)
                    if kc >= 3:
                        emit_av(kc - 3)
                for kc in range(TT - 3, TT):
                    emit_av(kc)
                kn = "P0"
                sc.op("T", lambda eng, base=base, p=p, qc=qc: eng.matmul(
                    pq[0][0:1, 0:256], nkcol[base:base + 64, p:p + 1],
                    Q2[base:base + 64, qc * 256:(qc + 1) * 256],
                    start=True, stop=True),
                    reads=Q2K + [prefix + "_nk"], writes=[kn],
                    tag=f"{prefix}_hn p{p} h{h} q{qc}")
                sc.op("S", lambda s: s.activation(
                    expn[0:1, 0:256], pq[0][0:1, 0:256], AF.Exp, scale=SCL8),
                    reads=[kn], writes=[f"{prefix}_expn"],
                    tag=f"{prefix}_hne p{p} h{h} q{qc}")
                def hnav(eng, hh=hh):
                    for qs in range(2):
                        ins = eng.matmul(
                            pav[qs][:, 0:65], expn[0:1, qs * 128:(qs + 1) * 128],
                            nvrow[0:1, (hh) * 65:(hh) * 65 + 65],
                            start=False, stop=True)
                    return ins
                sc.op("T", hnav, reads=[f"{prefix}_expn", prefix + "_nv"],
                      writes=["P4", "P5"],
                      tag=f"{prefix}_hnav p{p} h{h} q{qc}")
                for qs in range(2):
                    tglob = qc * 2 + qs
                    kA = f"P{4 + qs}"
                    sc.op("V", lambda v, qs=qs: v.reciprocal(
                        recL[:, 2 + qs:3 + qs], pav[qs][:, 64:65]),
                        reads=[kA], writes=[f"{prefix}_recH:{qs}"],
                        tag=f"{prefix}_hrec p{p} h{h} q{qc} s{qs}")
                    sc.op("V", lambda v, qs=qs, tglob=tglob, h=h: v.tensor_scalar(
                        aotL[:, tglob * 128 + h * 64: tglob * 128 + h * 64 + 64],
                        pav[qs][:, 0:64], recL[:, 2 + qs:3 + qs], None,
                        ALU.mult),
                        reads=[kA, f"{prefix}_recH:{qs}"],
                        writes=[f"{prefix}_aotL:{tglob}:{h}"],
                        tag=f"{prefix}_haev p{p} h{h} q{qc} s{qs}")
        for t in range(TT):
            pkk = f"PT{t % 2}"
            kst = f"{prefix}_aofst{t % 2}"
            sc.op("T", lambda eng, t=t, pkk=pkk: eng.transpose(
                b[pkk][:], aotL[:, t * 128:(t + 1) * 128], b["ident_bf"][:]),
                reads=[f"{prefix}_aotL:{t}:0", f"{prefix}_aotL:{t}:1", "ident_bf"],
                writes=[pkk], tag=f"{prefix}_hatr p{p} t{t}")
            sc.op("S", lambda s, t=t, pkk=pkk: s.copy(aofst[t % 2][:], b[pkk][:]),
                  reads=[pkk], writes=[kst], tag=f"{prefix}_hatre p{p} t{t}")
            sc.op("Q", lambda q, t=t, p=p: q.dma_start(
                aofh.ap()[p * TT + t:p * TT + t + 1, :].rearrange(
                    "a (p2 f) -> (a p2) f", p2=128), aofst[t % 2][:]),
                reads=[kst], writes=[f"{prefix}_aofh:{p}:{t}"], dma=True,
                tag=f"{prefix}_hast p{p} t{t}")

    guard(prefix + "_FBg")
    dma_in(sc, b, rowstage, W["null_q"].ap(), "SCR:0", keys_r=["SCR:0"],
           tag="ld_null_q")
    for half in range(2):
        kps = f"P{half}"
        sc.op("T", lambda t, half=half: t.matmul(
            pq[half][:], b["ones_row"][:],
            rowstage[:, half * 512:(half + 1) * 512],
            start=True, stop=True),
            reads=["ones_row", "SCR:0", prefix + "_FBg"],
            writes=[kps], tag=f"{prefix}_nqb{half}")
        sc.op("V", lambda v, half=half: v.tensor_copy(
            FB[:, half * 512:(half + 1) * 512], pq[half][:]),
            reads=[kps, prefix + "_FBg", f"{prefix}_FB:0"],
            writes=[f"{prefix}_FB:0"], tag=f"{prefix}_nqbe{half}")
    for t in range(TT):
        par = t % 2
        kxt = f"SCR:{par}"
        klt = f"{prefix}_FB:1"
        xtile = b["SCR"][:, par * 1024:(par + 1) * 1024]
        ltile = FB[:, 1024:2048]
        dma_in(sc, b, xtile, xin.ap()[t * 128:(t + 1) * 128, :], kxt,
               keys_r=[kxt], tag=f"{prefix}_cx t{t}")
        sc.op("Q", lambda q, t=t, ltile=ltile: q.dma_start(
            ltile, lighth.ap()[t * 128:(t + 1) * 128, :]),
            reads=[f"{prefix}_lh:{t}", klt, prefix + "_FBg"], writes=[klt],
            dma=True, tag=f"{prefix}_cl t{t}")
        kld = f"{prefix}_aofld{par}"
        for c in range(4):
            sc.op("Q", lambda q, t=t, c=c, par=par: q.dma_start(
                aofld[par][:, c * 128:(c + 1) * 128],
                aofh.ap()[c * TT + t:c * TT + t + 1, :].rearrange(
                    "a (p2 f) -> (a p2) f", p2=128)),
                reads=[f"{prefix}_aofh:{c}:{t}", kld + f":{c}"],
                writes=[kld + f":{c}"], dma=True, tag=f"{prefix}_chl t{t} c{c}")
        for half in range(2):
            kp = f"P{half}"
            def hpm(eng, par=par, half=half):
                pp = pq[half]
                for c in range(4):
                    ins = eng.matmul(
                        pp[:], aofld[par][:, c * 128:(c + 1) * 128],
                        wout[:, c * 1024 + half * 512: c * 1024 + half * 512 + 512],
                        start=(c == 0), stop=(c == 3))
                return ins
            sc.op("T", hpm,
                  reads=[kld + f":{c}" for c in range(4)] +
                        [f"{prefix}_wo:{c}" for c in range(4)],
                  writes=[kp], tag=f"{prefix}_chm t{t} h{half}")
            hs_ = slice(half * 512, (half + 1) * 512)
            sc.op("V", lambda v, half=half, t=t, hs_=hs_: v.tensor_scalar(
                ut[:, hs_], pq[half][:], rts[:, 32 + t:32 + t + 1], None,
                ALU.mult),
                reads=[kp, f"{prefix}_rb1", prefix + "_ut"],
                writes=[prefix + "_ut"], tag=f"{prefix}_ca t{t} h{half}")
            sc.op("V", lambda v, half=half, t=t, hs_=hs_: v.scalar_tensor_tensor(
                ut[:, hs_], FB[:, hs_], rts[:, 192 + t:192 + t + 1], ut[:, hs_],
                ALU.mult, ALU.add),
                reads=[f"{prefix}_FB:0", f"{prefix}_rb6", prefix + "_ut"],
                writes=[prefix + "_ut"], tag=f"{prefix}_cb t{t} h{half}")
            sc.op("V", lambda v, half=half, ltile=ltile, hs_=hs_: v.scalar_tensor_tensor(
                ut[:, hs_], ltile[:, hs_], 0.5, ut[:, hs_], ALU.mult, ALU.add),
                reads=[klt, prefix + "_ut"], writes=[prefix + "_ut"],
                tag=f"{prefix}_cc t{t} h{half}")
            sc.op("V", lambda v, half=half, xtile=xtile, hs_=hs_: v.tensor_tensor(
                ut[:, hs_], ut[:, hs_], xtile[:, hs_], ALU.add),
                reads=[kxt, prefix + "_ut"], writes=[prefix + "_ut"],
                tag=f"{prefix}_cd t{t} h{half}")
        sc.op("Q", lambda q, t=t: q.dma_start(
            xout.ap()[t * 128:(t + 1) * 128, :], ut[:]),
            reads=[prefix + "_ut"], writes=[f"xo:{t}", prefix + "_ut"], dma=True,
            tag=f"{prefix}_cs t{t}")


def _prep_l0a(g):
    wqkv = g["L0_attn_w_qkv"]
    return {
        "wqT": np.ascontiguousarray(wqkv[0:1024].T),
        "wkT": np.ascontiguousarray(wqkv[1024:2048].T),
        "wvT": np.ascontiguousarray(wqkv[2048:3072].T),
        "w_projT": np.ascontiguousarray(g["L0_attn_w_proj"].T),
        "b_proj": np.ascontiguousarray(g["L0_attn_b_proj"]),
    }


_L0A_SHAPES = {"wqT": [D, D], "wkT": [D, D], "wvT": [D, D], "w_projT": [D, D],
               "b_proj": [D]}


def _prep_l0f(g):
    qw = g["L0_ffn_q_w"].reshape(1024, 16, 64)[:, :, :32].reshape(1024, 512)
    qb = g["L0_ffn_q_b"].reshape(16, 64)[:, :32].reshape(512)
    return {
        "qw": np.ascontiguousarray(qw),
        "kw": np.ascontiguousarray(g["L0_ffn_k_w"]),
        "vw": np.ascontiguousarray(g["L0_ffn_v_w"]),
        "qb_dev": np.ascontiguousarray(qb.reshape(4, 128).T),
        "kb_dev": np.ascontiguousarray(g["L0_ffn_k_b"].reshape(4, 128).T),
        "vb_row": np.ascontiguousarray(g["L0_ffn_v_b"][None, :]),
        "ow": np.ascontiguousarray(g["L0_ffn_o_w"][:512]),
        "ob_dev": np.ascontiguousarray(g["L0_ffn_o_b"].reshape(8, 128).T),
        "rw": np.ascontiguousarray(g["L0_ffn_router_w"]),
        "rb_row": np.ascontiguousarray(g["L0_ffn_router_b"][None, :]),
        "w1": np.ascontiguousarray(g["L0_ffn_w1"]),
        "b1_dev": np.ascontiguousarray(
            g["L0_ffn_b1"].reshape(8, 16, 128).transpose(2, 0, 1).reshape(128, 128)),
        "w2": np.ascontiguousarray(g["L0_ffn_w2"]),
    }


_L0F_SHAPES = {"qw": [D, 512], "kw": [D, 512], "vw": [D, 512],
               "qb_dev": [128, 4], "kb_dev": [128, 4], "vb_row": [1, 512],
               "ow": [512, D], "ob_dev": [128, 8], "rw": [D, 8],
               "rb_row": [1, 8], "w1": [8, D, 2048], "b1_dev": [128, 128],
               "w2": [8, 2048, D]}


def _prep_l1t(g):
    Ws, Wq, Wk = g["L1_attn_W_share"], g["L1_attn_Wq"], g["L1_attn_Wk"]
    bs, bq, bk = g["L1_attn_b_share"], g["L1_attn_bq"], g["L1_attn_bk"]
    Wq_big = np.zeros((D, D), np.float32)
    Wk_big = np.zeros((D, D), np.float32)
    bq_big = np.zeros(D, np.float32)
    bk_big = np.zeros(D, np.float32)
    for h in range(16):
        Wq_big[:, h * 64:h * 64 + 16] = Ws
        Wq_big[:, h * 64 + 16:(h + 1) * 64] = Wq[:, h * 48:(h + 1) * 48]
        Wk_big[:, h * 64:h * 64 + 16] = Ws
        Wk_big[:, h * 64 + 16:(h + 1) * 64] = Wk[:, h * 48:(h + 1) * 48]
        bq_big[h * 64:h * 64 + 16] = bs
        bq_big[h * 64 + 16:(h + 1) * 64] = bq[h * 48:(h + 1) * 48]
        bk_big[h * 64:h * 64 + 16] = bs
        bk_big[h * 64 + 16:(h + 1) * 64] = bk[h * 48:(h + 1) * 48]
    return {
        "Wq_big": np.ascontiguousarray(Wq_big),
        "Wk_big": np.ascontiguousarray(Wk_big),
        "Wv": np.ascontiguousarray(g["L1_attn_Wv"]),
        "Wo": np.ascontiguousarray(g["L1_attn_Wo"]),
        "bq_dev": np.ascontiguousarray(bq_big.reshape(8, 128).T),
        "bk_dev": np.ascontiguousarray(bk_big.reshape(8, 128).T),
        "bv_row": np.ascontiguousarray(g["L1_attn_bv"][None, :]),
        "bo": np.ascontiguousarray(g["L1_attn_bo"]),
    }


_L1T_SHAPES = {"Wq_big": [D, D], "Wk_big": [D, D], "Wv": [D, D], "Wo": [D, D],
               "bq_dev": [128, 8], "bk_dev": [128, 8], "bv_row": [1, D],
               "bo": [D]}


def _prep_l1f(g):
    wkv = g["L1_ffn_w_kv_h"]
    nk = g["L1_ffn_null_kv"][0]
    nv = g["L1_ffn_null_kv"][1]
    nkd = np.zeros((128, 4), np.float32)
    for p in range(4):
        nkd[0:64, p] = nk[2 * p]
        nkd[64:128, p] = nk[2 * p + 1]
    nvr = np.zeros((1, 520), np.float32)
    for h in range(8):
        nvr[0, h * 65:h * 65 + 64] = nv[h]
        nvr[0, h * 65 + 64] = 1.0
    return {
        "wqkv_l": np.ascontiguousarray(g["L1_ffn_w_qkv_l"]),
        "wout_l": np.ascontiguousarray(g["L1_ffn_w_out_l"]),
        "wq_h": np.ascontiguousarray(g["L1_ffn_w_q_h"]),
        "wk_h": np.ascontiguousarray(wkv[:, :, :64].reshape(1024, 512)),
        "wv_h": np.ascontiguousarray(wkv[:, :, 64:].reshape(1024, 512)),
        "wout_h": np.ascontiguousarray(g["L1_ffn_w_out_h"]),
        "ln_g": np.ascontiguousarray(g["L1_ffn_ln_g"][None, :]),
        "ln_b": np.ascontiguousarray(g["L1_ffn_ln_b"][None, :]),
        "rms_g": np.ascontiguousarray(g["L1_ffn_rms_g"][None, :]),
        "null_q": np.ascontiguousarray(g["L1_ffn_null_q"][None, :]),
        "rt_q": np.ascontiguousarray(g["L1_ffn_rt_q"]),
        "rt_kv": np.ascontiguousarray(g["L1_ffn_rt_kv"]),
        "nullk_dev": nkd,
        "nullv_row": nvr,
    }


_L1F_SHAPES = {"wqkv_l": [D, 1536], "wout_l": [512, D], "wq_h": [D, 512],
               "wk_h": [D, 512], "wv_h": [D, 512], "wout_h": [512, D],
               "ln_g": [1, D], "ln_b": [1, D], "rms_g": [1, D],
               "null_q": [1, D], "rt_q": [1, D], "rt_kv": [1, D],
               "nullk_dev": [128, 4], "nullv_row": [1, 520]}


def _build_sub(kind):
    nc = bass.Bass("TRN2", target_bir_lowering=False)
    es = ExitStack()
    b = Bufs(nc, es)
    sc = Sched()
    xin = b.dram("x_in", [T, D], F32, kind="ExternalInput")
    xout = b.dram("x_out", [T, D], F32, kind="ExternalOutput")
    shapes = {"l0a": _L0A_SHAPES, "l0f": _L0F_SHAPES, "l1t": _L1T_SHAPES,
              "l1f": _L1F_SHAPES}[kind]
    W = {n: b.dram(kind + "_" + n, s, F32, kind="ExternalInput")
         for n, s in shapes.items()}
    with es:
        emit_setup(sc, b)
        if kind == "l0a":
            emit_ln_xnf(sc, b, xin, "xnf", "ln0")
            emit_l0_attn(sc, b, W, "xnf", xin, xout)
        elif kind == "l0f":
            w1bf = b.dram("w1bf", [8, D, 2048], BF16)
            w2bf = b.dram("w2bf", [8, 2048, D], BF16)
            emit_ln_xnf(sc, b, xin, "xnf", "ln0")
            emit_l0_ffn(sc, b, W, "xnf", xin, xout, "l0f",
                        w1bf=w1bf, w2bf=w2bf, convert_moe=True)
        elif kind == "l1t":
            emit_ln_xnf(sc, b, xin, "xnf", "ln0")
            emit_l1_attn(sc, b, W, "xnf", xin, xout)
        elif kind == "l1f":
            emit_l1_ffn(sc, b, W, xin, xout, "l1f")
        sc.build(nc, es)
        sc.emit(nc)
    return nc


_NC_CACHE = {}


def kernel(**inputs):
    g = {k: np.asarray(v, dtype=np.float32) for k, v in inputs.items()}
    Wl0f = _prep_l0f(g)
    Wl0a = _prep_l0a(g)
    Wl1f = _prep_l1f(g)
    Wl1t = _prep_l1t(g)
    for kind in ("l0f", "l0a", "l1t", "l1f"):
        if kind not in _NC_CACHE:
            _NC_CACHE[kind] = _build_sub(kind)
    x = np.ascontiguousarray(g["L0_attn_x"])
    shards = [np.ascontiguousarray(x[i]) for i in range(8)]
    seq = [("l0f", Wl0f), ("l0a", Wl0a), ("l0f", Wl0f),
           ("l1f", Wl1f), ("l1t", Wl1t), ("l1f", Wl1f)]
    for kind, Wm in seq:
        nc = _NC_CACHE[kind]
        wmap = {kind + "_" + k: v for k, v in Wm.items()}
        in_maps = [{"x_in": shards[i], **wmap} for i in range(8)]
        res = run_bass_kernel_spmd(nc, in_maps, core_ids=list(range(8)))
        shards = [np.ascontiguousarray(r["x_out"]) for r in res.results]
    return np.stack(shards, axis=0).astype(np.float32)


_REMAP = {
    "xnf": ("A8", 0),
    "l1f_xa": ("A8", 0),
    "l0a_wp": ("PB1", 0), "l0a_wqkv0": ("PB1", 8192), "l0a_wqkv1": ("PB1", 11264),
    "l0a_aofst0": ("PB1", 14336), "l0a_aofst1": ("PB1", 14464),
    "l0a_qf": ("PB2", 0), "l0a_ktv": ("PB2", 4096),
    "l0a_aot0": ("PB2", 12288), "l0a_aot1": ("PB3", 0),
    "l0a_aofld0": ("PB3", 4096), "l0a_aofld1": ("PB3", 5120),
    "l0a_stage0": ("PF1", 0), "l0a_stage1": ("PF1", 1024),
    "l0a_xt0": ("PF1", 2048), "l0a_xt1": ("PF1", 3072),
    "l0a_yt0": ("PF1", 4096), "l0a_yt1": ("PF1", 5120),
    "l0a_bproj_bc": ("PF1", 6144), "l0a_bproj_row": ("PF1", 7168),
    "l0a_c2b": ("PB1", 14592),
    "l1t_wo": ("PB1", 0), "l1t_wqkv0": ("PB1", 8192), "l1t_wqkv1": ("PB1", 11264),
    "l1t_aofst0": ("PB1", 14336), "l1t_aofst1": ("PB1", 14464),
    "l1t_qf": ("PB2", 0), "l1t_kf": ("PB2", 4096), "l1t_vt2": ("PB2", 8192),
    "l1t_aot0": ("PB3", 0), "l1t_aot1": ("PB3", 4096),
    "l1t_expb0": ("PB3", 8192), "l1t_expb1": ("PB3", 8704),
    "l1t_aofld0": ("PB3", 9216), "l1t_aofld1": ("PB3", 10240),
    "l1t_masks": ("PB3", 11264),
    "l1t_stage0": ("PF1", 0), "l1t_stage1": ("PF1", 1024),
    "l1t_xt0": ("PF1", 2048), "l1t_xt1": ("PF1", 3072),
    "l1t_bo_bc": ("PF1", 6144), "l1t_bo_row": ("PF1", 7168),
    "l1t_bvrow": ("PF1", 4096), "l1t_XR:0": ("PF1", 5120),
    "l1t_bqk": ("PF1", 5250), "l1t_recb": ("PF1", 5270),
    "l0f_AR1": ("PB1", 0), "l0f_AR2": ("PB2", 0), "l0f_aof": ("PB3", 0),
    "l0f_XR": ("PF1", 0),
    "l1f_v2l": ("PB1", 0), "l1f_wkv0": ("PB1", 8320), "l1f_wkv1": ("PB1", 11392),
    "l1f_Q2": ("PB2", 0), "l1f_K2": ("PB2", 4096), "l1f_aotL": ("PB2", 8192),
    "l1f_wout": ("PB2", 12288),
    "l1f_exph": ("PB3", 0), "l1f_expT": ("PB3", 1024), "l1f_GB": ("PB3", 1408),
    "l1f_xnlt": ("PB3", 4480), "l1f_xht": ("PB3", 5504),
    "l1f_aofst0": ("PB3", 6528), "l1f_aofst1": ("PB3", 6656),
    "l1f_aofld0": ("PB3", 6784), "l1f_aofld1": ("PB3", 7296),
    "l1f_Ebuf": ("PB3", 7808), "l1f_LT": ("PB3", 7840), "l1f_expn": ("PB3", 7968),
    "l1f_FB": ("PF1", 0), "l1f_ut": ("PF1", 2048),
    "l1f_nmr": ("PF1", 3072), "l1f_rts": ("PF1", 3200),
    "l1f_pstb": ("PF1", 3552), "l1f_bis": ("PF1", 3584),
    "l1f_recL": ("PF1", 3680), "l1f_nkf": ("PF1", 3690),
    "l1f_nvf": ("PF1", 3800), "l1f_nv": ("PB3", 9000),
    "l1f_nk": ("PB3", 9600),
}


def build_fused():
    nc = bass.Bass("TRN2", target_bir_lowering=False)
    es = ExitStack()
    b = Bufs(nc, es)
    sc = Sched()
    b.remap = dict(_REMAP)
    xin = b.dram("x_in", [T, D], F32, kind="ExternalInput")
    xout = b.dram("x_out", [T, D], F32, kind="ExternalOutput")
    xs = [xin] + [b.dram(f"xs{i}", [T, D], F32) for i in range(1, 6)] + [xout]
    Wl0a = {n: b.dram("l0a_" + n, s, F32, kind="ExternalInput")
            for n, s in _L0A_SHAPES.items()}
    Wl0f = {n: b.dram("l0f_" + n, s, F32, kind="ExternalInput")
            for n, s in _L0F_SHAPES.items()}
    Wl1t = {n: b.dram("l1t_" + n, s, F32, kind="ExternalInput")
            for n, s in _L1T_SHAPES.items()}
    Wl1f = {n: b.dram("l1f_" + n, s, F32, kind="ExternalInput")
            for n, s in _L1F_SHAPES.items()}
    w1bf = b.dram("w1bf", [8, D, 2048], BF16)
    w2bf = b.dram("w2bf", [8, 2048, D], BF16)
    with es:
        emit_setup(sc, b)
        b.sbuf("A8", [128, 32768], BF16)
        b.sbuf("PB1", [128, 16384], BF16)
        b.sbuf("PB2", [128, 16384], BF16)
        b.sbuf("PB3", [128, 16384], BF16)
        b.sbuf("PF1", [128, 8192], F32)
        emit_ln_xnf(sc, b, xs[0], "xnf", "ln0")
        emit_l0_ffn(sc, b, Wl0f, "xnf", xs[0], xs[1], "l0f",
                    w1bf=w1bf, w2bf=w2bf, convert_moe=True)
        sc.fence()
        emit_ln_xnf(sc, b, xs[1], "xnf", "ln0")
        emit_l0_attn(sc, b, Wl0a, "xnf", xs[1], xs[2])
        sc.fence()
        emit_ln_xnf(sc, b, xs[2], "xnf", "ln0")
        emit_l0_ffn(sc, b, Wl0f, "xnf", xs[2], xs[3], "l0f",
                    w1bf=w1bf, w2bf=w2bf, convert_moe=False)
        sc.fence()
        emit_l1_ffn(sc, b, Wl1f, xs[3], xs[4], "l1f")
        sc.fence()
        emit_ln_xnf(sc, b, xs[4], "xnf", "ln0")
        emit_l1_attn(sc, b, Wl1t, "xnf", xs[4], xs[5])
        sc.fence()
        emit_l1_ffn(sc, b, Wl1f, xs[5], xs[6], "l1f")
        sc.build(nc, es)
        sc.emit(nc)
    return nc, len(sc.ops)


_FUSED_CACHE = []


def kernel(**inputs):
    g = {k: np.asarray(v, dtype=np.float32) for k, v in inputs.items()}
    wmap = {}
    for kind, prep in (("l0a", _prep_l0a), ("l0f", _prep_l0f),
                       ("l1t", _prep_l1t), ("l1f", _prep_l1f)):
        for k, v in prep(g).items():
            wmap[kind + "_" + k] = v
    if not _FUSED_CACHE:
        nc, nops = build_fused()
        _FUSED_CACHE.append(nc)
    nc = _FUSED_CACHE[0]
    x = np.ascontiguousarray(g["L0_attn_x"])
    in_maps = [{"x_in": np.ascontiguousarray(x[i]), **wmap} for i in range(8)]
    res = run_bass_kernel_spmd(nc, in_maps, core_ids=list(range(8)))
    return np.stack([r["x_out"] for r in res.results], axis=0).astype(np.float32)
